# version 04
# baseline (speedup 1.0000x reference)
import contextlib
import numpy as np
import ml_dtypes
import concourse.bass as bass
import concourse.mybir as mybir
from concourse.bass_utils import run_bass_kernel_spmd
from concourse.aot_env import allow_ici_all_to_all

F32 = mybir.dt.float32
BF16 = mybir.dt.bfloat16
AF = mybir.ActivationFunctionType
ALU = mybir.AluOpType
AX = mybir.AxisListType
P = 128
T = 1024
D = 4096
NDC = 32
NCORE = 8
BLK0 = 1536
BLK1 = 768
OBLK = 512
LN_EPS = 1e-5
L0_SCALE = 192.0 ** -0.5
SWA_SCALE = 128.0 ** -0.5
GATE_NORM = 16.0
DEBUG = False



def emit_consts(nc, es):
    c = {}
    c["ones1"] = es.enter_context(nc.sbuf_tensor("c_ones1", [P, 1], BF16))
    c["onesr_f"] = es.enter_context(nc.sbuf_tensor("c_onesr_f", [1, P], F32))
    c["ident_b"] = es.enter_context(nc.sbuf_tensor("c_ident_b", [P, P], BF16))
    c["ident_f"] = es.enter_context(nc.sbuf_tensor("c_ident_f", [P, P], F32))
    c["u01_b"] = es.enter_context(nc.sbuf_tensor("c_u01_b", [P, P], BF16))
    c["u01_f"] = es.enter_context(nc.sbuf_tensor("c_u01_f", [P, P], F32))
    c["iota_d"] = es.enter_context(nc.sbuf_tensor("c_iota_d", [P, P], F32))
    c["r01_e4"] = es.enter_context(nc.sbuf_tensor("c_r01e4", [4, 32], BF16))
    c["r01_h8"] = es.enter_context(nc.sbuf_tensor("c_r01h8", [8, P], BF16))
    c["swa_m"] = es.enter_context(nc.sbuf_tensor("c_swam", [P, 256], BF16))
    scr1 = es.enter_context(nc.sbuf_tensor("c_scr1", [P, 2 * P], F32))
    scr2 = es.enter_context(nc.sbuf_tensor("c_scr2", [P, 2 * P], F32))
    sf = nc.alloc_semaphore("c_sf")
    with nc.Block() as block:
        @block.gpsimd
        def _(g):
            g.memset(c["ones1"][:, :], 1.0)
            g.memset(c["onesr_f"][:, :], 1.0)
            g.iota(c["iota_d"][:, :], [[1, P]], channel_multiplier=-1,
                   allow_small_or_imprecise_dtypes=True)
            g.iota(scr1[:, 0:32], [[1, 32]], channel_multiplier=-8,
                   allow_small_or_imprecise_dtypes=True)
            g.iota(scr1[:, 32:32 + P], [[1, P]], channel_multiplier=-16,
                   allow_small_or_imprecise_dtypes=True)
    with nc.Block() as block:
        @block.vector
        def _(v):
            io = c["iota_d"][:, :]
            n = [0]
            def step(instr):
                n[0] += 1
                instr.then_inc(sf, 1)
                v.wait_ge(sf, n[0])
            step(v.tensor_scalar(out=c["ident_f"][:, :], in0=io, scalar1=0.0,
                                 scalar2=None, op0=ALU.is_equal))
            step(v.tensor_copy(out=c["ident_b"][:, :], in_=c["ident_f"][:, :]))
            step(v.tensor_scalar(out=c["u01_f"][:, :], in0=io, scalar1=0.0,
                                 scalar2=None, op0=ALU.is_ge))
            step(v.tensor_copy(out=c["u01_b"][:, :], in_=c["u01_f"][:, :]))
            step(v.tensor_scalar(out=c["swa_m"][:, 0:P], in0=io, scalar1=-64.0,
                                 scalar2=None, op0=ALU.is_le))
            step(v.tensor_scalar(out=scr2[:, 0:P], in0=io, scalar1=64.0,
                                 scalar2=None, op0=ALU.is_le))
            step(v.tensor_mul(out=c["swa_m"][:, P:256], in0=scr2[:, 0:P],
                              in1=c["u01_f"][:, :]))
            step(v.tensor_scalar(out=scr2[:, 0:32], in0=scr1[:, 0:32], scalar1=0.0,
                                 scalar2=None, op0=ALU.is_ge))
            step(v.tensor_scalar(out=scr2[:, 32:64], in0=scr1[:, 0:32], scalar1=7.0,
                                 scalar2=None, op0=ALU.is_le))
            step(v.tensor_mul(out=c["r01_e4"][:, :], in0=scr2[0:4, 0:32],
                              in1=scr2[0:4, 32:64]))
            step(v.tensor_scalar(out=scr2[:, 0:P], in0=scr1[:, 32:32 + P], scalar1=0.0,
                                 scalar2=None, op0=ALU.is_ge))
            step(v.tensor_scalar(out=scr2[:, P:2 * P], in0=scr1[:, 32:32 + P],
                                 scalar1=15.0, scalar2=None, op0=ALU.is_le))
            step(v.tensor_mul(out=c["r01_h8"][:, :], in0=scr2[0:8, 0:P],
                              in1=scr2[0:8, P:2 * P]))
    return c


def emit_ln(nc, es, x_dram, xn_fm, rowscr_dram, tag=""):
    with contextlib.ExitStack() as ls:
        sb = lambda n, s, d: ls.enter_context(nc.sbuf_tensor(n + tag, s, d))
        xf = [sb(f"ln_xf{i}", [P, T], F32) for i in range(2)]
        xb = [sb(f"ln_xb{i}", [P, T], BF16) for i in range(2)]
        xq = [sb(f"ln_xq{i}", [P, T], BF16) for i in range(2)]
        rows = sb("ln_rows", [1, 8 * T], F32)
        c1b = sb("ln_c1b", [P, T], F32)
        c0b = sb("ln_c0b", [P, T], F32)
        tmp = sb("ln_tmp", [P, T], F32)
        ones1 = sb("ln_ones", [P, 1], BF16)
        eps_sb = sb("ln_eps", [1, 1], F32)
        ps_sum = ls.enter_context(nc.psum_tensor("ln_pssum" + tag, [1, T], F32))
        ps_sq = ls.enter_context(nc.psum_tensor("ln_pssq" + tag, [1, T], F32))
        ld = nc.alloc_semaphore("ln_ld" + tag)
        cp = nc.alloc_semaphore("ln_cp" + tag)
        sq = nc.alloc_semaphore("ln_sq" + tag)
        pe = nc.alloc_semaphore("ln_pe" + tag)

        with nc.Block() as block:
            @block.sync
            def _(sy):
                for i in range(NDC):
                    if i >= 2:
                        sy.wait_ge(pe, i - 1)
                        sy.wait_ge(sq, i - 1)
                        sy.wait_ge(cp, i - 1)
                    sy.dma_start(out=xf[i % 2][:, :],
                                 in_=x_dram[i * P:(i + 1) * P, :]).then_inc(ld, 16)

            @block.gpsimd
            def _(g):
                g.memset(ones1[:, :], 1.0)
                g.memset(eps_sb[:, :], LN_EPS)

            @block.vector
            def _(v):
                for i in range(NDC):
                    v.wait_ge(ld, 16 * (i + 1))
                    if i >= 2:
                        v.wait_ge(pe, i - 1)
                    v.tensor_copy(out=xb[i % 2][:, :], in_=xf[i % 2][:, :]).then_inc(cp, 1)

            @block.scalar
            def _(s):
                for i in range(NDC):
                    s.wait_ge(ld, 16 * (i + 1))
                    if i >= 2:
                        s.wait_ge(pe, i - 1)
                    s.activation(out=xq[i % 2][:, :], in_=xf[i % 2][:, :],
                                 func=AF.Square).then_inc(sq, 1)

            @block.tensor
            def _(t):
                for i in range(NDC):
                    p = i % 2
                    t.wait_ge(cp, i + 1)
                    t.wait_ge(sq, i + 1)
                    st, sp = (i == 0), (i == NDC - 1)
                    t.matmul(ps_sum[:, 0:512], ones1[:, :], xb[p][:, 0:512], start=st, stop=sp)
                    t.matmul(ps_sum[:, 512:T], ones1[:, :], xb[p][:, 512:T], start=st, stop=sp)
                    t.matmul(ps_sq[:, 0:512], ones1[:, :], xq[p][:, 0:512], start=st, stop=sp)
                    t.matmul(ps_sq[:, 512:T], ones1[:, :], xq[p][:, 512:T],
                             start=st, stop=sp).then_inc(pe, 1)

        SUM, SQ, M, MSQ, VAR, STD, C1, C0 = range(8)
        r = lambda k: rows[:, k * T:(k + 1) * T]
        with nc.Block() as block:
            @block.vector
            def _(v):
                nn = [0]
                def step(instr):
                    nn[0] += 1
                    instr.then_inc(cp, 1)
                    v.wait_ge(cp, NDC + nn[0])
                step(v.tensor_copy(out=r(SUM), in_=ps_sum[:, :]))
                step(v.tensor_copy(out=r(SQ), in_=ps_sq[:, :]))
                step(v.tensor_scalar_mul(out=r(M), in0=r(SUM), scalar1=1.0 / D))
                step(v.tensor_mul(out=r(MSQ), in0=r(M), in1=r(M)))
                step(v.scalar_tensor_tensor(out=r(VAR), in0=r(SQ), scalar=1.0 / D,
                                            in1=r(MSQ), op0=ALU.mult, op1=ALU.subtract))
        with nc.Block() as block:
            @block.scalar
            def _(s):
                s.activation(out=r(STD), in_=r(VAR), func=AF.Sqrt,
                             bias=eps_sb[:, :]).then_inc(sq, 1)
                s.wait_ge(sq, NDC + 1)
        with nc.Block() as block:
            @block.vector
            def _(v):
                nn = [5]
                def step(instr):
                    nn[0] += 1
                    instr.then_inc(cp, 1)
                    v.wait_ge(cp, NDC + nn[0])
                step(v.reciprocal(out=r(C1), in_=r(STD)))
                step(v.tensor_mul(out=r(MSQ), in0=r(M), in1=r(C1)))
                step(v.tensor_scalar_mul(out=r(C0), in0=r(MSQ), scalar1=-1.0))
        with nc.Block() as block:
            @block.sync
            def _(sy):
                sy.dma_start(out=rowscr_dram[0:1, :], in_=r(C1)).then_inc(ld, 16)
                sy.dma_start(out=rowscr_dram[1:2, :], in_=r(C0)).then_inc(ld, 16)
                sy.wait_ge(ld, 16 * (NDC + 2))
                sy.dma_start(out=c1b[:, :],
                             in_=rowscr_dram[0:1, :].to_broadcast((P, T))).then_inc(ld, 16)
                sy.dma_start(out=c0b[:, :],
                             in_=rowscr_dram[1:2, :].to_broadcast((P, T))).then_inc(ld, 16)
                sy.wait_ge(ld, 16 * (NDC + 4))

        with nc.Block() as block:
            @block.sync
            def _(sy):
                for i in range(NDC):
                    if i >= 2:
                        sy.wait_ge(cp, NDC + 8 + i - 1)
                    sy.dma_start(out=xf[i % 2][:, :],
                                 in_=x_dram[i * P:(i + 1) * P, :]).then_inc(ld, 16)

            @block.vector
            def _(v):
                for i in range(NDC):
                    v.wait_ge(ld, 16 * (NDC + 4 + i + 1))
                    v.tensor_mul(out=tmp[:, :], in0=xf[i % 2][:, :], in1=c1b[:, :])
                    v.tensor_add(out=xn_fm[:, i * T:(i + 1) * T], in0=tmp[:, :],
                                 in1=c0b[:, :]).then_inc(cp, 1)
                v.wait_ge(cp, NDC + 8 + NDC)


def emit_wstat_proj(nc, w_dram, col_slices, xn_fm, out_dram, dest_rows, tag,
                    evac="copy", extra_mms=None, silu_scale=1.0, n_dc=NDC):
    nsl = len(col_slices)
    with contextlib.ExitStack() as ls:
        wsl = [ls.enter_context(nc.sbuf_tensor(f"{tag}_w{i}", [P, n_dc * P], BF16))
               for i in range(2)]
        evt = [ls.enter_context(nc.sbuf_tensor(f"{tag}_e{i}", [P, T], BF16))
               for i in range(2)]
        ps = [ls.enter_context(nc.psum_tensor(f"{tag}_ps{i}", [P, T], F32))
              for i in range(2)]
        ld = nc.alloc_semaphore(f"{tag}_ld")
        pe = nc.alloc_semaphore(f"{tag}_pe")
        ev = nc.alloc_semaphore(f"{tag}_ev")
        st_s = nc.alloc_semaphore(f"{tag}_st")

        with nc.Block() as block:
            @block.gpsimd
            def _(g):
                for i, c0 in enumerate(col_slices):
                    if i >= 2:
                        g.wait_ge(pe, i - 1)
                    src = w_dram[:, c0:c0 + P].rearrange("(dc p) c -> p dc c", p=P)
                    dst = wsl[i % 2][:, :].rearrange("p (dc c) -> p dc c", dc=n_dc)
                    g.dma_start(out=dst, in_=src).then_inc(ld, 16)

            @block.tensor
            def _(t):
                for i in range(nsl):
                    p = i % 2
                    t.wait_ge(ld, 16 * (i + 1))
                    if i >= 2:
                        t.wait_ge(ev, i - 1)
                    last = n_dc - 1 if extra_mms is None else None
                    for dc in range(n_dc):
                        lhsT = wsl[p][:, dc * P:(dc + 1) * P]
                        st = (dc == 0)
                        sp = (dc == last)
                        t.matmul(ps[p][:, 0:512], lhsT, xn_fm[:, dc * T:dc * T + 512],
                                 start=st, stop=sp if extra_mms is None else False)
                        mm = t.matmul(ps[p][:, 512:T], lhsT,
                                      xn_fm[:, dc * T + 512:(dc + 1) * T],
                                      start=st, stop=(sp if extra_mms is None else False))
                        if extra_mms is None and sp:
                            mm.then_inc(pe, 1)
                    if extra_mms is not None:
                        extra_mms(t, i, ps[p], pe)

            if evac == "copy":
                @block.vector
                def _(v):
                    for i in range(nsl):
                        p = i % 2
                        v.wait_ge(pe, i + 1)
                        if i >= 2:
                            v.wait_ge(st_s, 16 * (i - 1))
                        v.tensor_copy(out=evt[p][:, :], in_=ps[p][:, :]).then_inc(ev, 1)
            else:
                @block.scalar
                def _(s):
                    for i in range(nsl):
                        p = i % 2
                        s.wait_ge(pe, i + 1)
                        if i >= 2:
                            s.wait_ge(st_s, 16 * (i - 1))
                        s.activation(out=evt[p][:, :], in_=ps[p][:, :], func=AF.Silu,
                                     scale=silu_scale).then_inc(ev, 1)

            @block.sync
            def _(sy):
                for i in range(nsl):
                    p = i % 2
                    sy.wait_ge(ev, i + 1)
                    sy.dma_start(out=out_dram[dest_rows[i]:dest_rows[i] + P, :],
                                 in_=evt[p][:, :]).then_inc(st_s, 16)
                sy.wait_ge(st_s, 16 * nsl)


def emit_v_tm(nc, wv_dram, xn_fm, out_dram, vcols, blk, tag, extra=None):
    vpb = vcols // 8
    ngrp = vcols // 512
    vsec = blk - (vpb * T) // T
    vrow0 = blk - vpb
    with contextlib.ExitStack() as ls:
        wsl = [ls.enter_context(nc.sbuf_tensor(f"{tag}_w{i}", [P, NDC * 512], BF16))
               for i in range(2)]
        evt = [ls.enter_context(nc.sbuf_tensor(f"{tag}_e{i}", [P, 512], BF16))
               for i in range(2)]
        ps = [ls.enter_context(nc.psum_tensor(f"{tag}_ps{i}", [P, 512], F32))
              for i in range(2)]
        ld = nc.alloc_semaphore(f"{tag}_ld")
        pe = nc.alloc_semaphore(f"{tag}_pe")
        ev = nc.alloc_semaphore(f"{tag}_ev")
        st_s = nc.alloc_semaphore(f"{tag}_st")

        with nc.Block() as block:
            @block.gpsimd
            def _(g):
                for grp in range(ngrp):
                    if grp >= 2:
                        g.wait_ge(pe, (grp - 1) * 8)
                    src = wv_dram[:, grp * 512:(grp + 1) * 512].rearrange(
                        "(dc p) c -> p dc c", p=P)
                    g.dma_start(out=wsl[grp % 2][:, :].rearrange("p (dc c) -> p dc c", dc=NDC),
                                in_=src).then_inc(ld, 16)

            @block.tensor
            def _(t):
                it = 0
                for grp in range(ngrp):
                    t.wait_ge(ld, 16 * (grp + 1))
                    for tt in range(8):
                        it += 1
                        if it >= 3:
                            t.wait_ge(ev, it - 2)
                        pv = ps[it % 2]
                        for dc in range(NDC):
                            st, sp = (dc == 0), (dc == NDC - 1)
                            lhsT = xn_fm[:, dc * T + tt * P:dc * T + (tt + 1) * P]
                            mm = t.matmul(pv[:, :], lhsT,
                                          wsl[grp % 2][:, dc * 512:(dc + 1) * 512],
                                          start=st, stop=sp)
                            if extra is not None:
                                extra(t, grp, tt, dc, lhsT)
                            if sp:
                                mm.then_inc(pe, 1)

            @block.vector
            def _(v):
                it = 0
                for grp in range(ngrp):
                    for tt in range(8):
                        it += 1
                        v.wait_ge(pe, it)
                        if it >= 3:
                            v.wait_ge(st_s, 16 * (it - 2))
                        v.tensor_copy(out=evt[it % 2][:, :], in_=ps[it % 2][:, :]).then_inc(ev, 1)

            @block.sync
            def _(sy):
                it = 0
                for grp in range(ngrp):
                    for tt in range(8):
                        it += 1
                        sy.wait_ge(ev, it)
                        nblk = 512 // vpb
                        for bsub in range(nblk):
                            j = grp * nblk + bsub
                            r0 = j * blk + vrow0 + tt * (P * vpb) // T
                            nrows = (P * vpb) // T
                            dst = out_dram[r0:r0 + nrows, :].rearrange(
                                "r (a c) -> (r a) c", a=T // vpb)
                            sy.dma_start(out=dst,
                                         in_=evt[it % 2][:, bsub * vpb:(bsub + 1) * vpb]
                                         ).then_inc(st_s, 16)
                sy.wait_ge(st_s, 16 * ngrp * 8 * (512 // vpb))


def emit_gate_l0(nc, es, gate_dram, ebias_dram, xn_fm, consts, m_fm, dbg_lg=None, dbg_rm=None, dbg_mtm=None):
    with contextlib.ExitStack() as ls:
        gatesl = ls.enter_context(nc.sbuf_tensor("g0_w", [P, NDC * 4], BF16))
        eb = ls.enter_context(nc.sbuf_tensor("g0_eb", [4, 1], F32))
        lgfm = ls.enter_context(nc.sbuf_tensor("g0_lgfm", [4, T], F32))
        lgtm = ls.enter_context(nc.sbuf_tensor("g0_lgtm", [P, 8 * 4], F32))
        rmax = ls.enter_context(nc.sbuf_tensor("g0_rmax", [P, 8], F32))
        m_tm = ls.enter_context(nc.sbuf_tensor("g0_mtm", [P, 8 * 4], BF16))
        m32 = ls.enter_context(nc.sbuf_tensor("g0_m32", [P, 8 * 4], F32))
        m32b = ls.enter_context(nc.sbuf_tensor("g0_m32b", [P, 8 * 4], F32))
        ps_g = ls.enter_context(nc.psum_tensor("g0_psg", [4, T], F32))
        ps_t = [ls.enter_context(nc.psum_tensor(f"g0_pst{i}", [P, 4], F32)) for i in range(2)]
        ps_m = [ls.enter_context(nc.psum_tensor(f"g0_psm{i}", [4, P], BF16)) for i in range(2)]
        ld = nc.alloc_semaphore("g0_ld")
        pe = nc.alloc_semaphore("g0_pe")
        ev = nc.alloc_semaphore("g0_ev")
        sf = nc.alloc_semaphore("g0_sf")

        with nc.Block() as block:
            @block.gpsimd
            def _(g):
                g.dma_start(out=gatesl[:, :].rearrange("p (dc c) -> p dc c", dc=NDC),
                            in_=gate_dram.rearrange("(dc p) c -> p dc c", p=P)).then_inc(ld, 16)
                g.dma_start(out=eb[:, :], in_=ebias_dram).then_inc(ld, 16)
                g.wait_ge(ld, 32)
        with nc.Block() as block:
            @block.tensor
            def _(t):
                for dc in range(NDC):
                    st, sp = (dc == 0), (dc == NDC - 1)
                    lhsT = gatesl[:, dc * 4:(dc + 1) * 4]
                    t.matmul(ps_g[:, 0:512], lhsT, xn_fm[:, dc * T:dc * T + 512],
                             start=st, stop=sp)
                    t.matmul(ps_g[:, 512:T], lhsT, xn_fm[:, dc * T + 512:(dc + 1) * T],
                             start=st, stop=sp)
        with nc.Block() as block:
            @block.vector
            def _(v):
                v.tensor_scalar(out=lgfm[:, :], in0=ps_g[:, :], scalar1=eb[:, :],
                                scalar2=None, op0=ALU.add).then_inc(sf, 1)
                v.wait_ge(sf, 1)
        with nc.Block() as block:
            @block.tensor
            def _(t):
                for tt in range(8):
                    if tt >= 2:
                        t.wait_ge(ev, tt - 1)
                    t.transpose(ps_t[tt % 2][:, :], lgfm[:, tt * P:(tt + 1) * P],
                                consts["ident_f"][0:4, 0:4]).then_inc(pe, 1)

            @block.vector
            def _(v):
                for tt in range(8):
                    v.wait_ge(pe, tt + 1)
                    v.tensor_copy(out=lgtm[:, tt * 4:(tt + 1) * 4],
                                  in_=ps_t[tt % 2][:, :]).then_inc(ev, 1)
                v.wait_ge(ev, 8)
        with nc.Block() as block:
            @block.vector
            def _(v):
                v.tensor_reduce(out=rmax[:, :],
                                in_=lgtm[:, :].rearrange("p (t e) -> p t e", e=4),
                                axis=AX.X, op=ALU.max).then_inc(sf, 1)
                v.wait_ge(sf, 2)
                for e in range(3):
                    v.tensor_copy(
                        out=m32[:, :].rearrange("p (t e) -> p t e", e=4)[:, :, e],
                        in_=rmax[:, :])
                v.tensor_copy(
                    out=m32[:, :].rearrange("p (t e) -> p t e", e=4)[:, :, 3],
                    in_=rmax[:, :]).then_inc(sf, 1)
                v.wait_ge(sf, 3)
                v.tensor_sub(out=m32b[:, :], in0=lgtm[:, :], in1=m32[:, :]).then_inc(sf, 1)
                v.wait_ge(sf, 4)
                v.tensor_scalar(out=m_tm[:, :], in0=m32b[:, :], scalar1=0.0,
                                scalar2=None, op0=ALU.is_ge).then_inc(sf, 1)
                v.wait_ge(sf, 5)
        if dbg_lg is not None:
            with nc.Block() as block:
                @block.sync
                def _(sy):
                    sy.dma_start(out=dbg_lg, in_=lgtm[:, :]).then_inc(ld, 16)
                    sy.dma_start(out=dbg_rm, in_=rmax[:, :]).then_inc(ld, 16)
                    sy.wait_ge(ld, 64)
                @block.gpsimd
                def _(g):
                    g.dma_start(out=dbg_mtm, in_=m_tm[:, :]).then_inc(ld, 16)
                    g.wait_ge(ld, 80)
        with nc.Block() as block:
            @block.tensor
            def _(t):
                for tt in range(8):
                    if tt >= 2:
                        t.wait_ge(ev, 8 + tt - 1)
                    t.transpose(ps_m[tt % 2][:, :], m_tm[:, tt * 4:(tt + 1) * 4],
                                consts["ident_b"][:, :]).then_inc(pe, 1)

            @block.vector
            def _(v):
                for tt in range(8):
                    v.wait_ge(pe, 8 + tt + 1)
                    v.tensor_copy(out=m_fm[:, tt * P:(tt + 1) * P],
                                  in_=ps_m[tt % 2][:, :]).then_inc(ev, 1)
                v.wait_ge(ev, 16)


def emit_ffn_l0(nc, es, acat_dram, bcat_dram, wup_dram, xn_fm, m_fm, consts, gout,
                dbg_m=None, dbg_zw=None, dbg_z2=None, dbg_z3=None):
    zw_sb = es.enter_context(nc.sbuf_tensor("f0_zw", [32, T], BF16))
    with contextlib.ExitStack() as ls:
        acat_sl = ls.enter_context(nc.sbuf_tensor("f0_a", [P, NDC * 32], BF16))
        z_sb = ls.enter_context(nc.sbuf_tensor("f0_z", [32, T], BF16))
        ps_z = ls.enter_context(nc.psum_tensor("f0_psz", [32, T], F32))
        ps_me = ls.enter_context(nc.psum_tensor("f0_psme", [32, T], F32))
        ld = nc.alloc_semaphore("f0_ld")

        with nc.Block() as block:
            @block.gpsimd
            def _(g):
                g.dma_start(out=acat_sl[:, :].rearrange("p (dc c) -> p dc c", dc=NDC),
                            in_=acat_dram.rearrange("(dc p) c -> p dc c", p=P)).then_inc(ld, 16)
                g.wait_ge(ld, 16)
        with nc.Block() as block:
            @block.tensor
            def _(t):
                for dc in range(NDC):
                    st, sp = (dc == 0), (dc == NDC - 1)
                    lhsT = acat_sl[:, dc * 32:(dc + 1) * 32]
                    t.matmul(ps_z[:, 0:512], lhsT, xn_fm[:, dc * T:dc * T + 512],
                             start=st, stop=sp)
                    t.matmul(ps_z[:, 512:T], lhsT, xn_fm[:, dc * T + 512:(dc + 1) * T],
                             start=st, stop=sp)
                t.matmul(ps_me[:, 0:512], consts["r01_e4"][:, :], m_fm[:, 0:512],
                         start=True, stop=True)
                t.matmul(ps_me[:, 512:T], consts["r01_e4"][:, :], m_fm[:, 512:T],
                         start=True, stop=True)
        with nc.Block() as block:
            @block.vector
            def _(v):
                v.tensor_copy(out=z_sb[:, :], in_=ps_z[:, :]).then_inc(ld, 1)
                v.wait_ge(ld, 17)
        with nc.Block() as block:
            @block.vector
            def _(v):
                v.tensor_mul(out=zw_sb[:, :], in0=z_sb[:, :],
                             in1=ps_me[:, :]).then_inc(ld, 1)
                v.wait_ge(ld, 18)
        if dbg_m is not None:
            dscr = ls.enter_context(nc.sbuf_tensor("f0_dbg", [32, T], F32))
            dscr2 = ls.enter_context(nc.sbuf_tensor("f0_dbg2", [32, T], F32))
            dscr3 = ls.enter_context(nc.sbuf_tensor("f0_dbg3", [32, T], F32))
            dsem = nc.alloc_semaphore("f0_dsem")
            with nc.Block() as block:
                @block.vector
                def _(v):
                    v.tensor_copy(out=dscr[:, :], in_=zw_sb[:, :])
                    v.tensor_copy(out=dscr2[:, :], in_=z_sb[:, :])
                    v.tensor_copy(out=dscr3[:, :], in_=ps_me[:, :])
            with nc.Block() as block:
                @block.gpsimd
                def _(sy):
                    sy.dma_start(out=dbg_zw, in_=dscr[:, :]).then_inc(dsem, 16)
                    sy.dma_start(out=dbg_m, in_=m_fm[:, :]).then_inc(dsem, 16)
                    sy.dma_start(out=dbg_z2, in_=dscr2[:, :]).then_inc(dsem, 16)
                    sy.dma_start(out=dbg_z3, in_=dscr3[:, :]).then_inc(dsem, 16)
                    sy.wait_ge(dsem, 64)

    bcat_sb = es.enter_context(nc.sbuf_tensor("f0_b", [32, D], BF16))
    with contextlib.ExitStack() as ls:
        ld2 = nc.alloc_semaphore("f0_ld2")
        with nc.Block() as block:
            @block.gpsimd
            def _(g):
                g.dma_start(out=bcat_sb[:, :], in_=bcat_dram).then_inc(ld2, 16)
                g.wait_ge(ld2, 16)

        def extra(t, i, ps_tile, pe):
            lhsT = bcat_sb[:, i * P:(i + 1) * P]
            t.matmul(ps_tile[:, 0:512], lhsT, zw_sb[:, 0:512], start=False, stop=False)
            t.matmul(ps_tile[:, 512:T], lhsT, zw_sb[:, 512:T], start=False,
                     stop=True).then_inc(pe, 1)

        emit_wstat_proj(nc, wup_dram, [oc * P for oc in range(NDC)], xn_fm, gout,
                        [oc * P for oc in range(NDC)], "f0up", evac="silu",
                        extra_mms=extra)



def build_l0_pre(nc=None, ext=None, tag=""):
    if nc is None:
        nc = bass.Bass("TRN2", target_bir_lowering=False)
    io = ext if ext is not None else {}
    def _in(name, shape):
        return io[name] if name in io else nc.dram_tensor(name, shape, F32, kind="ExternalInput").ap()
    x0 = _in("x0", [D, T])
    wq = _in("wq", [D, D])
    wk = _in("wk", [D, D])
    wv = _in("wv", [D, D])
    wup = _in("wup", [D, D])
    gate = _in("gate", [D, 4])
    ebias = _in("ebias", [4, 1])
    acat = _in("acat", [D, 32])
    bcat = _in("bcat", [32, D])
    qkv = io.get("qkv") or nc.dram_tensor("qkv", [NCORE * BLK0, T], BF16, kind="ExternalOutput").ap()
    gout = io.get("gout") or nc.dram_tensor("gout", [D, T], BF16, kind="ExternalOutput").ap()
    rowscr = nc.dram_tensor("rowscr" + tag, [2, T], F32).ap()
    dbg_m = nc.dram_tensor("dbg_m", [4, T], F32, kind="ExternalOutput").ap() if DEBUG else None
    dbg_zw = nc.dram_tensor("dbg_zw", [32, T], F32, kind="ExternalOutput").ap() if DEBUG else None
    dbg_lg = nc.dram_tensor("dbg_lg", [P, 32], F32, kind="ExternalOutput").ap() if DEBUG else None
    dbg_rm = nc.dram_tensor("dbg_rm", [P, 8], F32, kind="ExternalOutput").ap() if DEBUG else None
    dbg_mtm = nc.dram_tensor("dbg_mtm", [P, 32], F32, kind="ExternalOutput").ap() if DEBUG else None
    dbg_z2 = nc.dram_tensor("dbg_z2", [32, T], F32, kind="ExternalOutput").ap() if DEBUG else None
    dbg_z3 = nc.dram_tensor("dbg_z3", [32, T], F32, kind="ExternalOutput").ap() if DEBUG else None
    dbg_xn = nc.dram_tensor("dbg_xn", [P, 4 * T], F32, kind="ExternalOutput").ap() if DEBUG else None

    with contextlib.ExitStack() as es:
        consts = io.get("consts") or emit_consts(nc, es)
        xn_fm = es.enter_context(nc.sbuf_tensor("xn_fm", [P, NDC * T], BF16))
        m_fm = es.enter_context(nc.sbuf_tensor("m_fm", [4, T], BF16))
        emit_ln(nc, es, x0, xn_fm, rowscr)
        if DEBUG:
            with contextlib.ExitStack() as dls:
                dsm = dnc.alloc_semaphore("dbgxn_sem")
                with nc.Block() as block:
                    @block.gpsimd
                    def _(g):
                        g.dma_start(out=dbg_xn, in_=xn_fm[:, 0:4 * T]).then_inc(dsm, 16)
                        g.wait_ge(dsm, 16)
        emit_gate_l0(nc, es, gate, ebias, xn_fm, consts, m_fm, dbg_lg, dbg_rm, dbg_mtm)
        q_cols = [(4 * j + cc) * P for j in range(NCORE) for cc in range(4)]
        q_rows = [j * BLK0 + cc * P for j in range(NCORE) for cc in range(4)]
        emit_wstat_proj(nc, wq, q_cols, xn_fm, qkv, q_rows, "qp")
        k_rows = [j * BLK0 + 512 + cc * P for j in range(NCORE) for cc in range(4)]
        emit_wstat_proj(nc, wk, q_cols, xn_fm, qkv, k_rows, "kp")
        emit_v_tm(nc, wv, xn_fm, qkv, D, BLK0, "vp")
        emit_ffn_l0(nc, es, acat, bcat, wup, xn_fm, m_fm, consts, gout,
                    dbg_m=dbg_m, dbg_zw=dbg_zw, dbg_z2=dbg_z2, dbg_z3=dbg_z3)
    return nc



def _swap_blocks(arrs, blk):
    recv = [np.empty_like(a) for a in arrs]
    for j in range(NCORE):
        for i in range(NCORE):
            recv[j][i * blk:(i + 1) * blk] = arrs[i][j * blk:(j + 1) * blk]
    return recv


def _prep_weights(inputs):
    w = {}
    ct = np.ascontiguousarray
    w["wq"] = ct(inputs["L0_attn_wq"].reshape(D, 32, 192)[:, :, :128].reshape(D, D))
    w["wk"] = ct(inputs["L0_attn_wkv"][:, :D])
    w["wv"] = ct(inputs["L0_attn_wkv"][:, D:2 * D])
    w["wo0"] = ct(inputs["L0_attn_wo"])
    w["wup"] = ct(inputs["L0_ffn_W_up"].T)
    w["gate"] = ct(inputs["L0_ffn_gate_w"].T)
    w["ebias"] = ct(inputs["L0_ffn_expert_bias"].reshape(4, 1))
    w["acat"] = ct(inputs["L0_ffn_lora_A"].reshape(32, D).T)
    w["bcat"] = ct(inputs["L0_ffn_lora_B"].transpose(0, 2, 1).reshape(32, D))
    w["wq1"] = ct(inputs["L1_attn_Wq"])
    w["wk1"] = ct(inputs["L1_attn_Wk"])
    w["wv1"] = ct(inputs["L1_attn_Wv"])
    w["wo1"] = ct(inputs["L1_attn_Wo"])
    w["wr1"] = ct(inputs["L1_ffn_Wr"].T)
    w["br1"] = ct(inputs["L1_ffn_br"].reshape(8, 1))
    w["acat1"] = ct(inputs["L1_ffn_A"].reshape(128, D).T)
    w["bcat1"] = ct(inputs["L1_ffn_B"].transpose(0, 2, 1).reshape(128, D))
    j = np.arange(64, dtype=np.float64)
    inv = 10000.0 ** (-2.0 * j / 128.0)
    t = np.arange(4096, dtype=np.float64)
    ang = t[None, :] * inv[:, None]
    ang2 = np.concatenate([ang, ang], axis=0)
    w["cosT"] = np.cos(ang2).astype(np.float32)
    w["sinT"] = np.sin(ang2).astype(np.float32)
    return w


def _shard_x0(inputs):
    x = inputs["L0_attn_hidden_states"]
    shards = []
    for c in range(NCORE):
        b, q = c // 4, c % 4
        shards.append(np.ascontiguousarray(x[b, q * T:(q + 1) * T, :].T.astype(np.float32)))
    return shards



def build_l0_attn(nc=None, ext=None):
    if nc is None:
        nc = bass.Bass("TRN2", target_bir_lowering=False)
    io = ext if ext is not None else {}
    qkv = io.get("qkv") or nc.dram_tensor("qkv", [NCORE * BLK0, T], BF16, kind="ExternalInput").ap()
    ob = io.get("ob") or nc.dram_tensor("ob", [NCORE * OBLK, T], BF16, kind="ExternalOutput").ap()

    with contextlib.ExitStack() as es:
        consts = io.get("consts") or emit_consts(nc, es)
        Q = es.enter_context(nc.sbuf_tensor("Q", [P, 4 * T], BF16))
        Kb = es.enter_context(nc.sbuf_tensor("Kb", [P, 4 * T], BF16))
        V = es.enter_context(nc.sbuf_tensor("V", [P, 32 * P], BF16))
        sqb = es.enter_context(nc.sbuf_tensor("sqb", [P, 4 * T], BF16))
        Psb = [es.enter_context(nc.sbuf_tensor(f"Psb{i}", [P, T], BF16)) for i in range(2)]
        band = es.enter_context(nc.sbuf_tensor("band", [P, P], BF16))
        o_raw = es.enter_context(nc.sbuf_tensor("o_raw", [P, T], BF16))
        o_out = es.enter_context(nc.sbuf_tensor("o_out", [P, T], BF16))
        nq = es.enter_context(nc.sbuf_tensor("nq", [1, 4 * T], F32))
        nk = es.enter_context(nc.sbuf_tensor("nk", [1, 4 * T], F32))
        rsb = es.enter_context(nc.sbuf_tensor("rsb", [1, T], F32))
        rcp = es.enter_context(nc.sbuf_tensor("rcp", [1, T], F32))
        brow = es.enter_context(nc.sbuf_tensor("brow", [1, 4], F32))
        bias_sb = es.enter_context(nc.sbuf_tensor("bias_sb", [P, 1], F32))
        s_ps = [es.enter_context(nc.psum_tensor(f"s_ps{i}", [P, T], F32)) for i in range(2)]
        o_ps = es.enter_context(nc.psum_tensor("o_ps", [P, T], F32))
        r_ps = es.enter_context(nc.psum_tensor("r_ps", [1, T], F32))
        ld = nc.alloc_semaphore("at_ld")
        qs = nc.alloc_semaphore("at_qs")
        xs = nc.alloc_semaphore("at_xs")
        mk = nc.alloc_semaphore("at_mk")
        pv = nc.alloc_semaphore("at_pv")
        ev = nc.alloc_semaphore("at_ev")
        st = nc.alloc_semaphore("at_st")
        nr = nc.alloc_semaphore("at_nr")

        C = dict(ld=0, qs=0, xs=0, mk=0, pv=0, ev=0, st=0, nr=0)

        qkv3 = qkv.rearrange("(i r) t -> i r t", r=BLK0)

        for bh in range(8):
            b, cc = bh // 4, bh % 4
            i0 = 4 * b
            with nc.Block() as block:
                @block.gpsimd
                def _(g):
                    g.dma_start(out=Q[:, :].rearrange("p (i t) -> p i t", i=4),
                                in_=qkv3[i0:i0 + 4, cc * P:(cc + 1) * P, :]
                                .transpose([1, 0, 2])).then_inc(ld, 16)
                    g.dma_start(out=Kb[:, :].rearrange("p (i t) -> p i t", i=4),
                                in_=qkv3[i0:i0 + 4, 512 + cc * P:512 + (cc + 1) * P, :]
                                .transpose([1, 0, 2])).then_inc(ld, 16)
                    vsrc = qkv3[i0:i0 + 4, 1024:1536, :] \
                        .rearrange("i r t -> i (r t)") \
                        .rearrange("i (t v) -> i t v", v=512)[:, :, cc * P:(cc + 1) * P] \
                        .rearrange("i (th tl) v -> i th tl v", th=8)
                    vdst = V[:, :].rearrange("p (i th v) -> p i th v", i=4, th=8)
                    for i in range(4):
                        g.dma_start(out=vdst[:, i],
                                    in_=vsrc[i].transpose([1, 0, 2])).then_inc(ld, 16)
                    g.wait_ge(ld, C["ld"] + 96)
                C["ld"] += 96

            with nc.Block() as block:
                @block.scalar
                def _(s):
                    s.activation(out=sqb[:, :], in_=Q[:, :], func=AF.Square).then_inc(nr, 1)
                    s.wait_ge(nr, C["nr"] + 1)

                @block.tensor
                def _(t):
                    t.wait_ge(nr, C["nr"] + 1)
                    for i in range(8):
                        t.matmul(r_ps[:, 0:512], consts["ones1"][:, :],
                                 sqb[:, i * 512:(i + 1) * 512], start=True,
                                 stop=True).then_inc(qs, 1)
                        t.wait_ge(ev, C["ev"] + i + 1)

                @block.vector
                def _(v):
                    for i in range(8):
                        v.wait_ge(qs, C["qs"] + i + 1)
                        v.tensor_copy(out=nq[:, i * 512:(i + 1) * 512],
                                      in_=r_ps[:, 0:512]).then_inc(ev, 1)
                    v.wait_ge(ev, C["ev"] + 8)
            C["nr"] += 1
            C["qs"] += 8
            C["ev"] += 8
            with nc.Block() as block:
                @block.scalar
                def _(s):
                    s.activation(out=sqb[:, :], in_=Kb[:, :], func=AF.Square).then_inc(nr, 1)
                    s.wait_ge(nr, C["nr"] + 1)

                @block.tensor
                def _(t):
                    t.wait_ge(nr, C["nr"] + 1)
                    for i in range(8):
                        t.matmul(r_ps[:, 0:512], consts["ones1"][:, :],
                                 sqb[:, i * 512:(i + 1) * 512], start=True,
                                 stop=True).then_inc(qs, 1)
                        t.wait_ge(ev, C["ev"] + i + 1)

                @block.vector
                def _(v):
                    for i in range(8):
                        v.wait_ge(qs, C["qs"] + i + 1)
                        v.tensor_copy(out=nk[:, i * 512:(i + 1) * 512],
                                      in_=r_ps[:, 0:512]).then_inc(ev, 1)
                    v.wait_ge(ev, C["ev"] + 8)
            C["nr"] += 1
            C["qs"] += 8
            C["ev"] += 8
            with nc.Block() as block:
                @block.vector
                def _(v):
                    n0 = C["ev"]
                    v.reduce_max(out=brow[:, 0:1], in_=nq[:, :], axis=AX.X).then_inc(ev, 1)
                    v.wait_ge(ev, n0 + 1)
                    v.reduce_max(out=brow[:, 1:2], in_=nk[:, :], axis=AX.X).then_inc(ev, 1)
                    v.wait_ge(ev, n0 + 2)
                    v.tensor_mul(out=brow[:, 2:3], in0=brow[:, 0:1],
                                 in1=brow[:, 1:2]).then_inc(ev, 1)
                    v.wait_ge(ev, n0 + 3)
            C["ev"] += 3
            with nc.Block() as block:
                @block.scalar
                def _(s):
                    s.activation(out=brow[:, 3:4], in_=brow[:, 2:3], func=AF.Sqrt,
                                 scale=1.05 / 192.0).then_inc(nr, 1)
                    s.wait_ge(nr, C["nr"] + 1)
            C["nr"] += 1
            with nc.Block() as block:
                @block.vector
                def _(v):
                    v.tensor_scalar_mul(out=brow[:, 2:3], in0=brow[:, 3:4],
                                        scalar1=-1.0).then_inc(ev, 1)
                    v.wait_ge(ev, C["ev"] + 1)

                @block.tensor
                def _(t):
                    t.wait_ge(ev, C["ev"] + 1)
                    t.matmul(o_ps[:, 0:1], consts["onesr_f"][:, :], brow[:, 2:3],
                             start=True, stop=True).then_inc(qs, 1)

                @block.vector
                def _(v2):
                    v2.wait_ge(qs, C["qs"] + 1)
                    v2.tensor_copy(out=bias_sb[:, :], in_=o_ps[:, 0:1]).then_inc(ev, 1)
                    v2.wait_ge(ev, C["ev"] + 2)
            C["ev"] += 2
            C["qs"] += 1

            with nc.Block() as block:
                sched = []
                for qst in range(4):
                    nkt = 8 * qst + 8
                    for kt in range(nkt):
                        lo = max(0, (kt - 8 * qst)) * P if kt >= 8 * qst else 0
                        sched.append((qst, kt, lo))

                @block.tensor
                def _(t):
                    c_qs, c_xs, c_mk, c_ev = C["qs"], C["xs"], C["mk"], C["ev"]

                    def pv_mms(j):
                        qstj, ktj, loj = sched[j]
                        nktj = 8 * qstj + 8
                        slotj = j % 2
                        t.wait_ge(xs, c_xs + _expcalls(sched, j + 1))
                        if ktj >= 8 * qstj:
                            t.wait_ge(mk, c_mk + _diagcount(sched, j + 1))
                        first = (ktj == 0)
                        last = (ktj == nktj - 1)
                        mm2 = None
                        for h in range(2):
                            a = max(loj, 512 * h)
                            bnd = 512 * (h + 1)
                            if a < bnd:
                                t.matmul(o_ps[:, a:bnd], V[:, ktj * P:(ktj + 1) * P],
                                         Psb[slotj][:, a:bnd], start=first, stop=last)
                                mm2 = t.matmul(r_ps[:, a:bnd], consts["ones1"][:, :],
                                               Psb[slotj][:, a:bnd], start=first, stop=last)
                        mm2.then_inc(pv, 1)

                    it = 0
                    pvdone = -1
                    for qst in range(4):
                        nkt = 8 * qst + 8
                        for kt in range(nkt):
                            diag = kt >= 8 * qst
                            lo = (kt - 8 * qst) * P if diag else 0
                            slot = it % 2
                            if it >= 2:
                                t.wait_ge(xs, c_xs + _expcalls(sched, it - 1))
                            if kt == 0 and qst > 0:
                                t.wait_ge(ev, c_ev + 4 * (qst - 1) + 3)
                            mm = None
                            for h in range(2):
                                a = max(lo, 512 * h)
                                bnd = 512 * (h + 1)
                                if a < bnd:
                                    mm = t.matmul(s_ps[slot][:, a:bnd],
                                                  Kb[:, kt * P:(kt + 1) * P],
                                                  Q[:, qst * T + a:qst * T + bnd],
                                                  start=True, stop=True)
                            mm.then_inc(qs, 1)
                            if it >= 1 and it - 1 > pvdone:
                                pv_mms(it - 1)
                                pvdone = it - 1
                            it += 1
                        if it - 1 > pvdone:
                            pv_mms(it - 1)
                            pvdone = it - 1
                        t.wait_ge(ev, c_ev + 4 * qst + 2)
                        t.matmul(s_ps[0][:, 0:512], consts["onesr_f"][:, :],
                                 rcp[:, 0:512], start=True, stop=True)
                        t.matmul(s_ps[0][:, 512:T], consts["onesr_f"][:, :],
                                 rcp[:, 512:T], start=True, stop=True).then_inc(qs, 1)
                        t.wait_ge(ev, c_ev + 4 * qst + 4)

                @block.scalar
                def _(s):
                    c_qs = C["qs"]
                    it = 0
                    for qst in range(4):
                        nkt = 8 * qst + 8
                        for kt in range(nkt):
                            diag = kt >= 8 * qst
                            lo = (kt - 8 * qst) * P if diag else 0
                            slot = it % 2
                            s.wait_ge(qs, c_qs + it + 1 + qst)
                            if it >= 2:
                                s.wait_ge(pv, C["pv"] + it - 1)
                            if diag:
                                s.activation(out=band[:, :], in_=s_ps[slot][:, lo:lo + P],
                                             func=AF.Exp, scale=L0_SCALE,
                                             bias=bias_sb[:, :]).then_inc(xs, 1)
                                if lo + P < T:
                                    s.activation(out=Psb[slot][:, lo + P:T],
                                                 in_=s_ps[slot][:, lo + P:T],
                                                 func=AF.Exp, scale=L0_SCALE,
                                                 bias=bias_sb[:, :]).then_inc(xs, 1)
                            else:
                                s.activation(out=Psb[slot][:, :], in_=s_ps[slot][:, :],
                                             func=AF.Exp, scale=L0_SCALE,
                                             bias=bias_sb[:, :]).then_inc(xs, 1)
                            it += 1

                @block.vector
                def _(v):
                    c_xs, c_mk = C["xs"], C["mk"]
                    it = 0
                    nev = C["ev"]
                    for qst in range(4):
                        nkt = 8 * qst + 8
                        for kt in range(nkt):
                            diag = kt >= 8 * qst
                            lo = (kt - 8 * qst) * P if diag else 0
                            slot = it % 2
                            if diag:
                                v.wait_ge(xs, c_xs + _expcalls(sched, it) + 1)
                                v.tensor_mul(out=Psb[slot][:, lo:lo + P], in0=band[:, :],
                                             in1=consts["u01_b"][:, :]).then_inc(mk, 1)
                            it += 1
                        v.wait_ge(pv, C["pv"] + it)
                        v.tensor_copy(out=rsb[:, :], in_=r_ps[:, :]).then_inc(ev, 1)
                        v.wait_ge(ev, nev + 4 * qst + 1)
                        v.reciprocal(out=rcp[:, :], in_=rsb[:, :]).then_inc(ev, 1)
                        v.wait_ge(ev, nev + 4 * qst + 2)
                        v.tensor_copy(out=o_raw[:, :], in_=o_ps[:, :]).then_inc(ev, 1)
                        v.wait_ge(ev, nev + 4 * qst + 3)
                        v.wait_ge(qs, C["qs"] + it + qst + 1)
                        if qst > 0:
                            v.wait_ge(st, C["st"] + 16 * qst)
                        v.tensor_mul(out=o_out[:, :], in0=o_raw[:, :],
                                     in1=s_ps[0][:, :]).then_inc(ev, 1)
                        v.wait_ge(ev, nev + 4 * qst + 4)

                @block.sync
                def _(sy):
                    for qst in range(4):
                        sy.wait_ge(ev, C["ev"] + 4 * qst + 4)
                        dst = ob[(b * 4 + qst) * OBLK + cc * P:
                                 (b * 4 + qst) * OBLK + (cc + 1) * P, :]
                        sy.dma_start(out=dst, in_=o_out[:, :]).then_inc(st, 16)
                    sy.wait_ge(st, C["st"] + 64)

            nits = sum(8 * q + 8 for q in range(4))
            nexp = sum(_expcalls_per(sched, i) for i in range(nits))
            ndiag = sum(1 for (_q, _k, _l) in sched if _k >= 8 * _q)
            C["qs"] += nits + 4
            C["xs"] += nexp
            C["mk"] += ndiag
            C["pv"] += nits
            C["ev"] += 16
            C["st"] += 64
    return nc


def _expcalls_per(sched, i):
    qst, kt, lo = sched[i]
    diag = kt >= 8 * qst
    if diag:
        return 2 if lo + P < T else 1
    return 1


def _expcalls(sched, upto):
    return sum(_expcalls_per(sched, i) for i in range(min(upto, len(sched))))


def _diagcount(sched, upto):
    return sum(1 for (q, k, l) in sched[:min(upto, len(sched))] if k >= 8 * q)



def build_post(nc=None, ext=None, tag="c0"):
    if nc is None:
        nc = bass.Bass("TRN2", target_bir_lowering=False)
    io = ext if ext is not None else {}
    orc = io.get("orc") or nc.dram_tensor("orc", [NCORE * OBLK, T], BF16, kind="ExternalInput").ap()
    gfm = io.get("gfm") or nc.dram_tensor("gfm", [D, T], BF16, kind="ExternalInput").ap()
    xfm = io.get("xfm") or nc.dram_tensor("xfm", [D, T], F32, kind="ExternalInput").ap()
    wo = io.get("wo") or nc.dram_tensor("wo", [D, D], F32, kind="ExternalInput").ap()
    xo = io.get("xo") or nc.dram_tensor("xo", [D, T], F32, kind="ExternalOutput").ap()

    with contextlib.ExitStack() as es:
        attn_sb = es.enter_context(nc.sbuf_tensor("attn_sb" + tag, [P, 32 * T], BF16))
        wsl = [es.enter_context(nc.sbuf_tensor(f"po_w{i}" + tag, [P, NDC * P], BF16)) for i in range(2)]
        gsb = [es.enter_context(nc.sbuf_tensor(f"po_g{i}" + tag, [P, T], BF16)) for i in range(2)]
        xsb = [es.enter_context(nc.sbuf_tensor(f"po_x{i}" + tag, [P, T], F32)) for i in range(2)]
        tmp = es.enter_context(nc.sbuf_tensor("po_tmp" + tag, [P, T], F32))
        osb = [es.enter_context(nc.sbuf_tensor(f"po_o{i}" + tag, [P, T], F32)) for i in range(2)]
        ps = [es.enter_context(nc.psum_tensor(f"po_ps{i}" + tag, [P, T], F32)) for i in range(2)]
        ld = nc.alloc_semaphore("po_ld" + tag)
        wl = nc.alloc_semaphore("po_wl" + tag)
        pe = nc.alloc_semaphore("po_pe" + tag)
        ev = nc.alloc_semaphore("po_ev" + tag)
        st = nc.alloc_semaphore("po_st" + tag)

        with nc.Block() as block:
            @block.gpsimd
            def _(g):
                g.dma_start(out=attn_sb[:, :].rearrange("p (ac t) -> p ac t", ac=32),
                            in_=orc.rearrange("(ac p) t -> p ac t", p=P)).then_inc(ld, 16)
                g.wait_ge(ld, 16)
        with nc.Block() as block:
            @block.gpsimd
            def _(g):
                for oc in range(NDC):
                    if oc >= 2:
                        g.wait_ge(pe, oc - 1)
                    src = wo[:, oc * P:(oc + 1) * P].rearrange("(dc p) c -> p dc c", p=P)
                    g.dma_start(out=wsl[oc % 2][:, :].rearrange("p (dc c) -> p dc c", dc=NDC),
                                in_=src).then_inc(wl, 16)

            @block.sync
            def _(sy):
                for oc in range(NDC):
                    if oc >= 2:
                        sy.wait_ge(ev, 2 * (oc - 1))
                    sy.dma_start(out=gsb[oc % 2][:, :],
                                 in_=gfm[oc * P:(oc + 1) * P, :]).then_inc(ld, 16)
                    sy.dma_start(out=xsb[oc % 2][:, :],
                                 in_=xfm[oc * P:(oc + 1) * P, :]).then_inc(ld, 16)

            @block.tensor
            def _(t):
                for oc in range(NDC):
                    p = oc % 2
                    t.wait_ge(wl, 16 * (oc + 1))
                    if oc >= 2:
                        t.wait_ge(ev, 2 * (oc - 1))
                    for dc in range(NDC):
                        stt, spp = (dc == 0), (dc == NDC - 1)
                        lhsT = wsl[p][:, dc * P:(dc + 1) * P]
                        t.matmul(ps[p][:, 0:512], lhsT, attn_sb[:, dc * T:dc * T + 512],
                                 start=stt, stop=spp)
                        mm = t.matmul(ps[p][:, 512:T], lhsT,
                                      attn_sb[:, dc * T + 512:(dc + 1) * T],
                                      start=stt, stop=spp)
                    mm.then_inc(pe, 1)

            @block.vector
            def _(v):
                for oc in range(NDC):
                    p = oc % 2
                    v.wait_ge(pe, oc + 1)
                    v.wait_ge(ld, 16 * (1 + 2 * (oc + 1)))
                    if oc >= 2:
                        v.wait_ge(st, 16 * (oc - 1))
                    v.tensor_mul(out=tmp[:, :], in0=ps[p][:, :],
                                 in1=gsb[p][:, :]).then_inc(ev, 1)
                    v.wait_ge(ev, 2 * oc + 1)
                    v.tensor_add(out=osb[p][:, :], in0=tmp[:, :],
                                 in1=xsb[p][:, :]).then_inc(ev, 1)

            @block.scalar
            def _(s):
                for oc in range(NDC):
                    s.wait_ge(ev, 2 * (oc + 1))
                    s.dma_start(out=xo[oc * P:(oc + 1) * P, :],
                                in_=osb[oc % 2][:, :]).then_inc(st, 16)
                s.wait_ge(st, 16 * NDC)
    return nc



def build_l1_pre(nc=None, ext=None, tag=""):
    if nc is None:
        nc = bass.Bass("TRN2", target_bir_lowering=False)
    io = ext if ext is not None else {}
    def _in(name, shape):
        return io[name] if name in io else nc.dram_tensor(name, shape, F32, kind="ExternalInput").ap()
    x1 = _in("x1", [D, T])
    wq1 = _in("wq1", [D, D])
    wk1 = _in("wk1", [D, 1024])
    wv1 = _in("wv1", [D, 1024])
    wr1 = _in("wr1", [D, 8])
    br1 = _in("br1", [8, 1])
    acat1 = _in("acat1", [D, P])
    bcat1 = _in("bcat1", [P, D])
    qkv = io.get("qkv1") or nc.dram_tensor("qkv", [NCORE * BLK1, T], BF16, kind="ExternalOutput").ap()
    gout = io.get("gout1") or nc.dram_tensor("gout", [D, T], BF16, kind="ExternalOutput").ap()
    rowscr = nc.dram_tensor("rowscr1" + tag, [2, T], F32).ap()

    with contextlib.ExitStack() as es:
        consts = io.get("consts") or emit_consts(nc, es)
        xn_fm = es.enter_context(nc.sbuf_tensor("xn_fm1", [P, NDC * T], BF16))
        emit_ln(nc, es, x1, xn_fm, rowscr, tag="L1")

        sc_fm = es.enter_context(nc.sbuf_tensor("sc_fm", [8, T], BF16))
        with contextlib.ExitStack() as ls:
            wrsl = ls.enter_context(nc.sbuf_tensor("r1_w", [P, NDC * 8], BF16))
            brc = ls.enter_context(nc.sbuf_tensor("r1_br", [8, 1], F32))
            lgfm = ls.enter_context(nc.sbuf_tensor("r1_lgfm", [8, T], F32))
            lgtm = ls.enter_context(nc.sbuf_tensor("r1_lgtm", [P, 8 * 8], F32))
            nmax = ls.enter_context(nc.sbuf_tensor("r1_nmax", [P, 8], F32))
            ssum = ls.enter_context(nc.sbuf_tensor("r1_ssum", [P, 8], F32))
            srec = ls.enter_context(nc.sbuf_tensor("r1_srec", [P, 8], F32))
            etm = ls.enter_context(nc.sbuf_tensor("r1_etm", [P, 8 * 8], F32))
            stm = ls.enter_context(nc.sbuf_tensor("r1_stm", [P, 8 * 8], BF16))
            ps_g = ls.enter_context(nc.psum_tensor("r1_psg", [8, T], F32))
            ps_t = [ls.enter_context(nc.psum_tensor(f"r1_pst{i}", [P, 8], F32)) for i in range(2)]
            ps_m = [ls.enter_context(nc.psum_tensor(f"r1_psm{i}", [8, P], BF16)) for i in range(2)]
            ld = nc.alloc_semaphore("r1_ld")
            pe = nc.alloc_semaphore("r1_pe")
            ev = nc.alloc_semaphore("r1_ev")
            sf = nc.alloc_semaphore("r1_sf")

            with nc.Block() as block:
                @block.gpsimd
                def _(g):
                    g.dma_start(out=wrsl[:, :].rearrange("p (dc c) -> p dc c", dc=NDC),
                                in_=wr1.rearrange("(dc p) c -> p dc c", p=P)).then_inc(ld, 16)
                    g.dma_start(out=brc[:, :], in_=br1).then_inc(ld, 16)
                    g.wait_ge(ld, 32)
            with nc.Block() as block:
                @block.tensor
                def _(t):
                    for dc in range(NDC):
                        stt, spp = (dc == 0), (dc == NDC - 1)
                        lhsT = wrsl[:, dc * 8:(dc + 1) * 8]
                        t.matmul(ps_g[:, 0:512], lhsT, xn_fm[:, dc * T:dc * T + 512],
                                 start=stt, stop=spp)
                        t.matmul(ps_g[:, 512:T], lhsT, xn_fm[:, dc * T + 512:(dc + 1) * T],
                                 start=stt, stop=spp)
            with nc.Block() as block:
                @block.vector
                def _(v):
                    v.tensor_scalar(out=lgfm[:, :], in0=ps_g[:, :], scalar1=brc[:, :],
                                    scalar2=None, op0=ALU.add).then_inc(sf, 1)
                    v.wait_ge(sf, 1)
            with nc.Block() as block:
                @block.tensor
                def _(t):
                    for tt in range(8):
                        if tt >= 2:
                            t.wait_ge(ev, tt - 1)
                        t.transpose(ps_t[tt % 2][:, :], lgfm[:, tt * P:(tt + 1) * P],
                                    consts["ident_f"][0:8, 0:8]).then_inc(pe, 1)

                @block.vector
                def _(v):
                    for tt in range(8):
                        v.wait_ge(pe, tt + 1)
                        v.tensor_copy(out=lgtm[:, tt * 8:(tt + 1) * 8],
                                      in_=ps_t[tt % 2][:, :]).then_inc(ev, 1)
                    v.wait_ge(ev, 8)
            with nc.Block() as block:
                @block.vector
                def _(v):
                    v.tensor_reduce(out=nmax[:, :],
                                    in_=lgtm[:, :].rearrange("p (t e) -> p t e", e=8),
                                    axis=AX.X, op=ALU.max, negate=True).then_inc(sf, 1)
                    v.wait_ge(sf, 2)
            with nc.Block() as block:
                @block.scalar
                def _(s):
                    for tt in range(8):
                        s.activation(out=etm[:, tt * 8:(tt + 1) * 8],
                                     in_=lgtm[:, tt * 8:(tt + 1) * 8], func=AF.Exp,
                                     bias=nmax[:, tt:tt + 1]).then_inc(sf, 1)
                    s.wait_ge(sf, 10)
            with nc.Block() as block:
                @block.vector
                def _(v):
                    v.tensor_reduce(out=ssum[:, :],
                                    in_=etm[:, :].rearrange("p (t e) -> p t e", e=8),
                                    axis=AX.X, op=ALU.add).then_inc(sf, 1)
                    v.wait_ge(sf, 11)
                    v.reciprocal(out=srec[:, :], in_=ssum[:, :]).then_inc(sf, 1)
                    v.wait_ge(sf, 12)
                    for tt in range(8):
                        v.tensor_scalar(out=stm[:, tt * 8:(tt + 1) * 8],
                                        in0=etm[:, tt * 8:(tt + 1) * 8],
                                        scalar1=srec[:, tt:tt + 1], scalar2=8.0,
                                        op0=ALU.mult, op1=ALU.mult).then_inc(sf, 1)
                    v.wait_ge(sf, 20)
            with nc.Block() as block:
                @block.tensor
                def _(t):
                    for tt in range(8):
                        if tt >= 2:
                            t.wait_ge(ev, 8 + tt - 1)
                        t.transpose(ps_m[tt % 2][:, :], stm[:, tt * 8:(tt + 1) * 8],
                                    consts["ident_b"][:, :]).then_inc(pe, 1)

                @block.vector
                def _(v):
                    for tt in range(8):
                        v.wait_ge(pe, 8 + tt + 1)
                        v.tensor_copy(out=sc_fm[:, tt * P:(tt + 1) * P],
                                      in_=ps_m[tt % 2][:, :]).then_inc(ev, 1)
                    v.wait_ge(ev, 16)

        with contextlib.ExitStack() as ls:
            asl = ls.enter_context(nc.sbuf_tensor("l1_a", [P, NDC * P], BF16))
            bsl = ls.enter_context(nc.sbuf_tensor("l1_b", [P, D], BF16))
            low_sb = ls.enter_context(nc.sbuf_tensor("l1_low", [P, T], BF16))
            wgt = ls.enter_context(nc.sbuf_tensor("l1_wgt", [P, T], BF16))
            evg = [ls.enter_context(nc.sbuf_tensor(f"l1_e{i}", [P, T], BF16)) for i in range(2)]
            ps_l = ls.enter_context(nc.psum_tensor("l1_psl", [P, T], F32))
            ps_sc = ls.enter_context(nc.psum_tensor("l1_pssc", [P, T], F32))
            ps_o = [ls.enter_context(nc.psum_tensor(f"l1_pso{i}", [P, T], F32)) for i in range(2)]
            ld = nc.alloc_semaphore("l1_ld")
            pe = nc.alloc_semaphore("l1_pe")
            ev = nc.alloc_semaphore("l1_ev")
            st = nc.alloc_semaphore("l1_st")
            sf = nc.alloc_semaphore("l1_sf")

            with nc.Block() as block:
                @block.gpsimd
                def _(g):
                    g.dma_start(out=asl[:, :].rearrange("p (dc c) -> p dc c", dc=NDC),
                                in_=acat1.rearrange("(dc p) c -> p dc c", p=P)).then_inc(ld, 16)
                    g.dma_start(out=bsl[:, :], in_=bcat1).then_inc(ld, 16)
                    g.wait_ge(ld, 32)
            with nc.Block() as block:
                @block.tensor
                def _(t):
                    for dc in range(NDC):
                        stt, spp = (dc == 0), (dc == NDC - 1)
                        lhsT = asl[:, dc * P:(dc + 1) * P]
                        t.matmul(ps_l[:, 0:512], lhsT, xn_fm[:, dc * T:dc * T + 512],
                                 start=stt, stop=spp)
                        t.matmul(ps_l[:, 512:T], lhsT, xn_fm[:, dc * T + 512:(dc + 1) * T],
                                 start=stt, stop=spp)
                    t.matmul(ps_sc[:, 0:512], consts["r01_h8"][:, :], sc_fm[:, 0:512],
                             start=True, stop=True)
                    t.matmul(ps_sc[:, 512:T], consts["r01_h8"][:, :], sc_fm[:, 512:T],
                             start=True, stop=True)
            with nc.Block() as block:
                @block.vector
                def _(v):
                    v.tensor_copy(out=low_sb[:, :], in_=ps_l[:, :]).then_inc(sf, 1)
                    v.wait_ge(sf, 1)
                    v.tensor_mul(out=wgt[:, :], in0=low_sb[:, :],
                                 in1=ps_sc[:, :]).then_inc(sf, 1)
                    v.wait_ge(sf, 2)
            with nc.Block() as block:
                @block.tensor
                def _(t):
                    for oc in range(NDC):
                        p = oc % 2
                        if oc >= 2:
                            t.wait_ge(ev, oc - 1)
                        lhsT = bsl[:, oc * P:(oc + 1) * P]
                        t.matmul(ps_o[p][:, 0:512], lhsT, wgt[:, 0:512],
                                 start=True, stop=True)
                        t.matmul(ps_o[p][:, 512:T], lhsT, wgt[:, 512:T],
                                 start=True, stop=True).then_inc(pe, 1)

                @block.scalar
                def _(s):
                    for oc in range(NDC):
                        p = oc % 2
                        s.wait_ge(pe, oc + 1)
                        if oc >= 2:
                            s.wait_ge(st, 16 * (oc - 1))
                        s.activation(out=evg[p][:, :], in_=ps_o[p][:, :], func=AF.Silu,
                                     scale=2.0).then_inc(ev, 1)

                @block.sync
                def _(sy):
                    for oc in range(NDC):
                        sy.wait_ge(ev, oc + 1)
                        sy.dma_start(out=gout[oc * P:(oc + 1) * P, :],
                                     in_=evg[oc % 2][:, :]).then_inc(st, 16)
                    sy.wait_ge(st, 16 * NDC)

        q_cols = [(4 * j + cc) * P for j in range(NCORE) for cc in range(4)]
        q_rows = [j * BLK1 + cc * P for j in range(NCORE) for cc in range(4)]
        emit_wstat_proj(nc, wq1, q_cols, xn_fm, qkv, q_rows, "q1p")
        k_cols = [j * P for j in range(NCORE)]
        k_rows = [j * BLK1 + 512 for j in range(NCORE)]
        emit_wstat_proj(nc, wk1, k_cols, xn_fm, qkv, k_rows, "k1p")
        emit_v_tm(nc, wv1, xn_fm, qkv, 1024, BLK1, "v1p")
    return nc



class SC:
    def __init__(self, sem):
        self.sem = sem
        self.n = 0

    def inc(self, instr, k=1):
        self.n += k
        return instr.then_inc(self.sem, k)

    def wait(self, eng, n=None):
        eng.wait_ge(self.sem, self.n if n is None else n)


def _colmax_to_scalar(v, sc, src_col, scr32, row1):
    sc.inc(v.tensor_copy(out=scr32[0:64, 0:1], in_=src_col[64:128, :]))
    sc.wait(v)
    sc.inc(v.tensor_max(out=scr32[0:64, 1:2], in0=src_col[0:64, :], in1=scr32[0:64, 0:1]))
    sc.wait(v)
    sc.inc(v.tensor_copy(out=scr32[0:32, 2:3], in_=scr32[32:64, 1:2]))
    sc.wait(v)
    sc.inc(v.tensor_max(out=scr32[0:32, 3:4], in0=scr32[0:32, 1:2], in1=scr32[0:32, 2:3]))
    sc.wait(v)
    sc.inc(v.memset(scr32[0:32, 4:36], -1e30))
    sc.wait(v)
    sc.inc(v.tensor_copy(out=scr32[0:32, 4:5], in_=scr32[0:32, 3:4]))
    sc.wait(v)
    sc.inc(v.transpose(out=scr32[0:32, 36:68], in_=scr32[0:32, 4:36]))
    sc.wait(v)
    sc.inc(v.reduce_max(out=row1[:, 0:1], in_=scr32[0:1, 36:68], axis=AX.X))
    sc.wait(v)


def build_l1_attn(nc=None, ext=None):
    if nc is None:
        nc = bass.Bass("TRN2", target_bir_lowering=False)
    io = ext if ext is not None else {}
    qkv = io.get("qkv1r") or nc.dram_tensor("qkv", [NCORE * BLK1, T], BF16, kind="ExternalInput").ap()
    cosT = io.get("cosT") or nc.dram_tensor("cosT", [P, 4096], F32, kind="ExternalInput").ap()
    sinT = io.get("sinT") or nc.dram_tensor("sinT", [P, 4096], F32, kind="ExternalInput").ap()
    ob = io.get("ob1") or nc.dram_tensor("ob", [NCORE * OBLK, T], BF16, kind="ExternalOutput").ap()
    denk_d = nc.dram_tensor("denk_d", [1, 4096], F32).ap()
    denq_d = nc.dram_tensor("denq_d", [4, 4096], F32).ap()

    NS = 4096
    with contextlib.ExitStack() as es:
        consts = io.get("consts") or emit_consts(nc, es)
        u01x4 = es.enter_context(nc.sbuf_tensor("u01x4", [P, 512], BF16))
        swa4 = es.enter_context(nc.sbuf_tensor("swa4", [P, 1024], BF16))
        swa40 = es.enter_context(nc.sbuf_tensor("swa40", [P, 1024], BF16))
        cos_sb = es.enter_context(nc.sbuf_tensor("cos_sb", [P, NS], BF16))
        sin_sb = es.enter_context(nc.sbuf_tensor("sin_sb", [P, NS], BF16))
        qs4 = es.enter_context(nc.sbuf_tensor("qs4", [P, 4 * NS], BF16))
        sq4 = es.enter_context(nc.sbuf_tensor("sq4", [P, 4 * NS], BF16))
        k_in = es.enter_context(nc.sbuf_tensor("k_in", [P, NS], BF16))
        q_in = es.enter_context(nc.sbuf_tensor("q_in", [P, NS], BF16))
        ks = es.enter_context(nc.sbuf_tensor("ks", [P, NS], BF16))
        sk = es.enter_context(nc.sbuf_tensor("sk", [P, NS], BF16))
        G32 = es.enter_context(nc.sbuf_tensor("G32", [P, NS], F32))
        Vt = es.enter_context(nc.sbuf_tensor("Vt", [P, 32 * P], BF16))
        pool = [es.enter_context(nc.sbuf_tensor(f"pool{i}", [P, NS], BF16)) for i in range(3)]
        fscr = [es.enter_context(nc.sbuf_tensor("fscr0", [P, T], F32))]
        rbc = es.enter_context(nc.sbuf_tensor("rbc", [P, NS], BF16))
        row5 = es.enter_context(nc.sbuf_tensor("row5", [1, 512], F32))
        row6 = es.enter_context(nc.sbuf_tensor("row6", [1, 512], F32))
        scr32 = es.enter_context(nc.sbuf_tensor("scr32", [P, 68], F32))
        bias_g = es.enter_context(nc.sbuf_tensor("bias_g", [P, 4], F32))
        Gtm = es.enter_context(nc.sbuf_tensor("Gtm", [P, P], F32))
        ecol_all = es.enter_context(nc.sbuf_tensor("ecol_all", [P, 32], F32))
        qd = es.enter_context(nc.sbuf_tensor("qd", [P, 1024], BF16))
        A_sb = es.enter_context(nc.sbuf_tensor("A_sb", [P, 512], BF16))
        S_f32 = es.enter_context(nc.sbuf_tensor("S_f32", [P, P], F32))
        S_tmp = es.enter_context(nc.sbuf_tensor("S_tmp", [P, P], F32))
        S_bf = es.enter_context(nc.sbuf_tensor("S_bf", [P, P], BF16))
        P_w = es.enter_context(nc.sbuf_tensor("P_w", [P, 1024], BF16))
        P_wm = es.enter_context(nc.sbuf_tensor("P_wm", [P, 1024], BF16))
        rb_sb = es.enter_context(nc.sbuf_tensor("rb_sb", [P, 512], F32))
        y_sb = es.enter_context(nc.sbuf_tensor("y_sb", [P, 512], BF16))
        o_sb = es.enter_context(nc.sbuf_tensor("o_sb", [P, 512], BF16))
        rrow = row5
        rrec = row6
        rr05 = es.enter_context(nc.sbuf_tensor("rr05", [1, 512], F32))
        b_misc = es.enter_context(nc.psum_tensor("b_misc", [P, 384], F32))
        b_kd = es.enter_context(nc.psum_tensor("b_kd", [P, P], BF16))
        psA = es.enter_context(nc.psum_tensor("psA", [P, 512], F32))
        o_ps = es.enter_context(nc.psum_tensor("o_ps1", [P, 512], F32))
        ps_w = es.enter_context(nc.psum_tensor("ps_w", [P, 1024], F32))
        ps_y = es.enter_context(nc.psum_tensor("ps_y", [P, 512], F32))
        ps_rb = es.enter_context(nc.psum_tensor("ps_rb", [P, 512], F32))
        ps_r = ps_rb[0:1, :]

        semA = nc.alloc_semaphore("l1a_A")
        semB = nc.alloc_semaphore("l1a_B")
        semC = nc.alloc_semaphore("l1a_C")
        semL = nc.alloc_semaphore("l1a_L")
        semS = nc.alloc_semaphore("l1a_S")
        A, B_, C_, L, S_ = SC(semA), SC(semB), SC(semC), SC(semL), SC(semS)

        qkv3 = qkv.rearrange("(i r) t -> i r t", r=BLK1)

        with nc.Block() as block:
            @block.gpsimd
            def _(g):
                L.inc(g.dma_start(out=cos_sb[:, :], in_=cosT), 16)
                L.inc(g.dma_start(out=sin_sb[:, :], in_=sinT), 16)
                L.wait(g)

            @block.vector
            def _(v):
                for i in range(4):
                    C_.inc(v.tensor_copy(out=u01x4[:, i * P:(i + 1) * P],
                                         in_=consts["u01_b"][:, :]))
                    C_.inc(v.tensor_copy(out=swa4[:, i * 256:(i + 1) * 256],
                                         in_=consts["swa_m"][:, :]))
                    C_.inc(v.tensor_copy(out=swa40[:, i * 256 + P:(i + 1) * 256],
                                         in_=consts["swa_m"][:, P:256]))
                C_.inc(v.memset(swa40[:, 0:P], 0.0))
                for i in range(1, 4):
                    C_.inc(v.memset(swa40[:, i * 256:i * 256 + P], 0.0))
                C_.wait(v)

        for grp in range(2):
            b = grp
            i0 = 4 * b
            with nc.Block() as block:
                @block.gpsimd
                def _(g):
                    L.inc(g.dma_start(out=k_in[:, :].rearrange("p (i t) -> p i t", i=4),
                                      in_=qkv3[i0:i0 + 4, 512:512 + P, :]
                                      .transpose([1, 0, 2])), 16)
                    vsrc = qkv3[i0:i0 + 4, 640:768, :] \
                        .rearrange("i r t -> i (r t)") \
                        .rearrange("i (t v) -> i t v", v=P) \
                        .rearrange("i (th tl) v -> i th tl v", th=8)
                    vdst = Vt[:, :].rearrange("p (i th v) -> p i th v", i=4, th=8)
                    for i in range(4):
                        L.inc(g.dma_start(out=vdst[:, i],
                                          in_=vsrc[i].transpose([1, 0, 2])), 16)
                    L.wait(g)

            with nc.Block() as block:
                @block.vector
                def _(v):
                    C_.inc(v.reduce_max(out=bias_g[:, 3:4], in_=k_in[:, :], axis=AX.X))
                    C_.wait(v)
                    _colmax_to_scalar(v, C_, bias_g[:, 3:4], scr32, row5)
                    C_.inc(v.tensor_scalar_mul(out=row5[:, 1:2], in0=row5[:, 0:1],
                                               scalar1=-1.0))
                    C_.wait(v)

                @block.tensor
                def _(t):
                    C_.wait(t)
                    A.inc(t.matmul(ps_rb[:, 0:1], consts["onesr_f"][:, :], row5[:, 1:2],
                                   start=True, stop=True))

                @block.vector
                def _(v2):
                    A.wait(v2)
                    C_.inc(v2.tensor_copy(out=bias_g[:, 1:2], in_=ps_rb[:, 0:1]))
                    C_.wait(v2)
            with nc.Block() as block:
                @block.scalar
                def _(s):
                    B_.inc(s.activation(out=pool[0][:, :], in_=k_in[:, :], func=AF.Exp,
                                        bias=bias_g[:, 1:2]))
                    B_.wait(s)

                @block.tensor
                def _(t):
                    B_.wait(t)
                    for i in range(8):
                        if i >= 1:
                            C_.wait(t, C_.n + 2 * i - 1)
                        A.inc(t.matmul(ps_r, consts["ones1"][:, :],
                                       pool[0][:, i * 512:(i + 1) * 512],
                                       start=True, stop=True))

                @block.vector
                def _(v):
                    for i in range(8):
                        A.wait(v, A.n - 8 + i + 1)
                        v.tensor_copy(out=row5[:, :], in_=ps_r[:, :]).then_inc(semC, 1)
                        C_.n += 1
                        C_.wait(v)
                        v.reciprocal(out=row6[:, :], in_=row5[:, :]).then_inc(semC, 1)
                        C_.n += 1
                        C_.wait(v)

                @block.sync
                def _(sy):
                    for i in range(8):
                        sy.wait_ge(semC, C_.n - 16 + 2 * (i + 1))
                        S_.inc(sy.dma_start(out=denk_d[:, i * 512:(i + 1) * 512],
                                            in_=row6[:, :]), 16)
                    S_.wait(sy)
            with nc.Block() as block:
                @block.gpsimd
                def _(g):
                    L.inc(g.dma_start(out=rbc[:, :],
                                      in_=denk_d.to_broadcast((P, NS))), 16)
                    L.wait(g)
            with nc.Block() as block:
                @block.vector
                def _(v):
                    C_.inc(v.tensor_mul(out=ks[:, :], in0=pool[0][:, :], in1=rbc[:, :]))
                    C_.wait(v)
                    C_.inc(v.tensor_scalar_mul(out=pool[1][0:64, :], in0=k_in[64:128, :],
                                               scalar1=-1.0))
                    C_.inc(v.tensor_copy(out=pool[1][64:128, :], in_=k_in[0:64, :]))
                    C_.wait(v)
                    C_.inc(v.tensor_mul(out=pool[2][:, :], in0=k_in[:, :], in1=cos_sb[:, :]))
                    C_.inc(v.tensor_mul(out=pool[0][:, :], in0=pool[1][:, :], in1=sin_sb[:, :]))
                    C_.wait(v)
                    C_.inc(v.tensor_add(out=sk[:, :], in0=pool[2][:, :], in1=pool[0][:, :]))
                    C_.wait(v)
            with nc.Block() as block:
                @block.scalar
                def _(s):
                    for i in range(4):
                        sl = slice(i * T, (i + 1) * T)
                        B_.inc(s.activation(out=pool[0][:, sl], in_=k_in[:, sl], func=AF.Abs))
                        B_.wait(s)
                        B_.inc(s.activation(out=pool[1][:, sl], in_=pool[0][:, sl],
                                            func=AF.Exp, scale=-1.0))
                        B_.wait(s)
                        B_.inc(s.activation(out=fscr[0][:, :], in_=pool[1][:, sl],
                                            func=AF.Ln, bias=1.0))
                        B_.wait(s)

                @block.vector
                def _(v):
                    for i in range(4):
                        sl = slice(i * T, (i + 1) * T)
                        B_.wait(v, B_.n - 12 + 3 * (i + 1))
                        C_.inc(v.tensor_scalar_min(out=pool[0][:, sl], in0=k_in[:, sl],
                                                   scalar1=0.0))
                        C_.wait(v)
                        C_.inc(v.tensor_sub(out=G32[:, sl], in0=pool[0][:, sl],
                                            in1=fscr[0][:, :]))
                        C_.wait(v)

            for h in range(4):
                with nc.Block() as block:
                    @block.gpsimd
                    def _(g):
                        L.inc(g.dma_start(out=q_in[:, :].rearrange("p (i t) -> p i t", i=4),
                                          in_=qkv3[i0:i0 + 4, h * P:(h + 1) * P, :]
                                          .transpose([1, 0, 2])), 16)
                        L.wait(g)
                with nc.Block() as block:
                    @block.vector
                    def _(v):
                        C_.inc(v.reduce_max(out=bias_g[:, 3:4], in_=q_in[:, :], axis=AX.X))
                        C_.wait(v)
                        _colmax_to_scalar(v, C_, bias_g[:, 3:4], scr32, row5)
                        C_.inc(v.tensor_scalar_mul(out=row5[:, 1:2], in0=row5[:, 0:1],
                                                   scalar1=-1.0))
                        C_.wait(v)

                    @block.tensor
                    def _(t):
                        C_.wait(t)
                        A.inc(t.matmul(ps_rb[:, 0:1], consts["onesr_f"][:, :], row5[:, 1:2],
                                       start=True, stop=True))

                    @block.vector
                    def _(v2):
                        A.wait(v2)
                        C_.inc(v2.tensor_copy(out=bias_g[:, 0:1], in_=ps_rb[:, 0:1]))
                        C_.wait(v2)
                with nc.Block() as block:
                    @block.scalar
                    def _(s):
                        B_.inc(s.activation(out=pool[0][:, :], in_=q_in[:, :], func=AF.Exp,
                                            bias=bias_g[:, 0:1]))
                        B_.wait(s)

                    @block.tensor
                    def _(t):
                        B_.wait(t)
                        for i in range(8):
                            if i >= 1:
                                C_.wait(t, C_.n + 2 * i - 1)
                            A.inc(t.matmul(ps_r, consts["ones1"][:, :],
                                           pool[0][:, i * 512:(i + 1) * 512],
                                           start=True, stop=True))

                    @block.vector
                    def _(v):
                        for i in range(8):
                            A.wait(v, A.n - 8 + i + 1)
                            v.tensor_copy(out=row5[:, :], in_=ps_r[:, :]).then_inc(semC, 1)
                            C_.n += 1
                            C_.wait(v)
                            v.reciprocal(out=row6[:, :], in_=row5[:, :]).then_inc(semC, 1)
                            C_.n += 1
                            C_.wait(v)

                    @block.sync
                    def _(sy):
                        for i in range(8):
                            sy.wait_ge(semC, C_.n - 16 + 2 * (i + 1))
                            S_.inc(sy.dma_start(out=denq_d[h:h + 1, i * 512:(i + 1) * 512],
                                                in_=row6[:, :]), 16)
                        S_.wait(sy)
                with nc.Block() as block:
                    @block.gpsimd
                    def _(g):
                        L.inc(g.dma_start(out=rbc[:, :],
                                          in_=denq_d[h:h + 1, :].to_broadcast((P, NS))), 16)
                        L.wait(g)
                with nc.Block() as block:
                    @block.vector
                    def _(v):
                        C_.inc(v.tensor_mul(out=qs4[:, h * NS:(h + 1) * NS],
                                            in0=pool[0][:, :], in1=rbc[:, :]))
                        C_.wait(v)
                        C_.inc(v.tensor_scalar_mul(out=pool[1][0:64, :], in0=q_in[64:128, :],
                                                   scalar1=-1.0))
                        C_.inc(v.tensor_copy(out=pool[1][64:128, :], in_=q_in[0:64, :]))
                        C_.wait(v)
                        C_.inc(v.tensor_mul(out=pool[2][:, :], in0=q_in[:, :],
                                            in1=cos_sb[:, :]))
                        C_.inc(v.tensor_mul(out=pool[0][:, :], in0=pool[1][:, :],
                                            in1=sin_sb[:, :]))
                        C_.wait(v)
                        C_.inc(v.tensor_add(out=sq4[:, h * NS:(h + 1) * NS],
                                            in0=pool[2][:, :], in1=pool[0][:, :]))
                        C_.wait(v)

            with nc.Block() as block:
                @block.scalar
                def _(s):
                    B_.inc(s.activation(out=pool[0][:, :], in_=sk[:, :], func=AF.Square))
                    B_.wait(s)

                @block.tensor
                def _(t):
                    B_.wait(t)
                    for i in range(8):
                        if i >= 1:
                            t.wait_ge(semC, C_.n + 1 + 2 * i)
                        A.inc(t.matmul(ps_r, consts["ones1"][:, :],
                                       pool[0][:, i * 512:(i + 1) * 512],
                                       start=True, stop=True))

                @block.vector
                def _(v):
                    cols = [2, 4]
                    C_.inc(v.memset(row5[:, 2:3], 0.0))
                    C_.wait(v)
                    for i in range(8):
                        A.wait(v, A.n - 8 + i + 1)
                        v.reduce_max(out=row5[:, 3:4], in_=ps_r,
                                     axis=AX.X).then_inc(semC, 1)
                        C_.n += 1
                        C_.wait(v)
                        src, dst = cols[i % 2], cols[(i + 1) % 2]
                        v.tensor_max(out=row5[:, dst:dst + 1], in0=row5[:, src:src + 1],
                                     in1=row5[:, 3:4]).then_inc(semC, 1)
                        C_.n += 1
                        C_.wait(v)
                    C_.inc(v.tensor_copy(out=row5[:, 2:3], in_=row5[:, cols[8 % 2]:cols[8 % 2] + 1]))
                    C_.wait(v)
            with nc.Block() as block:
                @block.scalar
                def _(s):
                    pass

                @block.vector
                def _(v):
                    C_.inc(v.memset(row6[:, 0:1], 0.0))
                    C_.wait(v)
            for h in range(4):
                with nc.Block() as block:
                    @block.scalar
                    def _(s):
                        B_.inc(s.activation(out=pool[0][:, :],
                                            in_=sq4[:, h * NS:(h + 1) * NS],
                                            func=AF.Square))
                        B_.wait(s)

                    @block.tensor
                    def _(t):
                        B_.wait(t)
                        for i in range(8):
                            if i >= 1:
                                t.wait_ge(semC, C_.n + 2 * i - 1)
                            A.inc(t.matmul(ps_r, consts["ones1"][:, :],
                                           pool[0][:, i * 512:(i + 1) * 512],
                                           start=True, stop=True))

                    @block.vector
                    def _(v):
                        cols = [0, 5]
                        for i in range(8):
                            A.wait(v, A.n - 8 + i + 1)
                            v.reduce_max(out=row6[:, 1:2], in_=ps_r,
                                         axis=AX.X).then_inc(semC, 1)
                            C_.n += 1
                            C_.wait(v)
                            src, dst = cols[(h * 8 + i) % 2], cols[(h * 8 + i + 1) % 2]
                            v.tensor_max(out=row6[:, dst:dst + 1],
                                         in0=row6[:, src:src + 1],
                                         in1=row6[:, 1:2]).then_inc(semC, 1)
                            C_.n += 1
                            C_.wait(v)
            with nc.Block() as block:
                @block.vector
                def _(v):
                    C_.inc(v.tensor_mul(out=row6[:, 2:3], in0=row6[:, 0:1],
                                        in1=row5[:, 2:3]))
                    C_.wait(v)

                @block.scalar
                def _(s):
                    C_.wait(s)
                    B_.inc(s.activation(out=row6[:, 3:4], in_=row6[:, 2:3], func=AF.Sqrt,
                                        scale=1.05 / 128.0))
                    B_.wait(s)

                @block.vector
                def _(v2):
                    B_.wait(v2)
                    C_.inc(v2.tensor_scalar_mul(out=row6[:, 4:5], in0=row6[:, 3:4],
                                                scalar1=-1.0))
                    C_.wait(v2)

                @block.tensor
                def _(t):
                    C_.wait(t)
                    A.inc(t.matmul(ps_rb[:, 0:1], consts["onesr_f"][:, :], row6[:, 4:5],
                                   start=True, stop=True))

                @block.vector
                def _(v3):
                    A.wait(v3)
                    C_.inc(v3.tensor_copy(out=bias_g[:, 2:3], in_=ps_rb[:, 0:1]))
                    C_.wait(v3)

            with nc.Block() as block:
                @block.vector
                def _(v):
                    C_.inc(v.memset(S_f32[:, :], 0.0))
                    C_.inc(v.memset(S_bf[:, :], 0.0))
                    C_.wait(v)

            NCH = 32
            ebp_all = pool[1]
            ebm_all = pool[2]
            with nc.Block() as block:
                a0p, b0p, c0p = A.n, B_.n, C_.n

                @block.tensor
                def _(t):
                    for ch in range(NCH):
                        if ch >= 1:
                            t.wait_ge(semC, c0p + ch)
                        t.transpose(psA[:, 0:P], G32[:, ch * P:(ch + 1) * P],
                                    consts["ident_f"][:, :]).then_inc(semA, 1)
                        t.wait_ge(semC, c0p + ch + 1)
                        if ch >= 1:
                            t.wait_ge(semB, b0p + 3 * ch)
                        t.matmul(b_misc[:, 0:P], Gtm[:, :], consts["u01_f"][:, :],
                                 start=True, stop=True).then_inc(semA, 1)

                @block.vector
                def _(v):
                    for ch in range(NCH):
                        v.wait_ge(semA, a0p + 2 * ch + 1)
                        v.tensor_copy(out=Gtm[:, :], in_=psA[:, 0:P]).then_inc(semC, 1)

                @block.scalar
                def _(sx):
                    for ch in range(NCH):
                        sx.wait_ge(semA, a0p + 2 * ch + 2)
                        sx.activation(out=ebp_all[:, ch * P:(ch + 1) * P],
                                      in_=b_misc[:, 0:P], func=AF.Exp,
                                      scale=1.0 / 16.0).then_inc(semB, 1)
                        sx.activation(out=ebm_all[:, ch * P:(ch + 1) * P],
                                      in_=b_misc[:, 0:P], func=AF.Exp,
                                      scale=-1.0 / 16.0).then_inc(semB, 1)
                        sx.activation(out=ecol_all[:, ch:ch + 1],
                                      in_=b_misc[:, P - 1:P], func=AF.Exp,
                                      scale=1.0 / 16.0).then_inc(semB, 1)
                A.n = a0p + 2 * NCH
                B_.n = b0p + 3 * NCH
                C_.n = c0p + NCH

            kd_all = k_in
            kdtm_all = q_in
            with nc.Block() as block:
                @block.vector
                def _(v):
                    C_.inc(v.tensor_mul(out=kd_all[:, :], in0=ks[:, :], in1=ebm_all[:, :]))
                    C_.wait(v)
            with nc.Block() as block:
                a0k, c0k = A.n, C_.n

                @block.tensor
                def _(t):
                    for ch in range(NCH):
                        if ch >= 1:
                            t.wait_ge(semC, c0k + ch)
                        t.transpose(b_kd[:, :], kd_all[:, ch * P:(ch + 1) * P],
                                    consts["ident_b"][:, :]).then_inc(semA, 1)

                @block.vector
                def _(v):
                    for ch in range(NCH):
                        v.wait_ge(semA, a0k + ch + 1)
                        v.tensor_copy(out=kdtm_all[:, ch * P:(ch + 1) * P],
                                      in_=b_kd[:, :]).then_inc(semC, 1)
                A.n = a0k + NCH
                C_.n = c0k + NCH

            a0, b0, c0 = A.n, B_.n, C_.n
            NA, NB, NCC = 7, 1, 15
            with nc.Block() as block:
                @block.tensor
                def _(t):
                    for ch in range(NCH):
                        ab = a0 + ch * NA
                        cb = c0 + ch * NCC
                        chp = max(ch - 1, 0)
                        t.wait_ge(semC, cb + 4)
                        if ch >= 1:
                            t.wait_ge(semC, cb - NCC + 5)
                        slot = (ch % 2) * 512
                        for h in range(4):
                            mm = t.matmul(psA[:, h * P:(h + 1) * P],
                                          kd_all[:, ch * P:(ch + 1) * P],
                                          qd[:, slot + h * P:slot + (h + 1) * P],
                                          start=True, stop=True)
                        mm.then_inc(semA, 1)
                        t.wait_ge(semC, cb + 5)
                        if ch >= 1:
                            t.wait_ge(semC, cb - NCC + 15)
                            t.wait_ge(semC, cb - NCC + 8)
                        for h in range(4):
                            t.matmul(o_ps[:, h * P:(h + 1) * P], Vt[:, ch * P:(ch + 1) * P],
                                     A_sb[:, h * P:(h + 1) * P], start=True, stop=False)
                            mm = t.matmul(o_ps[:, h * P:(h + 1) * P], S_bf[:, :],
                                          qd[:, slot + h * P:slot + (h + 1) * P],
                                          start=False, stop=True)
                        mm.then_inc(semA, 1)
                        if ch >= 1:
                            t.wait_ge(semC, cb - NCC + 6)
                        t.matmul(b_misc[:, 0:P], kdtm_all[:, ch * P:(ch + 1) * P],
                                 Vt[:, ch * P:(ch + 1) * P],
                                 start=True, stop=True).then_inc(semA, 1)
                        if ch >= 1:
                            t.wait_ge(semB, b0 + ch)
                        for h in range(4):
                            t.matmul(ps_w[:, h * 256:h * 256 + P],
                                     sk[:, chp * P:(chp + 1) * P],
                                     sq4[:, h * NS + ch * P:h * NS + (ch + 1) * P],
                                     start=True, stop=True)
                            mm = t.matmul(ps_w[:, h * 256 + P:(h + 1) * 256],
                                          sk[:, ch * P:(ch + 1) * P],
                                          sq4[:, h * NS + ch * P:h * NS + (ch + 1) * P],
                                          start=True, stop=True)
                        mm.then_inc(semA, 1)
                        t.wait_ge(semC, cb + 9)
                        if ch >= 1:
                            t.wait_ge(semC, cb - NCC + 13)
                        for h in range(4):
                            t.matmul(ps_rb[0:1, h * P:(h + 1) * P], consts["ones1"][:, :],
                                     P_wm[:, h * 256:h * 256 + P], start=True, stop=False)
                            mm = t.matmul(ps_rb[0:1, h * P:(h + 1) * P], consts["ones1"][:, :],
                                          P_wm[:, h * 256 + P:(h + 1) * 256],
                                          start=False, stop=True)
                        mm.then_inc(semA, 1)
                        if ch >= 1:
                            t.wait_ge(semC, cb - NCC + 14)
                        for h in range(4):
                            t.matmul(ps_y[:, h * P:(h + 1) * P],
                                     Vt[:, chp * P:(chp + 1) * P],
                                     P_wm[:, h * 256:h * 256 + P], start=True, stop=False)
                            mm = t.matmul(ps_y[:, h * P:(h + 1) * P],
                                          Vt[:, ch * P:(ch + 1) * P],
                                          P_wm[:, h * 256 + P:(h + 1) * 256],
                                          start=False, stop=True)
                        mm.then_inc(semA, 1)
                        t.wait_ge(semC, cb + 12)
                        t.matmul(ps_rb[:, :], consts["onesr_f"][:, :], rr05[:, :],
                                 start=True, stop=True).then_inc(semA, 1)

                @block.scalar
                def _(sx):
                    for ch in range(NCH):
                        ab = a0 + ch * NA
                        cb = c0 + ch * NCC
                        sx.wait_ge(semA, ab + 4)
                        if ch >= 1:
                            sx.wait_ge(semC, cb - NCC + 9)
                        sx.activation(out=P_w[:, :], in_=ps_w[:, :], func=AF.Exp,
                                      scale=SWA_SCALE, bias=bias_g[:, 2:3]).then_inc(semB, 1)

                @block.vector
                def _(v):
                    for ch in range(NCH):
                        ab = a0 + ch * NA
                        bb = b0 + ch * NB
                        cb = c0 + ch * NCC
                        slot = (ch % 2) * 512
                        if ch >= 2:
                            v.wait_ge(semA, a0 + (ch - 2) * NA + 2)
                        for h in range(4):
                            v.tensor_mul(out=qd[:, slot + h * P:slot + (h + 1) * P],
                                         in0=qs4[:, h * NS + ch * P:h * NS + (ch + 1) * P],
                                         in1=ebp_all[:, ch * P:(ch + 1) * P]).then_inc(semC, 1)
                        v.wait_ge(semA, ab + 1)
                        v.tensor_mul(out=A_sb[:, :], in0=psA[:, :],
                                     in1=u01x4[:, :]).then_inc(semC, 1)
                        v.wait_ge(semA, ab + 3)
                        v.tensor_add(out=S_tmp[:, :], in0=S_f32[:, :],
                                     in1=b_misc[:, 0:P]).then_inc(semC, 1)
                        v.wait_ge(semC, cb + 6)
                        v.tensor_scalar(out=S_f32[:, :], in0=S_tmp[:, :],
                                        scalar1=ecol_all[:, ch:ch + 1], scalar2=None,
                                        op0=ALU.mult).then_inc(semC, 1)
                        v.wait_ge(semC, cb + 7)
                        v.wait_ge(semA, ab + 2)
                        v.tensor_copy(out=S_bf[:, :], in_=S_f32[:, :]).then_inc(semC, 1)
                        v.wait_ge(semB, bb + 1)
                        msk = swa40 if ch == 0 else swa4
                        v.tensor_mul(out=P_wm[:, :], in0=P_w[:, :],
                                     in1=msk[:, :]).then_inc(semC, 1)
                        v.wait_ge(semA, ab + 5)
                        v.tensor_copy(out=rrow[:, :], in_=ps_rb[0:1, :]).then_inc(semC, 1)
                        v.wait_ge(semC, cb + 10)
                        v.reciprocal(out=rrec[:, :], in_=rrow[:, :]).then_inc(semC, 1)
                        v.wait_ge(semC, cb + 11)
                        v.tensor_scalar_mul(out=rr05[:, :], in0=rrec[:, :],
                                            scalar1=0.5).then_inc(semC, 1)
                        v.wait_ge(semA, ab + 7)
                        v.tensor_copy(out=rb_sb[:, :], in_=ps_rb[:, :]).then_inc(semC, 1)
                        v.wait_ge(semC, cb + 13)
                        v.wait_ge(semA, ab + 6)
                        v.tensor_mul(out=y_sb[:, :], in0=ps_y[:, :],
                                     in1=rb_sb[:, :]).then_inc(semC, 1)
                        v.wait_ge(semC, cb + 14)
                        if ch >= 1:
                            v.wait_ge(semS, S_.n + 64 * ch)
                        v.scalar_tensor_tensor(out=o_sb[:, :], in0=o_ps[:, :], scalar=0.5,
                                               in1=y_sb[:, :], op0=ALU.mult,
                                               op1=ALU.add).then_inc(semC, 1)

                @block.sync
                def _(sy):
                    s_base = S_.n
                    for ch in range(NCH):
                        cb = c0 + ch * NCC
                        jj = b * 4 + ch // 8
                        tcol = (ch % 8) * P
                        sy.wait_ge(semC, cb + 15)
                        for h in range(4):
                            sy.dma_start(
                                out=ob[jj * OBLK + h * P:jj * OBLK + (h + 1) * P,
                                       tcol:tcol + P],
                                in_=o_sb[:, h * P:(h + 1) * P]).then_inc(semS, 16)
                    sy.wait_ge(semS, s_base + 64 * NCH)
            A.n = a0 + NCH * NA
            B_.n = b0 + NCH * NB
            C_.n = c0 + NCH * NCC
            S_.n += 64 * NCH
    return nc



_BUILD_CACHE = {}


def _get(name, fn):
    if name not in _BUILD_CACHE:
        _BUILD_CACHE[name] = fn()
    return _BUILD_CACHE[name]


def kernel(**inputs):
    inputs = {k: np.asarray(v, dtype=np.float32) for k, v in inputs.items()}
    if FUSED:
        return kernel_fused(inputs)
    w = _prep_weights(inputs)
    shards = _shard_x0(inputs)
    cores = list(range(NCORE))

    ncA = _get("l0_pre", build_l0_pre)
    resA = run_bass_kernel_spmd(
        ncA, [{"x0": shards[c], "wq": w["wq"], "wk": w["wk"], "wv": w["wv"],
               "wup": w["wup"], "gate": w["gate"], "ebias": w["ebias"],
               "acat": w["acat"], "bcat": w["bcat"]} for c in range(NCORE)], cores)
    qkv_recv = _swap_blocks([np.asarray(resA.results[c]["qkv"]) for c in range(NCORE)], BLK0)
    g0 = [np.asarray(resA.results[c]["gout"]) for c in range(NCORE)]

    ncB = _get("l0_attn", build_l0_attn)
    resB = run_bass_kernel_spmd(ncB, [{"qkv": qkv_recv[c]} for c in range(NCORE)], cores)
    o_recv = _swap_blocks([np.asarray(resB.results[c]["ob"]) for c in range(NCORE)], OBLK)

    ncC = _get("post", build_post)
    resC = run_bass_kernel_spmd(
        ncC, [{"orc": o_recv[c], "gfm": g0[c], "xfm": shards[c], "wo": w["wo0"]}
              for c in range(NCORE)], cores)
    x1 = [np.asarray(resC.results[c]["xo"]) for c in range(NCORE)]

    ncA1 = _get("l1_pre", build_l1_pre)
    resA1 = run_bass_kernel_spmd(
        ncA1, [{"x1": x1[c], "wq1": w["wq1"], "wk1": w["wk1"], "wv1": w["wv1"],
                "wr1": w["wr1"], "br1": w["br1"], "acat1": w["acat1"],
                "bcat1": w["bcat1"]} for c in range(NCORE)], cores)
    qkv1_recv = _swap_blocks([np.asarray(resA1.results[c]["qkv"]) for c in range(NCORE)], BLK1)
    g1 = [np.asarray(resA1.results[c]["gout"]) for c in range(NCORE)]

    ncB1 = _get("l1_attn", build_l1_attn)
    resB1 = run_bass_kernel_spmd(
        ncB1, [{"qkv": qkv1_recv[c], "cosT": w["cosT"], "sinT": w["sinT"]}
               for c in range(NCORE)], cores)
    o1_recv = _swap_blocks([np.asarray(resB1.results[c]["ob"]) for c in range(NCORE)], OBLK)

    resC1 = run_bass_kernel_spmd(
        ncC, [{"orc": o1_recv[c], "gfm": g1[c], "xfm": x1[c], "wo": w["wo1"]}
              for c in range(NCORE)], cores)

    out = np.empty((2, 4096, 4096), dtype=np.float32)
    for c in range(NCORE):
        b, qq = c // 4, c % 4
        out[b, qq * T:(qq + 1) * T, :] = np.asarray(resC1.results[c]["xo"]).T
    return out



def emit_a2a(nc, src, dst, name):
    with contextlib.ExitStack() as ls:
        cc = nc.alloc_semaphore("cc_" + name)
        with nc.Block() as block:
            @block.gpsimd
            def _(g):
                with allow_ici_all_to_all():
                    g.collective_compute("AllToAll", mybir.AluOpType.bypass,
                                         replica_groups=[list(range(NCORE))],
                                         ins=[src], outs=[dst]).then_inc(cc, 1)
                g.wait_ge(cc, 1)


def build_fused():
    nc = bass.Bass("TRN2", target_bir_lowering=False)
    ext = {}
    for name, shape in [("x0", [D, T]), ("wq", [D, D]), ("wk", [D, D]), ("wv", [D, D]),
                        ("wup", [D, D]), ("gate", [D, 4]), ("ebias", [4, 1]),
                        ("acat", [D, 32]), ("bcat", [32, D]), ("wo0", [D, D]),
                        ("wq1", [D, D]), ("wk1", [D, 1024]), ("wv1", [D, 1024]),
                        ("wr1", [D, 8]), ("br1", [8, 1]), ("acat1", [D, P]),
                        ("bcat1", [P, D]), ("wo1", [D, D]),
                        ("cosT", [P, 4096]), ("sinT", [P, 4096])]:
        ext[name] = nc.dram_tensor(name, shape, F32, kind="ExternalInput").ap()
    ext["qkv"] = nc.dram_tensor("qkv0s", [NCORE * BLK0, T], BF16).ap()
    qkv0r = nc.dram_tensor("qkv0r", [NCORE * BLK0, T], BF16).ap()
    ext["gout"] = nc.dram_tensor("g0d", [D, T], BF16).ap()
    ob0s = nc.dram_tensor("ob0s", [NCORE * OBLK, T], BF16).ap()
    ob0r = nc.dram_tensor("ob0r", [NCORE * OBLK, T], BF16).ap()
    x1d = nc.dram_tensor("x1d", [D, T], F32).ap()
    ext["qkv1"] = nc.dram_tensor("qkv1s", [NCORE * BLK1, T], BF16).ap()
    qkv1r = nc.dram_tensor("qkv1r", [NCORE * BLK1, T], BF16).ap()
    ext["gout1"] = nc.dram_tensor("g1d", [D, T], BF16).ap()
    ob1s = nc.dram_tensor("ob1s", [NCORE * OBLK, T], BF16).ap()
    ob1r = nc.dram_tensor("ob1r", [NCORE * OBLK, T], BF16).ap()
    x2 = nc.dram_tensor("x2", [D, T], F32, kind="ExternalOutput").ap()

    with contextlib.ExitStack() as es:
        ext["consts"] = emit_consts(nc, es)
        build_l0_pre(nc, ext)
        emit_a2a(nc, ext["qkv"], qkv0r, "q0")
        build_l0_attn(nc, {"consts": ext["consts"], "qkv": qkv0r, "ob": ob0s})
        emit_a2a(nc, ob0s, ob0r, "o0")
        build_post(nc, {"orc": ob0r, "gfm": ext["gout"], "xfm": ext["x0"],
                        "wo": ext["wo0"], "xo": x1d}, tag="c0")
        build_l1_pre(nc, {"consts": ext["consts"], "x1": x1d, "wq1": ext["wq1"],
                          "wk1": ext["wk1"], "wv1": ext["wv1"], "wr1": ext["wr1"],
                          "br1": ext["br1"], "acat1": ext["acat1"],
                          "bcat1": ext["bcat1"], "qkv1": ext["qkv1"],
                          "gout1": ext["gout1"]})
        emit_a2a(nc, ext["qkv1"], qkv1r, "q1")
        build_l1_attn(nc, {"consts": ext["consts"], "qkv1r": qkv1r,
                           "cosT": ext["cosT"], "sinT": ext["sinT"], "ob1": ob1s})
        emit_a2a(nc, ob1s, ob1r, "o1")
        build_post(nc, {"orc": ob1r, "gfm": ext["gout1"], "xfm": x1d,
                        "wo": ext["wo1"], "xo": x2}, tag="c1")
    return nc


FUSED = True


def kernel_fused(inputs):
    w = _prep_weights(inputs)
    shards = _shard_x0(inputs)
    nc = _get("fused", build_fused)
    in_maps = []
    for c in range(NCORE):
        m = {"x0": shards[c], "wq": w["wq"], "wk": w["wk"], "wv": w["wv"],
             "wup": w["wup"], "gate": w["gate"], "ebias": w["ebias"],
             "acat": w["acat"], "bcat": w["bcat"], "wo0": w["wo0"],
             "wq1": w["wq1"], "wk1": w["wk1"], "wv1": w["wv1"], "wr1": w["wr1"],
             "br1": w["br1"], "acat1": w["acat1"], "bcat1": w["bcat1"],
             "wo1": w["wo1"], "cosT": w["cosT"], "sinT": w["sinT"]}
        in_maps.append(m)
    res = run_bass_kernel_spmd(nc, in_maps, core_ids=list(range(NCORE)))
    out = np.empty((2, 4096, 4096), dtype=np.float32)
    for c in range(NCORE):
        b, qq = c // 4, c % 4
        out[b, qq * T:(qq + 1) * T, :] = np.asarray(res.results[c]["x2"]).T
    return out


# version 05
# speedup vs baseline: 1.0968x; 1.0968x over previous
import contextlib
import numpy as np
import ml_dtypes
import concourse.bass as bass
import concourse.mybir as mybir
from concourse.bass_utils import run_bass_kernel_spmd
from concourse.aot_env import allow_ici_all_to_all

F32 = mybir.dt.float32
BF16 = mybir.dt.bfloat16
AF = mybir.ActivationFunctionType
ALU = mybir.AluOpType
AX = mybir.AxisListType
P = 128
T = 1024
D = 4096
NDC = 32
NCORE = 8
BLK0 = 1536
BLK1 = 768
OBLK = 512
LN_EPS = 1e-5
L0_SCALE = 192.0 ** -0.5
SWA_SCALE = 128.0 ** -0.5
GATE_NORM = 16.0
DEBUG = False



def emit_consts(nc, es):
    c = {}
    c["ones1"] = es.enter_context(nc.sbuf_tensor("c_ones1", [P, 1], BF16))
    c["onesr_f"] = es.enter_context(nc.sbuf_tensor("c_onesr_f", [1, P], F32))
    c["ident_b"] = es.enter_context(nc.sbuf_tensor("c_ident_b", [P, P], BF16))
    c["ident_f"] = es.enter_context(nc.sbuf_tensor("c_ident_f", [P, P], F32))
    c["u01_b"] = es.enter_context(nc.sbuf_tensor("c_u01_b", [P, P], BF16))
    c["u01_f"] = es.enter_context(nc.sbuf_tensor("c_u01_f", [P, P], F32))
    c["iota_d"] = es.enter_context(nc.sbuf_tensor("c_iota_d", [P, P], F32))
    c["r01_e4"] = es.enter_context(nc.sbuf_tensor("c_r01e4", [4, 32], BF16))
    c["r01_h8"] = es.enter_context(nc.sbuf_tensor("c_r01h8", [8, P], BF16))
    c["swa_m"] = es.enter_context(nc.sbuf_tensor("c_swam", [P, 256], BF16))
    scr1 = es.enter_context(nc.sbuf_tensor("c_scr1", [P, 2 * P], F32))
    scr2 = es.enter_context(nc.sbuf_tensor("c_scr2", [P, 2 * P], F32))
    sf = nc.alloc_semaphore("c_sf")
    with nc.Block() as block:
        @block.gpsimd
        def _(g):
            g.memset(c["ones1"][:, :], 1.0)
            g.memset(c["onesr_f"][:, :], 1.0)
            g.iota(c["iota_d"][:, :], [[1, P]], channel_multiplier=-1,
                   allow_small_or_imprecise_dtypes=True)
            g.iota(scr1[:, 0:32], [[1, 32]], channel_multiplier=-8,
                   allow_small_or_imprecise_dtypes=True)
            g.iota(scr1[:, 32:32 + P], [[1, P]], channel_multiplier=-16,
                   allow_small_or_imprecise_dtypes=True)
    with nc.Block() as block:
        @block.vector
        def _(v):
            io = c["iota_d"][:, :]
            n = [0]
            def step(instr):
                n[0] += 1
                instr.then_inc(sf, 1)
                v.wait_ge(sf, n[0])
            step(v.tensor_scalar(out=c["ident_f"][:, :], in0=io, scalar1=0.0,
                                 scalar2=None, op0=ALU.is_equal))
            step(v.tensor_copy(out=c["ident_b"][:, :], in_=c["ident_f"][:, :]))
            step(v.tensor_scalar(out=c["u01_f"][:, :], in0=io, scalar1=0.0,
                                 scalar2=None, op0=ALU.is_ge))
            step(v.tensor_copy(out=c["u01_b"][:, :], in_=c["u01_f"][:, :]))
            step(v.tensor_scalar(out=c["swa_m"][:, 0:P], in0=io, scalar1=-64.0,
                                 scalar2=None, op0=ALU.is_le))
            step(v.tensor_scalar(out=scr2[:, 0:P], in0=io, scalar1=64.0,
                                 scalar2=None, op0=ALU.is_le))
            step(v.tensor_mul(out=c["swa_m"][:, P:256], in0=scr2[:, 0:P],
                              in1=c["u01_f"][:, :]))
            step(v.tensor_scalar(out=scr2[:, 0:32], in0=scr1[:, 0:32], scalar1=0.0,
                                 scalar2=None, op0=ALU.is_ge))
            step(v.tensor_scalar(out=scr2[:, 32:64], in0=scr1[:, 0:32], scalar1=7.0,
                                 scalar2=None, op0=ALU.is_le))
            step(v.tensor_mul(out=c["r01_e4"][:, :], in0=scr2[0:4, 0:32],
                              in1=scr2[0:4, 32:64]))
            step(v.tensor_scalar(out=scr2[:, 0:P], in0=scr1[:, 32:32 + P], scalar1=0.0,
                                 scalar2=None, op0=ALU.is_ge))
            step(v.tensor_scalar(out=scr2[:, P:2 * P], in0=scr1[:, 32:32 + P],
                                 scalar1=15.0, scalar2=None, op0=ALU.is_le))
            step(v.tensor_mul(out=c["r01_h8"][:, :], in0=scr2[0:8, 0:P],
                              in1=scr2[0:8, P:2 * P]))
    return c


def emit_ln(nc, es, x_dram, xn_fm, rowscr_dram, tag=""):
    with contextlib.ExitStack() as ls:
        sb = lambda n, s, d: ls.enter_context(nc.sbuf_tensor(n + tag, s, d))
        xf = [sb(f"ln_xf{i}", [P, T], F32) for i in range(2)]
        xb = [sb(f"ln_xb{i}", [P, T], BF16) for i in range(2)]
        xq = [sb(f"ln_xq{i}", [P, T], BF16) for i in range(2)]
        rows = sb("ln_rows", [1, 8 * T], F32)
        c1b = sb("ln_c1b", [P, T], F32)
        c0b = sb("ln_c0b", [P, T], F32)
        tmp = sb("ln_tmp", [P, T], F32)
        ones1 = sb("ln_ones", [P, 1], BF16)
        eps_sb = sb("ln_eps", [1, 1], F32)
        ps_sum = ls.enter_context(nc.psum_tensor("ln_pssum" + tag, [1, T], F32))
        ps_sq = ls.enter_context(nc.psum_tensor("ln_pssq" + tag, [1, T], F32))
        ld = nc.alloc_semaphore("ln_ld" + tag)
        cp = nc.alloc_semaphore("ln_cp" + tag)
        sq = nc.alloc_semaphore("ln_sq" + tag)
        pe = nc.alloc_semaphore("ln_pe" + tag)

        with nc.Block() as block:
            @block.sync
            def _(sy):
                for i in range(NDC):
                    if i >= 2:
                        sy.wait_ge(pe, i - 1)
                        sy.wait_ge(sq, i - 1)
                        sy.wait_ge(cp, i - 1)
                    sy.dma_start(out=xf[i % 2][:, :],
                                 in_=x_dram[i * P:(i + 1) * P, :]).then_inc(ld, 16)

            @block.gpsimd
            def _(g):
                g.memset(ones1[:, :], 1.0)
                g.memset(eps_sb[:, :], LN_EPS)

            @block.vector
            def _(v):
                for i in range(NDC):
                    v.wait_ge(ld, 16 * (i + 1))
                    if i >= 2:
                        v.wait_ge(pe, i - 1)
                    v.tensor_copy(out=xb[i % 2][:, :], in_=xf[i % 2][:, :]).then_inc(cp, 1)

            @block.scalar
            def _(s):
                for i in range(NDC):
                    s.wait_ge(ld, 16 * (i + 1))
                    if i >= 2:
                        s.wait_ge(pe, i - 1)
                    s.activation(out=xq[i % 2][:, :], in_=xf[i % 2][:, :],
                                 func=AF.Square).then_inc(sq, 1)

            @block.tensor
            def _(t):
                for i in range(NDC):
                    p = i % 2
                    t.wait_ge(cp, i + 1)
                    t.wait_ge(sq, i + 1)
                    st, sp = (i == 0), (i == NDC - 1)
                    t.matmul(ps_sum[:, 0:512], ones1[:, :], xb[p][:, 0:512], start=st, stop=sp)
                    t.matmul(ps_sum[:, 512:T], ones1[:, :], xb[p][:, 512:T], start=st, stop=sp)
                    t.matmul(ps_sq[:, 0:512], ones1[:, :], xq[p][:, 0:512], start=st, stop=sp)
                    t.matmul(ps_sq[:, 512:T], ones1[:, :], xq[p][:, 512:T],
                             start=st, stop=sp).then_inc(pe, 1)

        SUM, SQ, M, MSQ, VAR, STD, C1, C0 = range(8)
        r = lambda k: rows[:, k * T:(k + 1) * T]
        with nc.Block() as block:
            @block.vector
            def _(v):
                nn = [0]
                def step(instr):
                    nn[0] += 1
                    instr.then_inc(cp, 1)
                    v.wait_ge(cp, NDC + nn[0])
                step(v.tensor_copy(out=r(SUM), in_=ps_sum[:, :]))
                step(v.tensor_copy(out=r(SQ), in_=ps_sq[:, :]))
                step(v.tensor_scalar_mul(out=r(M), in0=r(SUM), scalar1=1.0 / D))
                step(v.tensor_mul(out=r(MSQ), in0=r(M), in1=r(M)))
                step(v.scalar_tensor_tensor(out=r(VAR), in0=r(SQ), scalar=1.0 / D,
                                            in1=r(MSQ), op0=ALU.mult, op1=ALU.subtract))
        with nc.Block() as block:
            @block.scalar
            def _(s):
                s.activation(out=r(STD), in_=r(VAR), func=AF.Sqrt,
                             bias=eps_sb[:, :]).then_inc(sq, 1)
                s.wait_ge(sq, NDC + 1)
        with nc.Block() as block:
            @block.vector
            def _(v):
                nn = [5]
                def step(instr):
                    nn[0] += 1
                    instr.then_inc(cp, 1)
                    v.wait_ge(cp, NDC + nn[0])
                step(v.reciprocal(out=r(C1), in_=r(STD)))
                step(v.tensor_mul(out=r(MSQ), in0=r(M), in1=r(C1)))
                step(v.tensor_scalar_mul(out=r(C0), in0=r(MSQ), scalar1=-1.0))
        with nc.Block() as block:
            @block.sync
            def _(sy):
                sy.dma_start(out=rowscr_dram[0:1, :], in_=r(C1)).then_inc(ld, 16)
                sy.dma_start(out=rowscr_dram[1:2, :], in_=r(C0)).then_inc(ld, 16)
                sy.wait_ge(ld, 16 * (NDC + 2))
                sy.dma_start(out=c1b[:, :],
                             in_=rowscr_dram[0:1, :].to_broadcast((P, T))).then_inc(ld, 16)
                sy.dma_start(out=c0b[:, :],
                             in_=rowscr_dram[1:2, :].to_broadcast((P, T))).then_inc(ld, 16)
                sy.wait_ge(ld, 16 * (NDC + 4))

        with nc.Block() as block:
            @block.sync
            def _(sy):
                for i in range(NDC):
                    if i >= 2:
                        sy.wait_ge(cp, NDC + 8 + i - 1)
                    sy.dma_start(out=xf[i % 2][:, :],
                                 in_=x_dram[i * P:(i + 1) * P, :]).then_inc(ld, 16)

            @block.vector
            def _(v):
                for i in range(NDC):
                    v.wait_ge(ld, 16 * (NDC + 4 + i + 1))
                    v.tensor_mul(out=tmp[:, :], in0=xf[i % 2][:, :], in1=c1b[:, :])
                    v.tensor_add(out=xn_fm[:, i * T:(i + 1) * T], in0=tmp[:, :],
                                 in1=c0b[:, :]).then_inc(cp, 1)
                v.wait_ge(cp, NDC + 8 + NDC)


def emit_wstat_proj(nc, w_dram, col_slices, xn_fm, out_dram, dest_rows, tag,
                    evac="copy", extra_mms=None, silu_scale=1.0, n_dc=NDC):
    nsl = len(col_slices)
    with contextlib.ExitStack() as ls:
        wsl = [ls.enter_context(nc.sbuf_tensor(f"{tag}_w{i}", [P, n_dc * P], BF16))
               for i in range(2)]
        evt = [ls.enter_context(nc.sbuf_tensor(f"{tag}_e{i}", [P, T], BF16))
               for i in range(2)]
        ps = [ls.enter_context(nc.psum_tensor(f"{tag}_ps{i}", [P, T], F32))
              for i in range(2)]
        ld = nc.alloc_semaphore(f"{tag}_ld")
        pe = nc.alloc_semaphore(f"{tag}_pe")
        ev = nc.alloc_semaphore(f"{tag}_ev")
        st_s = nc.alloc_semaphore(f"{tag}_st")

        with nc.Block() as block:
            @block.gpsimd
            def _(g):
                for i, c0 in enumerate(col_slices):
                    if i >= 2:
                        g.wait_ge(pe, i - 1)
                    src = w_dram[:, c0:c0 + P].rearrange("(dc p) c -> p dc c", p=P)
                    dst = wsl[i % 2][:, :].rearrange("p (dc c) -> p dc c", dc=n_dc)
                    g.dma_start(out=dst, in_=src).then_inc(ld, 16)

            @block.tensor
            def _(t):
                for i in range(nsl):
                    p = i % 2
                    t.wait_ge(ld, 16 * (i + 1))
                    if i >= 2:
                        t.wait_ge(ev, i - 1)
                    last = n_dc - 1 if extra_mms is None else None
                    for dc in range(n_dc):
                        lhsT = wsl[p][:, dc * P:(dc + 1) * P]
                        st = (dc == 0)
                        sp = (dc == last)
                        t.matmul(ps[p][:, 0:512], lhsT, xn_fm[:, dc * T:dc * T + 512],
                                 start=st, stop=sp if extra_mms is None else False)
                        mm = t.matmul(ps[p][:, 512:T], lhsT,
                                      xn_fm[:, dc * T + 512:(dc + 1) * T],
                                      start=st, stop=(sp if extra_mms is None else False))
                        if extra_mms is None and sp:
                            mm.then_inc(pe, 1)
                    if extra_mms is not None:
                        extra_mms(t, i, ps[p], pe)

            if evac == "copy":
                @block.vector
                def _(v):
                    for i in range(nsl):
                        p = i % 2
                        v.wait_ge(pe, i + 1)
                        if i >= 2:
                            v.wait_ge(st_s, 16 * (i - 1))
                        v.tensor_copy(out=evt[p][:, :], in_=ps[p][:, :]).then_inc(ev, 1)
            else:
                @block.scalar
                def _(s):
                    for i in range(nsl):
                        p = i % 2
                        s.wait_ge(pe, i + 1)
                        if i >= 2:
                            s.wait_ge(st_s, 16 * (i - 1))
                        s.activation(out=evt[p][:, :], in_=ps[p][:, :], func=AF.Silu,
                                     scale=silu_scale).then_inc(ev, 1)

            @block.sync
            def _(sy):
                for i in range(nsl):
                    p = i % 2
                    sy.wait_ge(ev, i + 1)
                    sy.dma_start(out=out_dram[dest_rows[i]:dest_rows[i] + P, :],
                                 in_=evt[p][:, :]).then_inc(st_s, 16)
                sy.wait_ge(st_s, 16 * nsl)


def emit_v_tm(nc, wv_dram, xn_fm, out_dram, vcols, blk, tag, extra=None):
    vpb = vcols // 8
    ngrp = vcols // 512
    vsec = blk - (vpb * T) // T
    vrow0 = blk - vpb
    with contextlib.ExitStack() as ls:
        wsl = [ls.enter_context(nc.sbuf_tensor(f"{tag}_w{i}", [P, NDC * 512], BF16))
               for i in range(2)]
        evt = [ls.enter_context(nc.sbuf_tensor(f"{tag}_e{i}", [P, 512], BF16))
               for i in range(2)]
        ps = [ls.enter_context(nc.psum_tensor(f"{tag}_ps{i}", [P, 512], F32))
              for i in range(2)]
        ld = nc.alloc_semaphore(f"{tag}_ld")
        pe = nc.alloc_semaphore(f"{tag}_pe")
        ev = nc.alloc_semaphore(f"{tag}_ev")
        st_s = nc.alloc_semaphore(f"{tag}_st")

        with nc.Block() as block:
            @block.gpsimd
            def _(g):
                for grp in range(ngrp):
                    if grp >= 2:
                        g.wait_ge(pe, (grp - 1) * 8)
                    src = wv_dram[:, grp * 512:(grp + 1) * 512].rearrange(
                        "(dc p) c -> p dc c", p=P)
                    g.dma_start(out=wsl[grp % 2][:, :].rearrange("p (dc c) -> p dc c", dc=NDC),
                                in_=src).then_inc(ld, 16)

            @block.tensor
            def _(t):
                it = 0
                for grp in range(ngrp):
                    t.wait_ge(ld, 16 * (grp + 1))
                    for tt in range(8):
                        it += 1
                        if it >= 3:
                            t.wait_ge(ev, it - 2)
                        pv = ps[it % 2]
                        for dc in range(NDC):
                            st, sp = (dc == 0), (dc == NDC - 1)
                            lhsT = xn_fm[:, dc * T + tt * P:dc * T + (tt + 1) * P]
                            mm = t.matmul(pv[:, :], lhsT,
                                          wsl[grp % 2][:, dc * 512:(dc + 1) * 512],
                                          start=st, stop=sp)
                            if extra is not None:
                                extra(t, grp, tt, dc, lhsT)
                            if sp:
                                mm.then_inc(pe, 1)

            @block.vector
            def _(v):
                it = 0
                for grp in range(ngrp):
                    for tt in range(8):
                        it += 1
                        v.wait_ge(pe, it)
                        if it >= 3:
                            v.wait_ge(st_s, 16 * (it - 2))
                        v.tensor_copy(out=evt[it % 2][:, :], in_=ps[it % 2][:, :]).then_inc(ev, 1)

            @block.sync
            def _(sy):
                it = 0
                for grp in range(ngrp):
                    for tt in range(8):
                        it += 1
                        sy.wait_ge(ev, it)
                        nblk = 512 // vpb
                        for bsub in range(nblk):
                            j = grp * nblk + bsub
                            r0 = j * blk + vrow0 + tt * (P * vpb) // T
                            nrows = (P * vpb) // T
                            dst = out_dram[r0:r0 + nrows, :].rearrange(
                                "r (a c) -> (r a) c", a=T // vpb)
                            sy.dma_start(out=dst,
                                         in_=evt[it % 2][:, bsub * vpb:(bsub + 1) * vpb]
                                         ).then_inc(st_s, 16)
                sy.wait_ge(st_s, 16 * ngrp * 8 * (512 // vpb))


def emit_gate_l0(nc, es, gate_dram, ebias_dram, xn_fm, consts, m_fm, dbg_lg=None, dbg_rm=None, dbg_mtm=None):
    with contextlib.ExitStack() as ls:
        gatesl = ls.enter_context(nc.sbuf_tensor("g0_w", [P, NDC * 4], BF16))
        eb = ls.enter_context(nc.sbuf_tensor("g0_eb", [4, 1], F32))
        lgfm = ls.enter_context(nc.sbuf_tensor("g0_lgfm", [4, T], F32))
        lgtm = ls.enter_context(nc.sbuf_tensor("g0_lgtm", [P, 8 * 4], F32))
        rmax = ls.enter_context(nc.sbuf_tensor("g0_rmax", [P, 8], F32))
        m_tm = ls.enter_context(nc.sbuf_tensor("g0_mtm", [P, 8 * 4], BF16))
        m32 = ls.enter_context(nc.sbuf_tensor("g0_m32", [P, 8 * 4], F32))
        m32b = ls.enter_context(nc.sbuf_tensor("g0_m32b", [P, 8 * 4], F32))
        ps_g = ls.enter_context(nc.psum_tensor("g0_psg", [4, T], F32))
        ps_t = [ls.enter_context(nc.psum_tensor(f"g0_pst{i}", [P, 4], F32)) for i in range(2)]
        ps_m = [ls.enter_context(nc.psum_tensor(f"g0_psm{i}", [4, P], BF16)) for i in range(2)]
        ld = nc.alloc_semaphore("g0_ld")
        pe = nc.alloc_semaphore("g0_pe")
        ev = nc.alloc_semaphore("g0_ev")
        sf = nc.alloc_semaphore("g0_sf")

        with nc.Block() as block:
            @block.gpsimd
            def _(g):
                g.dma_start(out=gatesl[:, :].rearrange("p (dc c) -> p dc c", dc=NDC),
                            in_=gate_dram.rearrange("(dc p) c -> p dc c", p=P)).then_inc(ld, 16)
                g.dma_start(out=eb[:, :], in_=ebias_dram).then_inc(ld, 16)
                g.wait_ge(ld, 32)
        with nc.Block() as block:
            @block.tensor
            def _(t):
                for dc in range(NDC):
                    st, sp = (dc == 0), (dc == NDC - 1)
                    lhsT = gatesl[:, dc * 4:(dc + 1) * 4]
                    t.matmul(ps_g[:, 0:512], lhsT, xn_fm[:, dc * T:dc * T + 512],
                             start=st, stop=sp)
                    t.matmul(ps_g[:, 512:T], lhsT, xn_fm[:, dc * T + 512:(dc + 1) * T],
                             start=st, stop=sp)
        with nc.Block() as block:
            @block.vector
            def _(v):
                v.tensor_scalar(out=lgfm[:, :], in0=ps_g[:, :], scalar1=eb[:, :],
                                scalar2=None, op0=ALU.add).then_inc(sf, 1)
                v.wait_ge(sf, 1)
        with nc.Block() as block:
            @block.tensor
            def _(t):
                for tt in range(8):
                    if tt >= 2:
                        t.wait_ge(ev, tt - 1)
                    t.transpose(ps_t[tt % 2][:, :], lgfm[:, tt * P:(tt + 1) * P],
                                consts["ident_f"][0:4, 0:4]).then_inc(pe, 1)

            @block.vector
            def _(v):
                for tt in range(8):
                    v.wait_ge(pe, tt + 1)
                    v.tensor_copy(out=lgtm[:, tt * 4:(tt + 1) * 4],
                                  in_=ps_t[tt % 2][:, :]).then_inc(ev, 1)
                v.wait_ge(ev, 8)
        with nc.Block() as block:
            @block.vector
            def _(v):
                v.tensor_reduce(out=rmax[:, :],
                                in_=lgtm[:, :].rearrange("p (t e) -> p t e", e=4),
                                axis=AX.X, op=ALU.max).then_inc(sf, 1)
                v.wait_ge(sf, 2)
                for e in range(3):
                    v.tensor_copy(
                        out=m32[:, :].rearrange("p (t e) -> p t e", e=4)[:, :, e],
                        in_=rmax[:, :])
                v.tensor_copy(
                    out=m32[:, :].rearrange("p (t e) -> p t e", e=4)[:, :, 3],
                    in_=rmax[:, :]).then_inc(sf, 1)
                v.wait_ge(sf, 3)
                v.tensor_sub(out=m32b[:, :], in0=lgtm[:, :], in1=m32[:, :]).then_inc(sf, 1)
                v.wait_ge(sf, 4)
                v.tensor_scalar(out=m_tm[:, :], in0=m32b[:, :], scalar1=0.0,
                                scalar2=None, op0=ALU.is_ge).then_inc(sf, 1)
                v.wait_ge(sf, 5)
        if dbg_lg is not None:
            with nc.Block() as block:
                @block.sync
                def _(sy):
                    sy.dma_start(out=dbg_lg, in_=lgtm[:, :]).then_inc(ld, 16)
                    sy.dma_start(out=dbg_rm, in_=rmax[:, :]).then_inc(ld, 16)
                    sy.wait_ge(ld, 64)
                @block.gpsimd
                def _(g):
                    g.dma_start(out=dbg_mtm, in_=m_tm[:, :]).then_inc(ld, 16)
                    g.wait_ge(ld, 80)
        with nc.Block() as block:
            @block.tensor
            def _(t):
                for tt in range(8):
                    if tt >= 2:
                        t.wait_ge(ev, 8 + tt - 1)
                    t.transpose(ps_m[tt % 2][:, :], m_tm[:, tt * 4:(tt + 1) * 4],
                                consts["ident_b"][:, :]).then_inc(pe, 1)

            @block.vector
            def _(v):
                for tt in range(8):
                    v.wait_ge(pe, 8 + tt + 1)
                    v.tensor_copy(out=m_fm[:, tt * P:(tt + 1) * P],
                                  in_=ps_m[tt % 2][:, :]).then_inc(ev, 1)
                v.wait_ge(ev, 16)


def emit_ffn_l0(nc, es, acat_dram, bcat_dram, wup_dram, xn_fm, m_fm, consts, gout,
                dbg_m=None, dbg_zw=None, dbg_z2=None, dbg_z3=None):
    zw_sb = es.enter_context(nc.sbuf_tensor("f0_zw", [32, T], BF16))
    with contextlib.ExitStack() as ls:
        acat_sl = ls.enter_context(nc.sbuf_tensor("f0_a", [P, NDC * 32], BF16))
        z_sb = ls.enter_context(nc.sbuf_tensor("f0_z", [32, T], BF16))
        ps_z = ls.enter_context(nc.psum_tensor("f0_psz", [32, T], F32))
        ps_me = ls.enter_context(nc.psum_tensor("f0_psme", [32, T], F32))
        ld = nc.alloc_semaphore("f0_ld")

        with nc.Block() as block:
            @block.gpsimd
            def _(g):
                g.dma_start(out=acat_sl[:, :].rearrange("p (dc c) -> p dc c", dc=NDC),
                            in_=acat_dram.rearrange("(dc p) c -> p dc c", p=P)).then_inc(ld, 16)
                g.wait_ge(ld, 16)
        with nc.Block() as block:
            @block.tensor
            def _(t):
                for dc in range(NDC):
                    st, sp = (dc == 0), (dc == NDC - 1)
                    lhsT = acat_sl[:, dc * 32:(dc + 1) * 32]
                    t.matmul(ps_z[:, 0:512], lhsT, xn_fm[:, dc * T:dc * T + 512],
                             start=st, stop=sp)
                    t.matmul(ps_z[:, 512:T], lhsT, xn_fm[:, dc * T + 512:(dc + 1) * T],
                             start=st, stop=sp)
                t.matmul(ps_me[:, 0:512], consts["r01_e4"][:, :], m_fm[:, 0:512],
                         start=True, stop=True)
                t.matmul(ps_me[:, 512:T], consts["r01_e4"][:, :], m_fm[:, 512:T],
                         start=True, stop=True)
        with nc.Block() as block:
            @block.vector
            def _(v):
                v.tensor_copy(out=z_sb[:, :], in_=ps_z[:, :]).then_inc(ld, 1)
                v.wait_ge(ld, 17)
        with nc.Block() as block:
            @block.vector
            def _(v):
                v.tensor_mul(out=zw_sb[:, :], in0=z_sb[:, :],
                             in1=ps_me[:, :]).then_inc(ld, 1)
                v.wait_ge(ld, 18)
        if dbg_m is not None:
            dscr = ls.enter_context(nc.sbuf_tensor("f0_dbg", [32, T], F32))
            dscr2 = ls.enter_context(nc.sbuf_tensor("f0_dbg2", [32, T], F32))
            dscr3 = ls.enter_context(nc.sbuf_tensor("f0_dbg3", [32, T], F32))
            dsem = nc.alloc_semaphore("f0_dsem")
            with nc.Block() as block:
                @block.vector
                def _(v):
                    v.tensor_copy(out=dscr[:, :], in_=zw_sb[:, :])
                    v.tensor_copy(out=dscr2[:, :], in_=z_sb[:, :])
                    v.tensor_copy(out=dscr3[:, :], in_=ps_me[:, :])
            with nc.Block() as block:
                @block.gpsimd
                def _(sy):
                    sy.dma_start(out=dbg_zw, in_=dscr[:, :]).then_inc(dsem, 16)
                    sy.dma_start(out=dbg_m, in_=m_fm[:, :]).then_inc(dsem, 16)
                    sy.dma_start(out=dbg_z2, in_=dscr2[:, :]).then_inc(dsem, 16)
                    sy.dma_start(out=dbg_z3, in_=dscr3[:, :]).then_inc(dsem, 16)
                    sy.wait_ge(dsem, 64)

    bcat_sb = es.enter_context(nc.sbuf_tensor("f0_b", [32, D], BF16))
    with contextlib.ExitStack() as ls:
        ld2 = nc.alloc_semaphore("f0_ld2")
        with nc.Block() as block:
            @block.gpsimd
            def _(g):
                g.dma_start(out=bcat_sb[:, :], in_=bcat_dram).then_inc(ld2, 16)
                g.wait_ge(ld2, 16)

        def extra(t, i, ps_tile, pe):
            lhsT = bcat_sb[:, i * P:(i + 1) * P]
            t.matmul(ps_tile[:, 0:512], lhsT, zw_sb[:, 0:512], start=False, stop=False)
            t.matmul(ps_tile[:, 512:T], lhsT, zw_sb[:, 512:T], start=False,
                     stop=True).then_inc(pe, 1)

        emit_wstat_proj(nc, wup_dram, [oc * P for oc in range(NDC)], xn_fm, gout,
                        [oc * P for oc in range(NDC)], "f0up", evac="silu",
                        extra_mms=extra)



def build_l0_pre(nc=None, ext=None, tag=""):
    if nc is None:
        nc = bass.Bass("TRN2", target_bir_lowering=False)
    io = ext if ext is not None else {}
    def _in(name, shape):
        return io[name] if name in io else nc.dram_tensor(name, shape, F32, kind="ExternalInput").ap()
    x0 = _in("x0", [D, T])
    wq = _in("wq", [D, D])
    wk = _in("wk", [D, D])
    wv = _in("wv", [D, D])
    wup = _in("wup", [D, D])
    gate = _in("gate", [D, 4])
    ebias = _in("ebias", [4, 1])
    acat = _in("acat", [D, 32])
    bcat = _in("bcat", [32, D])
    qkv = io.get("qkv") or nc.dram_tensor("qkv", [NCORE * BLK0, T], BF16, kind="ExternalOutput").ap()
    gout = io.get("gout") or nc.dram_tensor("gout", [D, T], BF16, kind="ExternalOutput").ap()
    rowscr = nc.dram_tensor("rowscr" + tag, [2, T], F32).ap()
    dbg_m = nc.dram_tensor("dbg_m", [4, T], F32, kind="ExternalOutput").ap() if DEBUG else None
    dbg_zw = nc.dram_tensor("dbg_zw", [32, T], F32, kind="ExternalOutput").ap() if DEBUG else None
    dbg_lg = nc.dram_tensor("dbg_lg", [P, 32], F32, kind="ExternalOutput").ap() if DEBUG else None
    dbg_rm = nc.dram_tensor("dbg_rm", [P, 8], F32, kind="ExternalOutput").ap() if DEBUG else None
    dbg_mtm = nc.dram_tensor("dbg_mtm", [P, 32], F32, kind="ExternalOutput").ap() if DEBUG else None
    dbg_z2 = nc.dram_tensor("dbg_z2", [32, T], F32, kind="ExternalOutput").ap() if DEBUG else None
    dbg_z3 = nc.dram_tensor("dbg_z3", [32, T], F32, kind="ExternalOutput").ap() if DEBUG else None
    dbg_xn = nc.dram_tensor("dbg_xn", [P, 4 * T], F32, kind="ExternalOutput").ap() if DEBUG else None

    with contextlib.ExitStack() as es:
        consts = io.get("consts") or emit_consts(nc, es)
        xn_fm = es.enter_context(nc.sbuf_tensor("xn_fm", [P, NDC * T], BF16))
        m_fm = es.enter_context(nc.sbuf_tensor("m_fm", [4, T], BF16))
        emit_ln(nc, es, x0, xn_fm, rowscr)
        if DEBUG:
            with contextlib.ExitStack() as dls:
                dsm = dnc.alloc_semaphore("dbgxn_sem")
                with nc.Block() as block:
                    @block.gpsimd
                    def _(g):
                        g.dma_start(out=dbg_xn, in_=xn_fm[:, 0:4 * T]).then_inc(dsm, 16)
                        g.wait_ge(dsm, 16)
        emit_gate_l0(nc, es, gate, ebias, xn_fm, consts, m_fm, dbg_lg, dbg_rm, dbg_mtm)
        q_cols = [(4 * j + cc) * P for j in range(NCORE) for cc in range(4)]
        q_rows = [j * BLK0 + cc * P for j in range(NCORE) for cc in range(4)]
        emit_wstat_proj(nc, wq, q_cols, xn_fm, qkv, q_rows, "qp")
        k_rows = [j * BLK0 + 512 + cc * P for j in range(NCORE) for cc in range(4)]
        emit_wstat_proj(nc, wk, q_cols, xn_fm, qkv, k_rows, "kp")
        emit_v_tm(nc, wv, xn_fm, qkv, D, BLK0, "vp")
        emit_ffn_l0(nc, es, acat, bcat, wup, xn_fm, m_fm, consts, gout,
                    dbg_m=dbg_m, dbg_zw=dbg_zw, dbg_z2=dbg_z2, dbg_z3=dbg_z3)
    return nc



def _swap_blocks(arrs, blk):
    recv = [np.empty_like(a) for a in arrs]
    for j in range(NCORE):
        for i in range(NCORE):
            recv[j][i * blk:(i + 1) * blk] = arrs[i][j * blk:(j + 1) * blk]
    return recv


def _prep_weights(inputs):
    w = {}
    ct = np.ascontiguousarray
    w["wq"] = ct(inputs["L0_attn_wq"].reshape(D, 32, 192)[:, :, :128].reshape(D, D))
    w["wk"] = ct(inputs["L0_attn_wkv"][:, :D])
    w["wv"] = ct(inputs["L0_attn_wkv"][:, D:2 * D])
    w["wo0"] = ct(inputs["L0_attn_wo"])
    w["wup"] = ct(inputs["L0_ffn_W_up"].T)
    w["gate"] = ct(inputs["L0_ffn_gate_w"].T)
    w["ebias"] = ct(inputs["L0_ffn_expert_bias"].reshape(4, 1))
    w["acat"] = ct(inputs["L0_ffn_lora_A"].reshape(32, D).T)
    w["bcat"] = ct(inputs["L0_ffn_lora_B"].transpose(0, 2, 1).reshape(32, D))
    w["wq1"] = ct(inputs["L1_attn_Wq"])
    w["wk1"] = ct(inputs["L1_attn_Wk"])
    w["wv1"] = ct(inputs["L1_attn_Wv"])
    w["wo1"] = ct(inputs["L1_attn_Wo"])
    w["wr1"] = ct(inputs["L1_ffn_Wr"].T)
    w["br1"] = ct(inputs["L1_ffn_br"].reshape(8, 1))
    w["acat1"] = ct(inputs["L1_ffn_A"].reshape(128, D).T)
    w["bcat1"] = ct(inputs["L1_ffn_B"].transpose(0, 2, 1).reshape(128, D))
    j = np.arange(64, dtype=np.float64)
    inv = 10000.0 ** (-2.0 * j / 128.0)
    t = np.arange(4096, dtype=np.float64)
    ang = t[None, :] * inv[:, None]
    ang2 = np.concatenate([ang, ang], axis=0)
    w["cosT"] = np.cos(ang2).astype(np.float32)
    w["sinT"] = np.sin(ang2).astype(np.float32)
    return w


def _shard_x0(inputs):
    x = inputs["L0_attn_hidden_states"]
    shards = []
    for c in range(NCORE):
        b, q = c // 4, c % 4
        shards.append(np.ascontiguousarray(x[b, q * T:(q + 1) * T, :].T.astype(np.float32)))
    return shards



def build_l0_attn(nc=None, ext=None):
    if nc is None:
        nc = bass.Bass("TRN2", target_bir_lowering=False)
    io = ext if ext is not None else {}
    qkv = io.get("qkv") or nc.dram_tensor("qkv", [NCORE * BLK0, T], BF16, kind="ExternalInput").ap()
    ob = io.get("ob") or nc.dram_tensor("ob", [NCORE * OBLK, T], BF16, kind="ExternalOutput").ap()

    with contextlib.ExitStack() as es:
        consts = io.get("consts") or emit_consts(nc, es)
        Q = es.enter_context(nc.sbuf_tensor("Q", [P, 4 * T], BF16))
        Kb = es.enter_context(nc.sbuf_tensor("Kb", [P, 4 * T], BF16))
        V = es.enter_context(nc.sbuf_tensor("V", [P, 32 * P], BF16))
        sqb = es.enter_context(nc.sbuf_tensor("sqb", [P, 4 * T], BF16))
        Psb = [es.enter_context(nc.sbuf_tensor(f"Psb{i}", [P, T], BF16)) for i in range(2)]
        band = es.enter_context(nc.sbuf_tensor("band", [P, P], BF16))
        o_raw = es.enter_context(nc.sbuf_tensor("o_raw", [P, T], BF16))
        o_out = es.enter_context(nc.sbuf_tensor("o_out", [P, T], BF16))
        nq = es.enter_context(nc.sbuf_tensor("nq", [1, 4 * T], F32))
        nk = es.enter_context(nc.sbuf_tensor("nk", [1, 4 * T], F32))
        rsb = es.enter_context(nc.sbuf_tensor("rsb", [1, T], F32))
        rcp = es.enter_context(nc.sbuf_tensor("rcp", [1, T], F32))
        brow = es.enter_context(nc.sbuf_tensor("brow", [1, 4], F32))
        bias_sb = es.enter_context(nc.sbuf_tensor("bias_sb", [P, 1], F32))
        s_ps = [es.enter_context(nc.psum_tensor(f"s_ps{i}", [P, T], F32)) for i in range(2)]
        o_ps = es.enter_context(nc.psum_tensor("o_ps", [P, T], F32))
        r_ps = es.enter_context(nc.psum_tensor("r_ps", [1, T], F32))
        ld = nc.alloc_semaphore("at_ld")
        qs = nc.alloc_semaphore("at_qs")
        xs = nc.alloc_semaphore("at_xs")
        mk = nc.alloc_semaphore("at_mk")
        pv = nc.alloc_semaphore("at_pv")
        ev = nc.alloc_semaphore("at_ev")
        st = nc.alloc_semaphore("at_st")
        nr = nc.alloc_semaphore("at_nr")

        C = dict(ld=0, qs=0, xs=0, mk=0, pv=0, ev=0, st=0, nr=0)

        qkv3 = qkv.rearrange("(i r) t -> i r t", r=BLK0)

        for bh in range(8):
            b, cc = bh // 4, bh % 4
            i0 = 4 * b
            with nc.Block() as block:
                @block.gpsimd
                def _(g):
                    g.dma_start(out=Q[:, :].rearrange("p (i t) -> p i t", i=4),
                                in_=qkv3[i0:i0 + 4, cc * P:(cc + 1) * P, :]
                                .transpose([1, 0, 2])).then_inc(ld, 16)
                    g.dma_start(out=Kb[:, :].rearrange("p (i t) -> p i t", i=4),
                                in_=qkv3[i0:i0 + 4, 512 + cc * P:512 + (cc + 1) * P, :]
                                .transpose([1, 0, 2])).then_inc(ld, 16)
                    vsrc = qkv3[i0:i0 + 4, 1024:1536, :] \
                        .rearrange("i r t -> i (r t)") \
                        .rearrange("i (t v) -> i t v", v=512)[:, :, cc * P:(cc + 1) * P] \
                        .rearrange("i (th tl) v -> i th tl v", th=8)
                    vdst = V[:, :].rearrange("p (i th v) -> p i th v", i=4, th=8)
                    for i in range(4):
                        g.dma_start(out=vdst[:, i],
                                    in_=vsrc[i].transpose([1, 0, 2])).then_inc(ld, 16)
                    g.wait_ge(ld, C["ld"] + 96)
                C["ld"] += 96

            with nc.Block() as block:
                @block.scalar
                def _(s):
                    s.activation(out=sqb[:, :], in_=Q[:, :], func=AF.Square).then_inc(nr, 1)
                    s.wait_ge(nr, C["nr"] + 1)

                @block.tensor
                def _(t):
                    t.wait_ge(nr, C["nr"] + 1)
                    for i in range(8):
                        t.matmul(r_ps[:, 0:512], consts["ones1"][:, :],
                                 sqb[:, i * 512:(i + 1) * 512], start=True,
                                 stop=True).then_inc(qs, 1)
                        t.wait_ge(ev, C["ev"] + i + 1)

                @block.vector
                def _(v):
                    for i in range(8):
                        v.wait_ge(qs, C["qs"] + i + 1)
                        v.tensor_copy(out=nq[:, i * 512:(i + 1) * 512],
                                      in_=r_ps[:, 0:512]).then_inc(ev, 1)
                    v.wait_ge(ev, C["ev"] + 8)
            C["nr"] += 1
            C["qs"] += 8
            C["ev"] += 8
            with nc.Block() as block:
                @block.scalar
                def _(s):
                    s.activation(out=sqb[:, :], in_=Kb[:, :], func=AF.Square).then_inc(nr, 1)
                    s.wait_ge(nr, C["nr"] + 1)

                @block.tensor
                def _(t):
                    t.wait_ge(nr, C["nr"] + 1)
                    for i in range(8):
                        t.matmul(r_ps[:, 0:512], consts["ones1"][:, :],
                                 sqb[:, i * 512:(i + 1) * 512], start=True,
                                 stop=True).then_inc(qs, 1)
                        t.wait_ge(ev, C["ev"] + i + 1)

                @block.vector
                def _(v):
                    for i in range(8):
                        v.wait_ge(qs, C["qs"] + i + 1)
                        v.tensor_copy(out=nk[:, i * 512:(i + 1) * 512],
                                      in_=r_ps[:, 0:512]).then_inc(ev, 1)
                    v.wait_ge(ev, C["ev"] + 8)
            C["nr"] += 1
            C["qs"] += 8
            C["ev"] += 8
            with nc.Block() as block:
                @block.vector
                def _(v):
                    n0 = C["ev"]
                    v.reduce_max(out=brow[:, 0:1], in_=nq[:, :], axis=AX.X).then_inc(ev, 1)
                    v.wait_ge(ev, n0 + 1)
                    v.reduce_max(out=brow[:, 1:2], in_=nk[:, :], axis=AX.X).then_inc(ev, 1)
                    v.wait_ge(ev, n0 + 2)
                    v.tensor_mul(out=brow[:, 2:3], in0=brow[:, 0:1],
                                 in1=brow[:, 1:2]).then_inc(ev, 1)
                    v.wait_ge(ev, n0 + 3)
            C["ev"] += 3
            with nc.Block() as block:
                @block.scalar
                def _(s):
                    s.activation(out=brow[:, 3:4], in_=brow[:, 2:3], func=AF.Sqrt,
                                 scale=1.05 / 192.0).then_inc(nr, 1)
                    s.wait_ge(nr, C["nr"] + 1)
            C["nr"] += 1
            with nc.Block() as block:
                @block.vector
                def _(v):
                    v.tensor_scalar_mul(out=brow[:, 2:3], in0=brow[:, 3:4],
                                        scalar1=-1.0).then_inc(ev, 1)
                    v.wait_ge(ev, C["ev"] + 1)

                @block.tensor
                def _(t):
                    t.wait_ge(ev, C["ev"] + 1)
                    t.matmul(o_ps[:, 0:1], consts["onesr_f"][:, :], brow[:, 2:3],
                             start=True, stop=True).then_inc(qs, 1)

                @block.vector
                def _(v2):
                    v2.wait_ge(qs, C["qs"] + 1)
                    v2.tensor_copy(out=bias_sb[:, :], in_=o_ps[:, 0:1]).then_inc(ev, 1)
                    v2.wait_ge(ev, C["ev"] + 2)
            C["ev"] += 2
            C["qs"] += 1

            with nc.Block() as block:
                sched = []
                for qst in range(4):
                    nkt = 8 * qst + 8
                    for kt in range(nkt):
                        lo = max(0, (kt - 8 * qst)) * P if kt >= 8 * qst else 0
                        sched.append((qst, kt, lo))

                @block.tensor
                def _(t):
                    c_qs, c_xs, c_mk, c_ev = C["qs"], C["xs"], C["mk"], C["ev"]

                    def pv_mms(j):
                        qstj, ktj, loj = sched[j]
                        nktj = 8 * qstj + 8
                        slotj = j % 2
                        t.wait_ge(xs, c_xs + _expcalls(sched, j + 1))
                        if ktj >= 8 * qstj:
                            t.wait_ge(mk, c_mk + _diagcount(sched, j + 1))
                        first = (ktj == 0)
                        last = (ktj == nktj - 1)
                        mm2 = None
                        for h in range(2):
                            a = max(loj, 512 * h)
                            bnd = 512 * (h + 1)
                            if a < bnd:
                                t.matmul(o_ps[:, a:bnd], V[:, ktj * P:(ktj + 1) * P],
                                         Psb[slotj][:, a:bnd], start=first, stop=last)
                                mm2 = t.matmul(r_ps[:, a:bnd], consts["ones1"][:, :],
                                               Psb[slotj][:, a:bnd], start=first, stop=last)
                        mm2.then_inc(pv, 1)

                    it = 0
                    for qst in range(4):
                        nkt = 8 * qst + 8
                        q_first = it
                        for kt in range(nkt):
                            diag = kt >= 8 * qst
                            lo = (kt - 8 * qst) * P if diag else 0
                            slot = it % 2
                            if it >= 2:
                                t.wait_ge(xs, c_xs + _expcalls(sched, it - 1))
                            if kt == 0 and qst > 0:
                                t.wait_ge(ev, c_ev + 4 * (qst - 1) + 3)
                            mm = None
                            for h in range(2):
                                a = max(lo, 512 * h)
                                bnd = 512 * (h + 1)
                                if a < bnd:
                                    mm = t.matmul(s_ps[slot][:, a:bnd],
                                                  Kb[:, kt * P:(kt + 1) * P],
                                                  Q[:, qst * T + a:qst * T + bnd],
                                                  start=True, stop=True)
                            mm.then_inc(qs, 1)
                            if it - 2 >= q_first:
                                pv_mms(it - 2)
                            it += 1
                        for j in (it - 2, it - 1):
                            if j >= q_first:
                                pv_mms(j)
                        t.wait_ge(ev, c_ev + 4 * qst + 2)
                        t.matmul(s_ps[0][:, 0:512], consts["onesr_f"][:, :],
                                 rcp[:, 0:512], start=True, stop=True)
                        t.matmul(s_ps[0][:, 512:T], consts["onesr_f"][:, :],
                                 rcp[:, 512:T], start=True, stop=True).then_inc(qs, 1)
                        t.wait_ge(ev, c_ev + 4 * qst + 4)

                @block.scalar
                def _(s):
                    c_qs = C["qs"]
                    it = 0
                    for qst in range(4):
                        nkt = 8 * qst + 8
                        for kt in range(nkt):
                            diag = kt >= 8 * qst
                            lo = (kt - 8 * qst) * P if diag else 0
                            slot = it % 2
                            s.wait_ge(qs, c_qs + it + 1 + qst)
                            if it >= 2:
                                s.wait_ge(pv, C["pv"] + it - 1)
                            if diag:
                                s.activation(out=band[:, :], in_=s_ps[slot][:, lo:lo + P],
                                             func=AF.Exp, scale=L0_SCALE,
                                             bias=bias_sb[:, :]).then_inc(xs, 1)
                                if lo + P < T:
                                    s.activation(out=Psb[slot][:, lo + P:T],
                                                 in_=s_ps[slot][:, lo + P:T],
                                                 func=AF.Exp, scale=L0_SCALE,
                                                 bias=bias_sb[:, :]).then_inc(xs, 1)
                            else:
                                s.activation(out=Psb[slot][:, :], in_=s_ps[slot][:, :],
                                             func=AF.Exp, scale=L0_SCALE,
                                             bias=bias_sb[:, :]).then_inc(xs, 1)
                            it += 1

                @block.vector
                def _(v):
                    c_xs, c_mk = C["xs"], C["mk"]
                    it = 0
                    nev = C["ev"]
                    for qst in range(4):
                        nkt = 8 * qst + 8
                        for kt in range(nkt):
                            diag = kt >= 8 * qst
                            lo = (kt - 8 * qst) * P if diag else 0
                            slot = it % 2
                            if diag:
                                v.wait_ge(xs, c_xs + _expcalls(sched, it) + 1)
                                v.tensor_mul(out=Psb[slot][:, lo:lo + P], in0=band[:, :],
                                             in1=consts["u01_b"][:, :]).then_inc(mk, 1)
                            it += 1
                        v.wait_ge(pv, C["pv"] + it)
                        v.tensor_copy(out=rsb[:, :], in_=r_ps[:, :]).then_inc(ev, 1)
                        v.wait_ge(ev, nev + 4 * qst + 1)
                        v.reciprocal(out=rcp[:, :], in_=rsb[:, :]).then_inc(ev, 1)
                        v.wait_ge(ev, nev + 4 * qst + 2)
                        v.tensor_copy(out=o_raw[:, :], in_=o_ps[:, :]).then_inc(ev, 1)
                        v.wait_ge(ev, nev + 4 * qst + 3)
                        v.wait_ge(qs, C["qs"] + it + qst + 1)
                        if qst > 0:
                            v.wait_ge(st, C["st"] + 16 * qst)
                        v.tensor_mul(out=o_out[:, :], in0=o_raw[:, :],
                                     in1=s_ps[0][:, :]).then_inc(ev, 1)
                        v.wait_ge(ev, nev + 4 * qst + 4)

                @block.sync
                def _(sy):
                    for qst in range(4):
                        sy.wait_ge(ev, C["ev"] + 4 * qst + 4)
                        dst = ob[(b * 4 + qst) * OBLK + cc * P:
                                 (b * 4 + qst) * OBLK + (cc + 1) * P, :]
                        sy.dma_start(out=dst, in_=o_out[:, :]).then_inc(st, 16)
                    sy.wait_ge(st, C["st"] + 64)

            nits = sum(8 * q + 8 for q in range(4))
            nexp = sum(_expcalls_per(sched, i) for i in range(nits))
            ndiag = sum(1 for (_q, _k, _l) in sched if _k >= 8 * _q)
            C["qs"] += nits + 4
            C["xs"] += nexp
            C["mk"] += ndiag
            C["pv"] += nits
            C["ev"] += 16
            C["st"] += 64
    return nc


def _expcalls_per(sched, i):
    qst, kt, lo = sched[i]
    diag = kt >= 8 * qst
    if diag:
        return 2 if lo + P < T else 1
    return 1


def _expcalls(sched, upto):
    return sum(_expcalls_per(sched, i) for i in range(min(upto, len(sched))))


def _diagcount(sched, upto):
    return sum(1 for (q, k, l) in sched[:min(upto, len(sched))] if k >= 8 * q)



def build_post(nc=None, ext=None, tag="c0"):
    if nc is None:
        nc = bass.Bass("TRN2", target_bir_lowering=False)
    io = ext if ext is not None else {}
    orc = io.get("orc") or nc.dram_tensor("orc", [NCORE * OBLK, T], BF16, kind="ExternalInput").ap()
    gfm = io.get("gfm") or nc.dram_tensor("gfm", [D, T], BF16, kind="ExternalInput").ap()
    xfm = io.get("xfm") or nc.dram_tensor("xfm", [D, T], F32, kind="ExternalInput").ap()
    wo = io.get("wo") or nc.dram_tensor("wo", [D, D], F32, kind="ExternalInput").ap()
    xo = io.get("xo") or nc.dram_tensor("xo", [D, T], F32, kind="ExternalOutput").ap()

    with contextlib.ExitStack() as es:
        attn_sb = es.enter_context(nc.sbuf_tensor("attn_sb" + tag, [P, 32 * T], BF16))
        wsl = [es.enter_context(nc.sbuf_tensor(f"po_w{i}" + tag, [P, NDC * P], BF16)) for i in range(2)]
        gsb = [es.enter_context(nc.sbuf_tensor(f"po_g{i}" + tag, [P, T], BF16)) for i in range(2)]
        xsb = [es.enter_context(nc.sbuf_tensor(f"po_x{i}" + tag, [P, T], F32)) for i in range(2)]
        tmp = es.enter_context(nc.sbuf_tensor("po_tmp" + tag, [P, T], F32))
        osb = [es.enter_context(nc.sbuf_tensor(f"po_o{i}" + tag, [P, T], F32)) for i in range(2)]
        ps = [es.enter_context(nc.psum_tensor(f"po_ps{i}" + tag, [P, T], F32)) for i in range(2)]
        ld = nc.alloc_semaphore("po_ld" + tag)
        wl = nc.alloc_semaphore("po_wl" + tag)
        pe = nc.alloc_semaphore("po_pe" + tag)
        ev = nc.alloc_semaphore("po_ev" + tag)
        st = nc.alloc_semaphore("po_st" + tag)

        with nc.Block() as block:
            @block.gpsimd
            def _(g):
                g.dma_start(out=attn_sb[:, :].rearrange("p (ac t) -> p ac t", ac=32),
                            in_=orc.rearrange("(ac p) t -> p ac t", p=P)).then_inc(ld, 16)
                g.wait_ge(ld, 16)
        with nc.Block() as block:
            @block.gpsimd
            def _(g):
                for oc in range(NDC):
                    if oc >= 2:
                        g.wait_ge(pe, oc - 1)
                    src = wo[:, oc * P:(oc + 1) * P].rearrange("(dc p) c -> p dc c", p=P)
                    g.dma_start(out=wsl[oc % 2][:, :].rearrange("p (dc c) -> p dc c", dc=NDC),
                                in_=src).then_inc(wl, 16)

            @block.sync
            def _(sy):
                for oc in range(NDC):
                    if oc >= 2:
                        sy.wait_ge(ev, 2 * (oc - 1))
                    sy.dma_start(out=gsb[oc % 2][:, :],
                                 in_=gfm[oc * P:(oc + 1) * P, :]).then_inc(ld, 16)
                    sy.dma_start(out=xsb[oc % 2][:, :],
                                 in_=xfm[oc * P:(oc + 1) * P, :]).then_inc(ld, 16)

            @block.tensor
            def _(t):
                for oc in range(NDC):
                    p = oc % 2
                    t.wait_ge(wl, 16 * (oc + 1))
                    if oc >= 2:
                        t.wait_ge(ev, 2 * (oc - 1))
                    for dc in range(NDC):
                        stt, spp = (dc == 0), (dc == NDC - 1)
                        lhsT = wsl[p][:, dc * P:(dc + 1) * P]
                        t.matmul(ps[p][:, 0:512], lhsT, attn_sb[:, dc * T:dc * T + 512],
                                 start=stt, stop=spp)
                        mm = t.matmul(ps[p][:, 512:T], lhsT,
                                      attn_sb[:, dc * T + 512:(dc + 1) * T],
                                      start=stt, stop=spp)
                    mm.then_inc(pe, 1)

            @block.vector
            def _(v):
                for oc in range(NDC):
                    p = oc % 2
                    v.wait_ge(pe, oc + 1)
                    v.wait_ge(ld, 16 * (1 + 2 * (oc + 1)))
                    if oc >= 2:
                        v.wait_ge(st, 16 * (oc - 1))
                    v.tensor_mul(out=tmp[:, :], in0=ps[p][:, :],
                                 in1=gsb[p][:, :]).then_inc(ev, 1)
                    v.wait_ge(ev, 2 * oc + 1)
                    v.tensor_add(out=osb[p][:, :], in0=tmp[:, :],
                                 in1=xsb[p][:, :]).then_inc(ev, 1)

            @block.scalar
            def _(s):
                for oc in range(NDC):
                    s.wait_ge(ev, 2 * (oc + 1))
                    s.dma_start(out=xo[oc * P:(oc + 1) * P, :],
                                in_=osb[oc % 2][:, :]).then_inc(st, 16)
                s.wait_ge(st, 16 * NDC)
    return nc



def build_l1_pre(nc=None, ext=None, tag=""):
    if nc is None:
        nc = bass.Bass("TRN2", target_bir_lowering=False)
    io = ext if ext is not None else {}
    def _in(name, shape):
        return io[name] if name in io else nc.dram_tensor(name, shape, F32, kind="ExternalInput").ap()
    x1 = _in("x1", [D, T])
    wq1 = _in("wq1", [D, D])
    wk1 = _in("wk1", [D, 1024])
    wv1 = _in("wv1", [D, 1024])
    wr1 = _in("wr1", [D, 8])
    br1 = _in("br1", [8, 1])
    acat1 = _in("acat1", [D, P])
    bcat1 = _in("bcat1", [P, D])
    qkv = io.get("qkv1") or nc.dram_tensor("qkv", [NCORE * BLK1, T], BF16, kind="ExternalOutput").ap()
    gout = io.get("gout1") or nc.dram_tensor("gout", [D, T], BF16, kind="ExternalOutput").ap()
    rowscr = nc.dram_tensor("rowscr1" + tag, [2, T], F32).ap()

    with contextlib.ExitStack() as es:
        consts = io.get("consts") or emit_consts(nc, es)
        xn_fm = es.enter_context(nc.sbuf_tensor("xn_fm1", [P, NDC * T], BF16))
        emit_ln(nc, es, x1, xn_fm, rowscr, tag="L1")

        sc_fm = es.enter_context(nc.sbuf_tensor("sc_fm", [8, T], BF16))
        with contextlib.ExitStack() as ls:
            wrsl = ls.enter_context(nc.sbuf_tensor("r1_w", [P, NDC * 8], BF16))
            brc = ls.enter_context(nc.sbuf_tensor("r1_br", [8, 1], F32))
            lgfm = ls.enter_context(nc.sbuf_tensor("r1_lgfm", [8, T], F32))
            lgtm = ls.enter_context(nc.sbuf_tensor("r1_lgtm", [P, 8 * 8], F32))
            nmax = ls.enter_context(nc.sbuf_tensor("r1_nmax", [P, 8], F32))
            ssum = ls.enter_context(nc.sbuf_tensor("r1_ssum", [P, 8], F32))
            srec = ls.enter_context(nc.sbuf_tensor("r1_srec", [P, 8], F32))
            etm = ls.enter_context(nc.sbuf_tensor("r1_etm", [P, 8 * 8], F32))
            stm = ls.enter_context(nc.sbuf_tensor("r1_stm", [P, 8 * 8], BF16))
            ps_g = ls.enter_context(nc.psum_tensor("r1_psg", [8, T], F32))
            ps_t = [ls.enter_context(nc.psum_tensor(f"r1_pst{i}", [P, 8], F32)) for i in range(2)]
            ps_m = [ls.enter_context(nc.psum_tensor(f"r1_psm{i}", [8, P], BF16)) for i in range(2)]
            ld = nc.alloc_semaphore("r1_ld")
            pe = nc.alloc_semaphore("r1_pe")
            ev = nc.alloc_semaphore("r1_ev")
            sf = nc.alloc_semaphore("r1_sf")

            with nc.Block() as block:
                @block.gpsimd
                def _(g):
                    g.dma_start(out=wrsl[:, :].rearrange("p (dc c) -> p dc c", dc=NDC),
                                in_=wr1.rearrange("(dc p) c -> p dc c", p=P)).then_inc(ld, 16)
                    g.dma_start(out=brc[:, :], in_=br1).then_inc(ld, 16)
                    g.wait_ge(ld, 32)
            with nc.Block() as block:
                @block.tensor
                def _(t):
                    for dc in range(NDC):
                        stt, spp = (dc == 0), (dc == NDC - 1)
                        lhsT = wrsl[:, dc * 8:(dc + 1) * 8]
                        t.matmul(ps_g[:, 0:512], lhsT, xn_fm[:, dc * T:dc * T + 512],
                                 start=stt, stop=spp)
                        t.matmul(ps_g[:, 512:T], lhsT, xn_fm[:, dc * T + 512:(dc + 1) * T],
                                 start=stt, stop=spp)
            with nc.Block() as block:
                @block.vector
                def _(v):
                    v.tensor_scalar(out=lgfm[:, :], in0=ps_g[:, :], scalar1=brc[:, :],
                                    scalar2=None, op0=ALU.add).then_inc(sf, 1)
                    v.wait_ge(sf, 1)
            with nc.Block() as block:
                @block.tensor
                def _(t):
                    for tt in range(8):
                        if tt >= 2:
                            t.wait_ge(ev, tt - 1)
                        t.transpose(ps_t[tt % 2][:, :], lgfm[:, tt * P:(tt + 1) * P],
                                    consts["ident_f"][0:8, 0:8]).then_inc(pe, 1)

                @block.vector
                def _(v):
                    for tt in range(8):
                        v.wait_ge(pe, tt + 1)
                        v.tensor_copy(out=lgtm[:, tt * 8:(tt + 1) * 8],
                                      in_=ps_t[tt % 2][:, :]).then_inc(ev, 1)
                    v.wait_ge(ev, 8)
            with nc.Block() as block:
                @block.vector
                def _(v):
                    v.tensor_reduce(out=nmax[:, :],
                                    in_=lgtm[:, :].rearrange("p (t e) -> p t e", e=8),
                                    axis=AX.X, op=ALU.max, negate=True).then_inc(sf, 1)
                    v.wait_ge(sf, 2)
            with nc.Block() as block:
                @block.scalar
                def _(s):
                    for tt in range(8):
                        s.activation(out=etm[:, tt * 8:(tt + 1) * 8],
                                     in_=lgtm[:, tt * 8:(tt + 1) * 8], func=AF.Exp,
                                     bias=nmax[:, tt:tt + 1]).then_inc(sf, 1)
                    s.wait_ge(sf, 10)
            with nc.Block() as block:
                @block.vector
                def _(v):
                    v.tensor_reduce(out=ssum[:, :],
                                    in_=etm[:, :].rearrange("p (t e) -> p t e", e=8),
                                    axis=AX.X, op=ALU.add).then_inc(sf, 1)
                    v.wait_ge(sf, 11)
                    v.reciprocal(out=srec[:, :], in_=ssum[:, :]).then_inc(sf, 1)
                    v.wait_ge(sf, 12)
                    for tt in range(8):
                        v.tensor_scalar(out=stm[:, tt * 8:(tt + 1) * 8],
                                        in0=etm[:, tt * 8:(tt + 1) * 8],
                                        scalar1=srec[:, tt:tt + 1], scalar2=8.0,
                                        op0=ALU.mult, op1=ALU.mult).then_inc(sf, 1)
                    v.wait_ge(sf, 20)
            with nc.Block() as block:
                @block.tensor
                def _(t):
                    for tt in range(8):
                        if tt >= 2:
                            t.wait_ge(ev, 8 + tt - 1)
                        t.transpose(ps_m[tt % 2][:, :], stm[:, tt * 8:(tt + 1) * 8],
                                    consts["ident_b"][:, :]).then_inc(pe, 1)

                @block.vector
                def _(v):
                    for tt in range(8):
                        v.wait_ge(pe, 8 + tt + 1)
                        v.tensor_copy(out=sc_fm[:, tt * P:(tt + 1) * P],
                                      in_=ps_m[tt % 2][:, :]).then_inc(ev, 1)
                    v.wait_ge(ev, 16)

        with contextlib.ExitStack() as ls:
            asl = ls.enter_context(nc.sbuf_tensor("l1_a", [P, NDC * P], BF16))
            bsl = ls.enter_context(nc.sbuf_tensor("l1_b", [P, D], BF16))
            low_sb = ls.enter_context(nc.sbuf_tensor("l1_low", [P, T], BF16))
            wgt = ls.enter_context(nc.sbuf_tensor("l1_wgt", [P, T], BF16))
            evg = [ls.enter_context(nc.sbuf_tensor(f"l1_e{i}", [P, T], BF16)) for i in range(2)]
            ps_l = ls.enter_context(nc.psum_tensor("l1_psl", [P, T], F32))
            ps_sc = ls.enter_context(nc.psum_tensor("l1_pssc", [P, T], F32))
            ps_o = [ls.enter_context(nc.psum_tensor(f"l1_pso{i}", [P, T], F32)) for i in range(2)]
            ld = nc.alloc_semaphore("l1_ld")
            pe = nc.alloc_semaphore("l1_pe")
            ev = nc.alloc_semaphore("l1_ev")
            st = nc.alloc_semaphore("l1_st")
            sf = nc.alloc_semaphore("l1_sf")

            with nc.Block() as block:
                @block.gpsimd
                def _(g):
                    g.dma_start(out=asl[:, :].rearrange("p (dc c) -> p dc c", dc=NDC),
                                in_=acat1.rearrange("(dc p) c -> p dc c", p=P)).then_inc(ld, 16)
                    g.dma_start(out=bsl[:, :], in_=bcat1).then_inc(ld, 16)
                    g.wait_ge(ld, 32)
            with nc.Block() as block:
                @block.tensor
                def _(t):
                    for dc in range(NDC):
                        stt, spp = (dc == 0), (dc == NDC - 1)
                        lhsT = asl[:, dc * P:(dc + 1) * P]
                        t.matmul(ps_l[:, 0:512], lhsT, xn_fm[:, dc * T:dc * T + 512],
                                 start=stt, stop=spp)
                        t.matmul(ps_l[:, 512:T], lhsT, xn_fm[:, dc * T + 512:(dc + 1) * T],
                                 start=stt, stop=spp)
                    t.matmul(ps_sc[:, 0:512], consts["r01_h8"][:, :], sc_fm[:, 0:512],
                             start=True, stop=True)
                    t.matmul(ps_sc[:, 512:T], consts["r01_h8"][:, :], sc_fm[:, 512:T],
                             start=True, stop=True)
            with nc.Block() as block:
                @block.vector
                def _(v):
                    v.tensor_copy(out=low_sb[:, :], in_=ps_l[:, :]).then_inc(sf, 1)
                    v.wait_ge(sf, 1)
                    v.tensor_mul(out=wgt[:, :], in0=low_sb[:, :],
                                 in1=ps_sc[:, :]).then_inc(sf, 1)
                    v.wait_ge(sf, 2)
            with nc.Block() as block:
                @block.tensor
                def _(t):
                    for oc in range(NDC):
                        p = oc % 2
                        if oc >= 2:
                            t.wait_ge(ev, oc - 1)
                        lhsT = bsl[:, oc * P:(oc + 1) * P]
                        t.matmul(ps_o[p][:, 0:512], lhsT, wgt[:, 0:512],
                                 start=True, stop=True)
                        t.matmul(ps_o[p][:, 512:T], lhsT, wgt[:, 512:T],
                                 start=True, stop=True).then_inc(pe, 1)

                @block.scalar
                def _(s):
                    for oc in range(NDC):
                        p = oc % 2
                        s.wait_ge(pe, oc + 1)
                        if oc >= 2:
                            s.wait_ge(st, 16 * (oc - 1))
                        s.activation(out=evg[p][:, :], in_=ps_o[p][:, :], func=AF.Silu,
                                     scale=2.0).then_inc(ev, 1)

                @block.sync
                def _(sy):
                    for oc in range(NDC):
                        sy.wait_ge(ev, oc + 1)
                        sy.dma_start(out=gout[oc * P:(oc + 1) * P, :],
                                     in_=evg[oc % 2][:, :]).then_inc(st, 16)
                    sy.wait_ge(st, 16 * NDC)

        q_cols = [(4 * j + cc) * P for j in range(NCORE) for cc in range(4)]
        q_rows = [j * BLK1 + cc * P for j in range(NCORE) for cc in range(4)]
        emit_wstat_proj(nc, wq1, q_cols, xn_fm, qkv, q_rows, "q1p")
        k_cols = [j * P for j in range(NCORE)]
        k_rows = [j * BLK1 + 512 for j in range(NCORE)]
        emit_wstat_proj(nc, wk1, k_cols, xn_fm, qkv, k_rows, "k1p")
        emit_v_tm(nc, wv1, xn_fm, qkv, 1024, BLK1, "v1p")
    return nc



class SC:
    def __init__(self, sem):
        self.sem = sem
        self.n = 0

    def inc(self, instr, k=1):
        self.n += k
        return instr.then_inc(self.sem, k)

    def wait(self, eng, n=None):
        eng.wait_ge(self.sem, self.n if n is None else n)


def _colmax_to_scalar(v, sc, src_col, scr32, row1):
    sc.inc(v.tensor_copy(out=scr32[0:64, 0:1], in_=src_col[64:128, :]))
    sc.wait(v)
    sc.inc(v.tensor_max(out=scr32[0:64, 1:2], in0=src_col[0:64, :], in1=scr32[0:64, 0:1]))
    sc.wait(v)
    sc.inc(v.tensor_copy(out=scr32[0:32, 2:3], in_=scr32[32:64, 1:2]))
    sc.wait(v)
    sc.inc(v.tensor_max(out=scr32[0:32, 3:4], in0=scr32[0:32, 1:2], in1=scr32[0:32, 2:3]))
    sc.wait(v)
    sc.inc(v.memset(scr32[0:32, 4:36], -1e30))
    sc.wait(v)
    sc.inc(v.tensor_copy(out=scr32[0:32, 4:5], in_=scr32[0:32, 3:4]))
    sc.wait(v)
    sc.inc(v.transpose(out=scr32[0:32, 36:68], in_=scr32[0:32, 4:36]))
    sc.wait(v)
    sc.inc(v.reduce_max(out=row1[:, 0:1], in_=scr32[0:1, 36:68], axis=AX.X))
    sc.wait(v)


def build_l1_attn(nc=None, ext=None):
    if nc is None:
        nc = bass.Bass("TRN2", target_bir_lowering=False)
    io = ext if ext is not None else {}
    qkv = io.get("qkv1r") or nc.dram_tensor("qkv", [NCORE * BLK1, T], BF16, kind="ExternalInput").ap()
    cosT = io.get("cosT") or nc.dram_tensor("cosT", [P, 4096], F32, kind="ExternalInput").ap()
    sinT = io.get("sinT") or nc.dram_tensor("sinT", [P, 4096], F32, kind="ExternalInput").ap()
    ob = io.get("ob1") or nc.dram_tensor("ob", [NCORE * OBLK, T], BF16, kind="ExternalOutput").ap()
    denk_d = nc.dram_tensor("denk_d", [1, 4096], F32).ap()
    denq_d = nc.dram_tensor("denq_d", [4, 4096], F32).ap()

    NS = 4096
    with contextlib.ExitStack() as es:
        consts = io.get("consts") or emit_consts(nc, es)
        u01x4 = es.enter_context(nc.sbuf_tensor("u01x4", [P, 512], BF16))
        swa4 = es.enter_context(nc.sbuf_tensor("swa4", [P, 1024], BF16))
        swa40 = es.enter_context(nc.sbuf_tensor("swa40", [P, 1024], BF16))
        cos_sb = es.enter_context(nc.sbuf_tensor("cos_sb", [P, NS], BF16))
        sin_sb = es.enter_context(nc.sbuf_tensor("sin_sb", [P, NS], BF16))
        qs4 = es.enter_context(nc.sbuf_tensor("qs4", [P, 4 * NS], BF16))
        sq4 = es.enter_context(nc.sbuf_tensor("sq4", [P, 4 * NS], BF16))
        k_in = es.enter_context(nc.sbuf_tensor("k_in", [P, NS], BF16))
        q_in = es.enter_context(nc.sbuf_tensor("q_in", [P, NS], BF16))
        ks = es.enter_context(nc.sbuf_tensor("ks", [P, NS], BF16))
        sk = es.enter_context(nc.sbuf_tensor("sk", [P, NS], BF16))
        G32 = es.enter_context(nc.sbuf_tensor("G32", [P, NS], F32))
        Vt = es.enter_context(nc.sbuf_tensor("Vt", [P, 32 * P], BF16))
        pool = [es.enter_context(nc.sbuf_tensor(f"pool{i}", [P, NS], BF16)) for i in range(3)]
        fscr = [es.enter_context(nc.sbuf_tensor("fscr0", [P, T], F32))]
        rbc = es.enter_context(nc.sbuf_tensor("rbc", [P, NS], BF16))
        row5 = es.enter_context(nc.sbuf_tensor("row5", [1, 512], F32))
        row6 = es.enter_context(nc.sbuf_tensor("row6", [1, 512], F32))
        scr32 = es.enter_context(nc.sbuf_tensor("scr32", [P, 68], F32))
        bias_g = es.enter_context(nc.sbuf_tensor("bias_g", [P, 4], F32))
        Gtm = es.enter_context(nc.sbuf_tensor("Gtm", [P, P], F32))
        ecol_all = es.enter_context(nc.sbuf_tensor("ecol_all", [P, 32], F32))
        qd = es.enter_context(nc.sbuf_tensor("qd", [P, 1024], BF16))
        A_sb = es.enter_context(nc.sbuf_tensor("A_sb", [P, 512], BF16))
        S_f32 = es.enter_context(nc.sbuf_tensor("S_f32", [P, P], F32))
        S_tmp = es.enter_context(nc.sbuf_tensor("S_tmp", [P, P], F32))
        S_bf = es.enter_context(nc.sbuf_tensor("S_bf", [P, P], BF16))
        P_w = es.enter_context(nc.sbuf_tensor("P_w", [P, 1024], BF16))
        P_wm = es.enter_context(nc.sbuf_tensor("P_wm", [P, 1024], BF16))
        rb_sb = es.enter_context(nc.sbuf_tensor("rb_sb", [P, 512], F32))
        y_sb = es.enter_context(nc.sbuf_tensor("y_sb", [P, 512], BF16))
        o_sb = es.enter_context(nc.sbuf_tensor("o_sb", [P, 512], BF16))
        rrow = row5
        rrec = row6
        rr05 = es.enter_context(nc.sbuf_tensor("rr05", [1, 512], F32))
        b_misc = es.enter_context(nc.psum_tensor("b_misc", [P, 384], F32))
        b_kd = es.enter_context(nc.psum_tensor("b_kd", [P, P], BF16))
        psA = es.enter_context(nc.psum_tensor("psA", [P, 512], F32))
        o_ps = es.enter_context(nc.psum_tensor("o_ps1", [P, 512], F32))
        ps_w = es.enter_context(nc.psum_tensor("ps_w", [P, 1024], F32))
        ps_y = es.enter_context(nc.psum_tensor("ps_y", [P, 512], F32))
        ps_rb = es.enter_context(nc.psum_tensor("ps_rb", [P, 512], F32))
        ps_r = ps_rb[0:1, :]

        semA = nc.alloc_semaphore("l1a_A")
        semB = nc.alloc_semaphore("l1a_B")
        semC = nc.alloc_semaphore("l1a_C")
        semL = nc.alloc_semaphore("l1a_L")
        semS = nc.alloc_semaphore("l1a_S")
        A, B_, C_, L, S_ = SC(semA), SC(semB), SC(semC), SC(semL), SC(semS)

        qkv3 = qkv.rearrange("(i r) t -> i r t", r=BLK1)

        with nc.Block() as block:
            @block.gpsimd
            def _(g):
                L.inc(g.dma_start(out=cos_sb[:, :], in_=cosT), 16)
                L.inc(g.dma_start(out=sin_sb[:, :], in_=sinT), 16)
                L.wait(g)

            @block.vector
            def _(v):
                for i in range(4):
                    C_.inc(v.tensor_copy(out=u01x4[:, i * P:(i + 1) * P],
                                         in_=consts["u01_b"][:, :]))
                    C_.inc(v.tensor_copy(out=swa4[:, i * 256:(i + 1) * 256],
                                         in_=consts["swa_m"][:, :]))
                    C_.inc(v.tensor_copy(out=swa40[:, i * 256 + P:(i + 1) * 256],
                                         in_=consts["swa_m"][:, P:256]))
                C_.inc(v.memset(swa40[:, 0:P], 0.0))
                for i in range(1, 4):
                    C_.inc(v.memset(swa40[:, i * 256:i * 256 + P], 0.0))
                C_.wait(v)

        for grp in range(2):
            b = grp
            i0 = 4 * b
            with nc.Block() as block:
                @block.gpsimd
                def _(g):
                    L.inc(g.dma_start(out=k_in[:, :].rearrange("p (i t) -> p i t", i=4),
                                      in_=qkv3[i0:i0 + 4, 512:512 + P, :]
                                      .transpose([1, 0, 2])), 16)
                    vsrc = qkv3[i0:i0 + 4, 640:768, :] \
                        .rearrange("i r t -> i (r t)") \
                        .rearrange("i (t v) -> i t v", v=P) \
                        .rearrange("i (th tl) v -> i th tl v", th=8)
                    vdst = Vt[:, :].rearrange("p (i th v) -> p i th v", i=4, th=8)
                    for i in range(4):
                        L.inc(g.dma_start(out=vdst[:, i],
                                          in_=vsrc[i].transpose([1, 0, 2])), 16)
                    L.wait(g)

            with nc.Block() as block:
                @block.vector
                def _(v):
                    C_.inc(v.reduce_max(out=bias_g[:, 3:4], in_=k_in[:, :], axis=AX.X))
                    C_.wait(v)
                    _colmax_to_scalar(v, C_, bias_g[:, 3:4], scr32, row5)
                    C_.inc(v.tensor_scalar_mul(out=row5[:, 1:2], in0=row5[:, 0:1],
                                               scalar1=-1.0))
                    C_.wait(v)

                @block.tensor
                def _(t):
                    C_.wait(t)
                    A.inc(t.matmul(ps_rb[:, 0:1], consts["onesr_f"][:, :], row5[:, 1:2],
                                   start=True, stop=True))

                @block.vector
                def _(v2):
                    A.wait(v2)
                    C_.inc(v2.tensor_copy(out=bias_g[:, 1:2], in_=ps_rb[:, 0:1]))
                    C_.wait(v2)
            with nc.Block() as block:
                @block.scalar
                def _(s):
                    B_.inc(s.activation(out=pool[0][:, :], in_=k_in[:, :], func=AF.Exp,
                                        bias=bias_g[:, 1:2]))
                    B_.wait(s)

                @block.tensor
                def _(t):
                    B_.wait(t)
                    for i in range(8):
                        if i >= 1:
                            C_.wait(t, C_.n + 2 * i - 1)
                        A.inc(t.matmul(ps_r, consts["ones1"][:, :],
                                       pool[0][:, i * 512:(i + 1) * 512],
                                       start=True, stop=True))

                @block.vector
                def _(v):
                    for i in range(8):
                        A.wait(v, A.n - 8 + i + 1)
                        v.tensor_copy(out=row5[:, :], in_=ps_r[:, :]).then_inc(semC, 1)
                        C_.n += 1
                        C_.wait(v)
                        v.reciprocal(out=row6[:, :], in_=row5[:, :]).then_inc(semC, 1)
                        C_.n += 1
                        C_.wait(v)

                @block.sync
                def _(sy):
                    for i in range(8):
                        sy.wait_ge(semC, C_.n - 16 + 2 * (i + 1))
                        S_.inc(sy.dma_start(out=denk_d[:, i * 512:(i + 1) * 512],
                                            in_=row6[:, :]), 16)
                    S_.wait(sy)
            with nc.Block() as block:
                @block.gpsimd
                def _(g):
                    L.inc(g.dma_start(out=rbc[:, :],
                                      in_=denk_d.to_broadcast((P, NS))), 16)
                    L.wait(g)
            with nc.Block() as block:
                @block.vector
                def _(v):
                    C_.inc(v.tensor_mul(out=ks[:, :], in0=pool[0][:, :], in1=rbc[:, :]))
                    C_.wait(v)
                    C_.inc(v.tensor_scalar_mul(out=pool[1][0:64, :], in0=k_in[64:128, :],
                                               scalar1=-1.0))
                    C_.inc(v.tensor_copy(out=pool[1][64:128, :], in_=k_in[0:64, :]))
                    C_.wait(v)
                    C_.inc(v.tensor_mul(out=pool[2][:, :], in0=k_in[:, :], in1=cos_sb[:, :]))
                    C_.inc(v.tensor_mul(out=pool[0][:, :], in0=pool[1][:, :], in1=sin_sb[:, :]))
                    C_.wait(v)
                    C_.inc(v.tensor_add(out=sk[:, :], in0=pool[2][:, :], in1=pool[0][:, :]))
                    C_.wait(v)
            with nc.Block() as block:
                @block.scalar
                def _(s):
                    for i in range(4):
                        sl = slice(i * T, (i + 1) * T)
                        B_.inc(s.activation(out=pool[0][:, sl], in_=k_in[:, sl], func=AF.Abs))
                        B_.wait(s)
                        B_.inc(s.activation(out=pool[1][:, sl], in_=pool[0][:, sl],
                                            func=AF.Exp, scale=-1.0))
                        B_.wait(s)
                        B_.inc(s.activation(out=fscr[0][:, :], in_=pool[1][:, sl],
                                            func=AF.Ln, bias=1.0))
                        B_.wait(s)

                @block.vector
                def _(v):
                    for i in range(4):
                        sl = slice(i * T, (i + 1) * T)
                        B_.wait(v, B_.n - 12 + 3 * (i + 1))
                        C_.inc(v.tensor_scalar_min(out=pool[0][:, sl], in0=k_in[:, sl],
                                                   scalar1=0.0))
                        C_.wait(v)
                        C_.inc(v.tensor_sub(out=G32[:, sl], in0=pool[0][:, sl],
                                            in1=fscr[0][:, :]))
                        C_.wait(v)

            for h in range(4):
                with nc.Block() as block:
                    @block.gpsimd
                    def _(g):
                        L.inc(g.dma_start(out=q_in[:, :].rearrange("p (i t) -> p i t", i=4),
                                          in_=qkv3[i0:i0 + 4, h * P:(h + 1) * P, :]
                                          .transpose([1, 0, 2])), 16)
                        L.wait(g)
                with nc.Block() as block:
                    @block.vector
                    def _(v):
                        C_.inc(v.reduce_max(out=bias_g[:, 3:4], in_=q_in[:, :], axis=AX.X))
                        C_.wait(v)
                        _colmax_to_scalar(v, C_, bias_g[:, 3:4], scr32, row5)
                        C_.inc(v.tensor_scalar_mul(out=row5[:, 1:2], in0=row5[:, 0:1],
                                                   scalar1=-1.0))
                        C_.wait(v)

                    @block.tensor
                    def _(t):
                        C_.wait(t)
                        A.inc(t.matmul(ps_rb[:, 0:1], consts["onesr_f"][:, :], row5[:, 1:2],
                                       start=True, stop=True))

                    @block.vector
                    def _(v2):
                        A.wait(v2)
                        C_.inc(v2.tensor_copy(out=bias_g[:, 0:1], in_=ps_rb[:, 0:1]))
                        C_.wait(v2)
                with nc.Block() as block:
                    @block.scalar
                    def _(s):
                        B_.inc(s.activation(out=pool[0][:, :], in_=q_in[:, :], func=AF.Exp,
                                            bias=bias_g[:, 0:1]))
                        B_.wait(s)

                    @block.tensor
                    def _(t):
                        B_.wait(t)
                        for i in range(8):
                            if i >= 1:
                                C_.wait(t, C_.n + 2 * i - 1)
                            A.inc(t.matmul(ps_r, consts["ones1"][:, :],
                                           pool[0][:, i * 512:(i + 1) * 512],
                                           start=True, stop=True))

                    @block.vector
                    def _(v):
                        for i in range(8):
                            A.wait(v, A.n - 8 + i + 1)
                            v.tensor_copy(out=row5[:, :], in_=ps_r[:, :]).then_inc(semC, 1)
                            C_.n += 1
                            C_.wait(v)
                            v.reciprocal(out=row6[:, :], in_=row5[:, :]).then_inc(semC, 1)
                            C_.n += 1
                            C_.wait(v)

                    @block.sync
                    def _(sy):
                        for i in range(8):
                            sy.wait_ge(semC, C_.n - 16 + 2 * (i + 1))
                            S_.inc(sy.dma_start(out=denq_d[h:h + 1, i * 512:(i + 1) * 512],
                                                in_=row6[:, :]), 16)
                        S_.wait(sy)
                with nc.Block() as block:
                    @block.gpsimd
                    def _(g):
                        L.inc(g.dma_start(out=rbc[:, :],
                                          in_=denq_d[h:h + 1, :].to_broadcast((P, NS))), 16)
                        L.wait(g)
                with nc.Block() as block:
                    @block.vector
                    def _(v):
                        C_.inc(v.tensor_mul(out=qs4[:, h * NS:(h + 1) * NS],
                                            in0=pool[0][:, :], in1=rbc[:, :]))
                        C_.wait(v)
                        C_.inc(v.tensor_scalar_mul(out=pool[1][0:64, :], in0=q_in[64:128, :],
                                                   scalar1=-1.0))
                        C_.inc(v.tensor_copy(out=pool[1][64:128, :], in_=q_in[0:64, :]))
                        C_.wait(v)
                        C_.inc(v.tensor_mul(out=pool[2][:, :], in0=q_in[:, :],
                                            in1=cos_sb[:, :]))
                        C_.inc(v.tensor_mul(out=pool[0][:, :], in0=pool[1][:, :],
                                            in1=sin_sb[:, :]))
                        C_.wait(v)
                        C_.inc(v.tensor_add(out=sq4[:, h * NS:(h + 1) * NS],
                                            in0=pool[2][:, :], in1=pool[0][:, :]))
                        C_.wait(v)

            with nc.Block() as block:
                @block.scalar
                def _(s):
                    B_.inc(s.activation(out=pool[0][:, :], in_=sk[:, :], func=AF.Square))
                    B_.wait(s)

                @block.tensor
                def _(t):
                    B_.wait(t)
                    for i in range(8):
                        if i >= 1:
                            t.wait_ge(semC, C_.n + 1 + 2 * i)
                        A.inc(t.matmul(ps_r, consts["ones1"][:, :],
                                       pool[0][:, i * 512:(i + 1) * 512],
                                       start=True, stop=True))

                @block.vector
                def _(v):
                    cols = [2, 4]
                    C_.inc(v.memset(row5[:, 2:3], 0.0))
                    C_.wait(v)
                    for i in range(8):
                        A.wait(v, A.n - 8 + i + 1)
                        v.reduce_max(out=row5[:, 3:4], in_=ps_r,
                                     axis=AX.X).then_inc(semC, 1)
                        C_.n += 1
                        C_.wait(v)
                        src, dst = cols[i % 2], cols[(i + 1) % 2]
                        v.tensor_max(out=row5[:, dst:dst + 1], in0=row5[:, src:src + 1],
                                     in1=row5[:, 3:4]).then_inc(semC, 1)
                        C_.n += 1
                        C_.wait(v)
                    C_.inc(v.tensor_copy(out=row5[:, 2:3], in_=row5[:, cols[8 % 2]:cols[8 % 2] + 1]))
                    C_.wait(v)
            with nc.Block() as block:
                @block.scalar
                def _(s):
                    pass

                @block.vector
                def _(v):
                    C_.inc(v.memset(row6[:, 0:1], 0.0))
                    C_.wait(v)
            for h in range(4):
                with nc.Block() as block:
                    @block.scalar
                    def _(s):
                        B_.inc(s.activation(out=pool[0][:, :],
                                            in_=sq4[:, h * NS:(h + 1) * NS],
                                            func=AF.Square))
                        B_.wait(s)

                    @block.tensor
                    def _(t):
                        B_.wait(t)
                        for i in range(8):
                            if i >= 1:
                                t.wait_ge(semC, C_.n + 2 * i - 1)
                            A.inc(t.matmul(ps_r, consts["ones1"][:, :],
                                           pool[0][:, i * 512:(i + 1) * 512],
                                           start=True, stop=True))

                    @block.vector
                    def _(v):
                        cols = [0, 5]
                        for i in range(8):
                            A.wait(v, A.n - 8 + i + 1)
                            v.reduce_max(out=row6[:, 1:2], in_=ps_r,
                                         axis=AX.X).then_inc(semC, 1)
                            C_.n += 1
                            C_.wait(v)
                            src, dst = cols[(h * 8 + i) % 2], cols[(h * 8 + i + 1) % 2]
                            v.tensor_max(out=row6[:, dst:dst + 1],
                                         in0=row6[:, src:src + 1],
                                         in1=row6[:, 1:2]).then_inc(semC, 1)
                            C_.n += 1
                            C_.wait(v)
            with nc.Block() as block:
                @block.vector
                def _(v):
                    C_.inc(v.tensor_mul(out=row6[:, 2:3], in0=row6[:, 0:1],
                                        in1=row5[:, 2:3]))
                    C_.wait(v)

                @block.scalar
                def _(s):
                    C_.wait(s)
                    B_.inc(s.activation(out=row6[:, 3:4], in_=row6[:, 2:3], func=AF.Sqrt,
                                        scale=1.05 / 128.0))
                    B_.wait(s)

                @block.vector
                def _(v2):
                    B_.wait(v2)
                    C_.inc(v2.tensor_scalar_mul(out=row6[:, 4:5], in0=row6[:, 3:4],
                                                scalar1=-1.0))
                    C_.wait(v2)

                @block.tensor
                def _(t):
                    C_.wait(t)
                    A.inc(t.matmul(ps_rb[:, 0:1], consts["onesr_f"][:, :], row6[:, 4:5],
                                   start=True, stop=True))

                @block.vector
                def _(v3):
                    A.wait(v3)
                    C_.inc(v3.tensor_copy(out=bias_g[:, 2:3], in_=ps_rb[:, 0:1]))
                    C_.wait(v3)

            with nc.Block() as block:
                @block.vector
                def _(v):
                    C_.inc(v.memset(S_f32[:, :], 0.0))
                    C_.inc(v.memset(S_bf[:, :], 0.0))
                    C_.wait(v)

            NCH = 32
            ebp_all = pool[1]
            ebm_all = pool[2]
            with nc.Block() as block:
                a0p, b0p, c0p = A.n, B_.n, C_.n

                @block.tensor
                def _(t):
                    for ch in range(NCH):
                        if ch >= 1:
                            t.wait_ge(semC, c0p + ch)
                        t.transpose(psA[:, 0:P], G32[:, ch * P:(ch + 1) * P],
                                    consts["ident_f"][:, :]).then_inc(semA, 1)
                        t.wait_ge(semC, c0p + ch + 1)
                        if ch >= 1:
                            t.wait_ge(semB, b0p + 3 * ch)
                        t.matmul(b_misc[:, 0:P], Gtm[:, :], consts["u01_f"][:, :],
                                 start=True, stop=True).then_inc(semA, 1)

                @block.vector
                def _(v):
                    for ch in range(NCH):
                        v.wait_ge(semA, a0p + 2 * ch + 1)
                        v.tensor_copy(out=Gtm[:, :], in_=psA[:, 0:P]).then_inc(semC, 1)

                @block.scalar
                def _(sx):
                    for ch in range(NCH):
                        sx.wait_ge(semA, a0p + 2 * ch + 2)
                        sx.activation(out=ebp_all[:, ch * P:(ch + 1) * P],
                                      in_=b_misc[:, 0:P], func=AF.Exp,
                                      scale=1.0 / 16.0).then_inc(semB, 1)
                        sx.activation(out=ebm_all[:, ch * P:(ch + 1) * P],
                                      in_=b_misc[:, 0:P], func=AF.Exp,
                                      scale=-1.0 / 16.0).then_inc(semB, 1)
                        sx.activation(out=ecol_all[:, ch:ch + 1],
                                      in_=b_misc[:, P - 1:P], func=AF.Exp,
                                      scale=1.0 / 16.0).then_inc(semB, 1)
                A.n = a0p + 2 * NCH
                B_.n = b0p + 3 * NCH
                C_.n = c0p + NCH

            kd_all = k_in
            kdtm_all = q_in
            with nc.Block() as block:
                @block.vector
                def _(v):
                    C_.inc(v.tensor_mul(out=kd_all[:, :], in0=ks[:, :], in1=ebm_all[:, :]))
                    C_.wait(v)
            with nc.Block() as block:
                a0k, c0k = A.n, C_.n

                @block.tensor
                def _(t):
                    for ch in range(NCH):
                        if ch >= 1:
                            t.wait_ge(semC, c0k + ch)
                        t.transpose(b_kd[:, :], kd_all[:, ch * P:(ch + 1) * P],
                                    consts["ident_b"][:, :]).then_inc(semA, 1)

                @block.vector
                def _(v):
                    for ch in range(NCH):
                        v.wait_ge(semA, a0k + ch + 1)
                        v.tensor_copy(out=kdtm_all[:, ch * P:(ch + 1) * P],
                                      in_=b_kd[:, :]).then_inc(semC, 1)
                A.n = a0k + NCH
                C_.n = c0k + NCH

            a0, b0, c0 = A.n, B_.n, C_.n
            NA, NB, NCC = 7, 1, 15
            with nc.Block() as block:
                @block.tensor
                def _(t):
                    for ch in range(NCH):
                        ab = a0 + ch * NA
                        cb = c0 + ch * NCC
                        chp = max(ch - 1, 0)
                        t.wait_ge(semC, cb + 4)
                        if ch >= 1:
                            t.wait_ge(semC, cb - NCC + 5)
                        slot = (ch % 2) * 512
                        for h in range(4):
                            mm = t.matmul(psA[:, h * P:(h + 1) * P],
                                          kd_all[:, ch * P:(ch + 1) * P],
                                          qd[:, slot + h * P:slot + (h + 1) * P],
                                          start=True, stop=True)
                        mm.then_inc(semA, 1)
                        t.wait_ge(semC, cb + 5)
                        if ch >= 1:
                            t.wait_ge(semC, cb - NCC + 15)
                            t.wait_ge(semC, cb - NCC + 8)
                        for h in range(4):
                            t.matmul(o_ps[:, h * P:(h + 1) * P], Vt[:, ch * P:(ch + 1) * P],
                                     A_sb[:, h * P:(h + 1) * P], start=True, stop=False)
                            mm = t.matmul(o_ps[:, h * P:(h + 1) * P], S_bf[:, :],
                                          qd[:, slot + h * P:slot + (h + 1) * P],
                                          start=False, stop=True)
                        mm.then_inc(semA, 1)
                        if ch >= 1:
                            t.wait_ge(semC, cb - NCC + 6)
                        t.matmul(b_misc[:, 0:P], kdtm_all[:, ch * P:(ch + 1) * P],
                                 Vt[:, ch * P:(ch + 1) * P],
                                 start=True, stop=True).then_inc(semA, 1)
                        if ch >= 1:
                            t.wait_ge(semB, b0 + ch)
                        for h in range(4):
                            t.matmul(ps_w[:, h * 256:h * 256 + P],
                                     sk[:, chp * P:(chp + 1) * P],
                                     sq4[:, h * NS + ch * P:h * NS + (ch + 1) * P],
                                     start=True, stop=True)
                            mm = t.matmul(ps_w[:, h * 256 + P:(h + 1) * 256],
                                          sk[:, ch * P:(ch + 1) * P],
                                          sq4[:, h * NS + ch * P:h * NS + (ch + 1) * P],
                                          start=True, stop=True)
                        mm.then_inc(semA, 1)
                        t.wait_ge(semC, cb + 9)
                        if ch >= 1:
                            t.wait_ge(semC, cb - NCC + 13)
                        for h in range(4):
                            t.matmul(ps_rb[0:1, h * P:(h + 1) * P], consts["ones1"][:, :],
                                     P_wm[:, h * 256:h * 256 + P], start=True, stop=False)
                            mm = t.matmul(ps_rb[0:1, h * P:(h + 1) * P], consts["ones1"][:, :],
                                          P_wm[:, h * 256 + P:(h + 1) * 256],
                                          start=False, stop=True)
                        mm.then_inc(semA, 1)
                        if ch >= 1:
                            t.wait_ge(semC, cb - NCC + 14)
                        for h in range(4):
                            t.matmul(ps_y[:, h * P:(h + 1) * P],
                                     Vt[:, chp * P:(chp + 1) * P],
                                     P_wm[:, h * 256:h * 256 + P], start=True, stop=False)
                            mm = t.matmul(ps_y[:, h * P:(h + 1) * P],
                                          Vt[:, ch * P:(ch + 1) * P],
                                          P_wm[:, h * 256 + P:(h + 1) * 256],
                                          start=False, stop=True)
                        mm.then_inc(semA, 1)
                        t.wait_ge(semC, cb + 12)
                        t.matmul(ps_rb[:, :], consts["onesr_f"][:, :], rr05[:, :],
                                 start=True, stop=True).then_inc(semA, 1)

                @block.scalar
                def _(sx):
                    for ch in range(NCH):
                        ab = a0 + ch * NA
                        cb = c0 + ch * NCC
                        sx.wait_ge(semA, ab + 4)
                        if ch >= 1:
                            sx.wait_ge(semC, cb - NCC + 9)
                        sx.activation(out=P_w[:, :], in_=ps_w[:, :], func=AF.Exp,
                                      scale=SWA_SCALE, bias=bias_g[:, 2:3]).then_inc(semB, 1)

                @block.vector
                def _(v):
                    for ch in range(NCH):
                        ab = a0 + ch * NA
                        bb = b0 + ch * NB
                        cb = c0 + ch * NCC
                        slot = (ch % 2) * 512
                        if ch >= 2:
                            v.wait_ge(semA, a0 + (ch - 2) * NA + 2)
                        for h in range(4):
                            v.tensor_mul(out=qd[:, slot + h * P:slot + (h + 1) * P],
                                         in0=qs4[:, h * NS + ch * P:h * NS + (ch + 1) * P],
                                         in1=ebp_all[:, ch * P:(ch + 1) * P]).then_inc(semC, 1)
                        v.wait_ge(semA, ab + 1)
                        v.tensor_mul(out=A_sb[:, :], in0=psA[:, :],
                                     in1=u01x4[:, :]).then_inc(semC, 1)
                        v.wait_ge(semA, ab + 3)
                        v.tensor_add(out=S_tmp[:, :], in0=S_f32[:, :],
                                     in1=b_misc[:, 0:P]).then_inc(semC, 1)
                        v.wait_ge(semC, cb + 6)
                        v.tensor_scalar(out=S_f32[:, :], in0=S_tmp[:, :],
                                        scalar1=ecol_all[:, ch:ch + 1], scalar2=None,
                                        op0=ALU.mult).then_inc(semC, 1)
                        v.wait_ge(semC, cb + 7)
                        v.wait_ge(semA, ab + 2)
                        v.tensor_copy(out=S_bf[:, :], in_=S_f32[:, :]).then_inc(semC, 1)
                        v.wait_ge(semB, bb + 1)
                        msk = swa40 if ch == 0 else swa4
                        v.tensor_mul(out=P_wm[:, :], in0=P_w[:, :],
                                     in1=msk[:, :]).then_inc(semC, 1)
                        v.wait_ge(semA, ab + 5)
                        v.tensor_copy(out=rrow[:, :], in_=ps_rb[0:1, :]).then_inc(semC, 1)
                        v.wait_ge(semC, cb + 10)
                        v.reciprocal(out=rrec[:, :], in_=rrow[:, :]).then_inc(semC, 1)
                        v.wait_ge(semC, cb + 11)
                        v.tensor_scalar_mul(out=rr05[:, :], in0=rrec[:, :],
                                            scalar1=0.5).then_inc(semC, 1)
                        v.wait_ge(semA, ab + 7)
                        v.tensor_copy(out=rb_sb[:, :], in_=ps_rb[:, :]).then_inc(semC, 1)
                        v.wait_ge(semC, cb + 13)
                        v.wait_ge(semA, ab + 6)
                        v.tensor_mul(out=y_sb[:, :], in0=ps_y[:, :],
                                     in1=rb_sb[:, :]).then_inc(semC, 1)
                        v.wait_ge(semC, cb + 14)
                        if ch >= 1:
                            v.wait_ge(semS, S_.n + 64 * ch)
                        v.scalar_tensor_tensor(out=o_sb[:, :], in0=o_ps[:, :], scalar=0.5,
                                               in1=y_sb[:, :], op0=ALU.mult,
                                               op1=ALU.add).then_inc(semC, 1)

                @block.sync
                def _(sy):
                    s_base = S_.n
                    for ch in range(NCH):
                        cb = c0 + ch * NCC
                        jj = b * 4 + ch // 8
                        tcol = (ch % 8) * P
                        sy.wait_ge(semC, cb + 15)
                        for h in range(4):
                            sy.dma_start(
                                out=ob[jj * OBLK + h * P:jj * OBLK + (h + 1) * P,
                                       tcol:tcol + P],
                                in_=o_sb[:, h * P:(h + 1) * P]).then_inc(semS, 16)
                    sy.wait_ge(semS, s_base + 64 * NCH)
            A.n = a0 + NCH * NA
            B_.n = b0 + NCH * NB
            C_.n = c0 + NCH * NCC
            S_.n += 64 * NCH
    return nc



_BUILD_CACHE = {}


def _get(name, fn):
    if name not in _BUILD_CACHE:
        _BUILD_CACHE[name] = fn()
    return _BUILD_CACHE[name]


def kernel(**inputs):
    inputs = {k: np.asarray(v, dtype=np.float32) for k, v in inputs.items()}
    if FUSED:
        return kernel_fused(inputs)
    w = _prep_weights(inputs)
    shards = _shard_x0(inputs)
    cores = list(range(NCORE))

    ncA = _get("l0_pre", build_l0_pre)
    resA = run_bass_kernel_spmd(
        ncA, [{"x0": shards[c], "wq": w["wq"], "wk": w["wk"], "wv": w["wv"],
               "wup": w["wup"], "gate": w["gate"], "ebias": w["ebias"],
               "acat": w["acat"], "bcat": w["bcat"]} for c in range(NCORE)], cores)
    qkv_recv = _swap_blocks([np.asarray(resA.results[c]["qkv"]) for c in range(NCORE)], BLK0)
    g0 = [np.asarray(resA.results[c]["gout"]) for c in range(NCORE)]

    ncB = _get("l0_attn", build_l0_attn)
    resB = run_bass_kernel_spmd(ncB, [{"qkv": qkv_recv[c]} for c in range(NCORE)], cores)
    o_recv = _swap_blocks([np.asarray(resB.results[c]["ob"]) for c in range(NCORE)], OBLK)

    ncC = _get("post", build_post)
    resC = run_bass_kernel_spmd(
        ncC, [{"orc": o_recv[c], "gfm": g0[c], "xfm": shards[c], "wo": w["wo0"]}
              for c in range(NCORE)], cores)
    x1 = [np.asarray(resC.results[c]["xo"]) for c in range(NCORE)]

    ncA1 = _get("l1_pre", build_l1_pre)
    resA1 = run_bass_kernel_spmd(
        ncA1, [{"x1": x1[c], "wq1": w["wq1"], "wk1": w["wk1"], "wv1": w["wv1"],
                "wr1": w["wr1"], "br1": w["br1"], "acat1": w["acat1"],
                "bcat1": w["bcat1"]} for c in range(NCORE)], cores)
    qkv1_recv = _swap_blocks([np.asarray(resA1.results[c]["qkv"]) for c in range(NCORE)], BLK1)
    g1 = [np.asarray(resA1.results[c]["gout"]) for c in range(NCORE)]

    ncB1 = _get("l1_attn", build_l1_attn)
    resB1 = run_bass_kernel_spmd(
        ncB1, [{"qkv": qkv1_recv[c], "cosT": w["cosT"], "sinT": w["sinT"]}
               for c in range(NCORE)], cores)
    o1_recv = _swap_blocks([np.asarray(resB1.results[c]["ob"]) for c in range(NCORE)], OBLK)

    resC1 = run_bass_kernel_spmd(
        ncC, [{"orc": o1_recv[c], "gfm": g1[c], "xfm": x1[c], "wo": w["wo1"]}
              for c in range(NCORE)], cores)

    out = np.empty((2, 4096, 4096), dtype=np.float32)
    for c in range(NCORE):
        b, qq = c // 4, c % 4
        out[b, qq * T:(qq + 1) * T, :] = np.asarray(resC1.results[c]["xo"]).T
    return out



def emit_a2a(nc, src, dst, name):
    with contextlib.ExitStack() as ls:
        cc = nc.alloc_semaphore("cc_" + name)
        with nc.Block() as block:
            @block.gpsimd
            def _(g):
                with allow_ici_all_to_all():
                    g.collective_compute("AllToAll", mybir.AluOpType.bypass,
                                         replica_groups=[list(range(NCORE))],
                                         ins=[src], outs=[dst]).then_inc(cc, 1)
                g.wait_ge(cc, 1)


def build_fused():
    nc = bass.Bass("TRN2", target_bir_lowering=False)
    ext = {}
    for name, shape in [("x0", [D, T]), ("wq", [D, D]), ("wk", [D, D]), ("wv", [D, D]),
                        ("wup", [D, D]), ("gate", [D, 4]), ("ebias", [4, 1]),
                        ("acat", [D, 32]), ("bcat", [32, D]), ("wo0", [D, D]),
                        ("wq1", [D, D]), ("wk1", [D, 1024]), ("wv1", [D, 1024]),
                        ("wr1", [D, 8]), ("br1", [8, 1]), ("acat1", [D, P]),
                        ("bcat1", [P, D]), ("wo1", [D, D]),
                        ("cosT", [P, 4096]), ("sinT", [P, 4096])]:
        ext[name] = nc.dram_tensor(name, shape, F32, kind="ExternalInput").ap()
    ext["qkv"] = nc.dram_tensor("qkv0s", [NCORE * BLK0, T], BF16).ap()
    qkv0r = nc.dram_tensor("qkv0r", [NCORE * BLK0, T], BF16).ap()
    ext["gout"] = nc.dram_tensor("g0d", [D, T], BF16).ap()
    ob0s = nc.dram_tensor("ob0s", [NCORE * OBLK, T], BF16).ap()
    ob0r = nc.dram_tensor("ob0r", [NCORE * OBLK, T], BF16).ap()
    x1d = nc.dram_tensor("x1d", [D, T], F32).ap()
    ext["qkv1"] = nc.dram_tensor("qkv1s", [NCORE * BLK1, T], BF16).ap()
    qkv1r = nc.dram_tensor("qkv1r", [NCORE * BLK1, T], BF16).ap()
    ext["gout1"] = nc.dram_tensor("g1d", [D, T], BF16).ap()
    ob1s = nc.dram_tensor("ob1s", [NCORE * OBLK, T], BF16).ap()
    ob1r = nc.dram_tensor("ob1r", [NCORE * OBLK, T], BF16).ap()
    x2 = nc.dram_tensor("x2", [D, T], F32, kind="ExternalOutput").ap()

    with contextlib.ExitStack() as es:
        ext["consts"] = emit_consts(nc, es)
        build_l0_pre(nc, ext)
        emit_a2a(nc, ext["qkv"], qkv0r, "q0")
        build_l0_attn(nc, {"consts": ext["consts"], "qkv": qkv0r, "ob": ob0s})
        emit_a2a(nc, ob0s, ob0r, "o0")
        build_post(nc, {"orc": ob0r, "gfm": ext["gout"], "xfm": ext["x0"],
                        "wo": ext["wo0"], "xo": x1d}, tag="c0")
        build_l1_pre(nc, {"consts": ext["consts"], "x1": x1d, "wq1": ext["wq1"],
                          "wk1": ext["wk1"], "wv1": ext["wv1"], "wr1": ext["wr1"],
                          "br1": ext["br1"], "acat1": ext["acat1"],
                          "bcat1": ext["bcat1"], "qkv1": ext["qkv1"],
                          "gout1": ext["gout1"]})
        emit_a2a(nc, ext["qkv1"], qkv1r, "q1")
        build_l1_attn(nc, {"consts": ext["consts"], "qkv1r": qkv1r,
                           "cosT": ext["cosT"], "sinT": ext["sinT"], "ob1": ob1s})
        emit_a2a(nc, ob1s, ob1r, "o1")
        build_post(nc, {"orc": ob1r, "gfm": ext["gout1"], "xfm": x1d,
                        "wo": ext["wo1"], "xo": x2}, tag="c1")
    return nc


FUSED = True


def kernel_fused(inputs):
    w = _prep_weights(inputs)
    shards = _shard_x0(inputs)
    nc = _get("fused", build_fused)
    in_maps = []
    for c in range(NCORE):
        m = {"x0": shards[c], "wq": w["wq"], "wk": w["wk"], "wv": w["wv"],
             "wup": w["wup"], "gate": w["gate"], "ebias": w["ebias"],
             "acat": w["acat"], "bcat": w["bcat"], "wo0": w["wo0"],
             "wq1": w["wq1"], "wk1": w["wk1"], "wv1": w["wv1"], "wr1": w["wr1"],
             "br1": w["br1"], "acat1": w["acat1"], "bcat1": w["bcat1"],
             "wo1": w["wo1"], "cosT": w["cosT"], "sinT": w["sinT"]}
        in_maps.append(m)
    res = run_bass_kernel_spmd(nc, in_maps, core_ids=list(range(NCORE)))
    out = np.empty((2, 4096, 4096), dtype=np.float32)
    for c in range(NCORE):
        b, qq = c // 4, c % 4
        out[b, qq * T:(qq + 1) * T, :] = np.asarray(res.results[c]["x2"]).T
    return out
